# version 03
# baseline (speedup 1.0000x reference)
import contextlib
import numpy as np
import concourse.bass as bass
import concourse.mybir as mybir
from concourse.bass_utils import run_bass_kernel_spmd

F32 = mybir.dt.float32
BF16 = mybir.dt.bfloat16
AF = mybir.ActivationFunctionType
ALU = mybir.AluOpType
AX = mybir.AxisListType

DM, DI, S, RANK, E, HF = 1024, 2048, 16, 64, 8, 2048
NJ, NKD = DI // 128, DM // 128
EPS = 1e-5
PEN = -30000.0


def build(T, CH, dbg=False):
    NCH = T // CH
    NT = CH // 128
    QT = max(1, NT // 8)
    NQ = NT // QT
    SC = CH // 2
    MM = min(512, CH)
    NMM = CH // MM
    OWN = NCH - 1
    nc = bass.Bass("TRN2", target_bir_lowering=False)
    dram = nc.dram_tensor

    xf = dram("xf", [T, DM], F32, kind="ExternalInput").ap()
    xb = dram("xb", [T, DM], F32, kind="ExternalInput").ap()
    pen = dram("pen", [2, T], F32, kind="ExternalInput").ap()
    gates = dram("gates", [128, 2 * NCH], F32, kind="ExternalInput").ap()
    w_xi = dram("w_xi", [2, DM, DI], F32, kind="ExternalInput").ap()
    w_z = dram("w_z", [2, DM, DI], F32, kind="ExternalInput").ap()
    w_xp = dram("w_xp", [2, DI, 96], F32, kind="ExternalInput").ap()
    w_dt = dram("w_dt", [2, RANK, DI], F32, kind="ExternalInput").ap()
    dtb = dram("dtb", [2, 128, NJ], F32, kind="ExternalInput").ap()
    convw = dram("convw", [2, 128, NJ * 4], F32, kind="ExternalInput").ap()
    convb = dram("convb", [2, 128, NJ], F32, kind="ExternalInput").ap()
    alog = dram("alog", [2, 128, NJ * S], F32, kind="ExternalInput").ap()
    dvec = dram("dvec", [2, 128, NJ], F32, kind="ExternalInput").ap()
    w_out = dram("w_out", [2, DI, DM], F32, kind="ExternalInput").ap()
    w_fus = dram("w_fus", [2 * DM, DM], F32, kind="ExternalInput").ap()
    fusb = dram("fusb", [128, NKD], F32, kind="ExternalInput").ap()
    lng = dram("lng", [128, NKD], F32, kind="ExternalInput").ap()
    lnb = dram("lnb", [128, NKD], F32, kind="ExternalInput").ap()
    w12 = dram("w12", [E, DM, 2 * HF], F32, kind="ExternalInput").ap()
    w3 = dram("w3", [E, HF, DM], F32, kind="ExternalInput").ap()
    wgate = dram("wgate", [DM, E], F32, kind="ExternalInput").ap()
    ident_in = dram("ident", [128, 128], F32, kind="ExternalInput").ap()

    y_out = dram("y_out", [CH, DM], F32, kind="ExternalOutput").ap()

    IK = "ExternalOutput" if dbg else "Internal"
    wxi_h = dram("wxi_h", [2, DM, DI], BF16, kind="Internal").ap()
    wz_h = dram("wz_h", [2, DM, DI], BF16, kind="Internal").ap()
    wout_h = dram("wout_h", [2, DI, DM], BF16, kind="Internal").ap()
    wfus_h = dram("wfus_h", [2 * DM, DM], BF16, kind="Internal").ap()
    w12_h = dram("w12_h", [E, DM, 2 * HF], BF16, kind="Internal").ap()
    w3_h = dram("w3_h", [E, HF, DM], BF16, kind="Internal").ap()
    bc_h = dram("bc_h", [32, CH], BF16, kind=IK).ap()
    xc_h = dram("xc_h", [DI, CH], BF16, kind=IK).ap()
    xn_h = dram("xn_h", [CH, DM], F32, kind=IK).ap()
    xnfm_h = dram("xnfm_h", [DM, CH], BF16, kind=IK).ap()
    yfin_h = dram("yfin_h", [2, DI, CH], BF16, kind=IK).ap()
    p_h = dram("p_h", [E, CH], BF16, kind=IK).ap()
    xid = dram("xid", [DI, CH], BF16, kind=IK).ap() if dbg else None
    xn2d = dram("xn2d", [DM, CH], BF16, kind=IK).ap() if dbg else None

    ctr = dict(ld=0, st=0, pe=0, act=0, vec=0)

    with contextlib.ExitStack() as ES:
        sLD = ES.enter_context(nc.semaphore("sLD"))
        sST = ES.enter_context(nc.semaphore("sST"))
        sPE = ES.enter_context(nc.semaphore("sPE"))
        sACT = ES.enter_context(nc.semaphore("sACT"))
        sVEC = ES.enter_context(nc.semaphore("sVEC"))
        block = ES.enter_context(nc.Block())
        SEM = dict(ld=sLD, st=sST, pe=sPE, act=sACT, vec=sVEC)

        def wall(eng, me):
            for k, sem in SEM.items():
                if ctr[k] > 0:
                    eng.wait_ge(sem, ctr[k])

        def b_(key, inst):
            n = 16 if key in ("ld", "st") else 1
            ctr[key] += n
            inst.then_inc(SEM[key], n)
            return inst

        import inspect as _insp

        def _wants_chain(fn):
            try:
                return len(_insp.signature(fn).parameters) >= 2 and \
                    "C" in _insp.signature(fn).parameters
            except (TypeError, ValueError):
                return False

        def _mk_chain(eng, key):
            def C(inst):
                ctr[key] += 1
                inst.then_inc(SEM[key], 1)
                eng.wait_ge(SEM[key], ctr[key])
                return inst
            return C

        def _group(block_deco, key):
            def wrap(fn):
                @block_deco
                def _(eng):
                    wall(eng, key)
                    if _wants_chain(fn):
                        r = fn(eng, _mk_chain(eng, key))
                    else:
                        r = fn(eng)
                    if r is not None:
                        b_(key, r)
            return wrap

        def SY(fn):
            _group(block.sync, "ld")(fn)

        def GP(fn):
            _group(block.gpsimd, "st")(fn)

        def PE(fn):
            _group(block.tensor, "pe")(fn)

        def ACTG(fn):
            _group(block.scalar, "act")(fn)

        def VEC(fn):
            _group(block.vector, "vec")(fn)

        NCON = 2 * NJ * S + 14 * NJ + 3 * NKD + 2 * NCH + 16
        if True:
            ident_b = ES.enter_context(nc.sbuf_tensor("ident_b", [128, 128], BF16))
            ident_f = ES.enter_context(nc.sbuf_tensor("ident_f", [128, 128], F32))
            consts = ES.enter_context(nc.sbuf_tensor("consts", [128, NCON], F32))
            wdt_sb = ES.enter_context(nc.sbuf_tensor("wdt_sb", [65, 2 * DI], BF16))
            wxp_sb = ES.enter_context(nc.sbuf_tensor("wxp_sb", [128, 2 * NJ * 96], BF16))
            wg_sb = ES.enter_context(nc.sbuf_tensor("wg_sb", [128, NKD * E], BF16))
            oA = 0
            oDTB = oA + 2 * NJ * S
            oCW = oDTB + 2 * NJ
            oCB = oCW + 2 * NJ * 4
            oD = oCB + 2 * NJ
            oFB = oD + 2 * NJ
            oLG = oFB + NKD
            oLB = oLG + NKD
            oGT = oLB + NKD

            def CC(i):
                return consts[:, i:i + 1]

            cA = lambda d, j, s: CC(oA + d * NJ * S + j * S + s)
            cDTB = lambda d, j: CC(oDTB + d * NJ + j)
            cCW = lambda d, j, tap: CC(oCW + d * NJ * 4 + j * 4 + tap)
            cCB = lambda d, j: CC(oCB + d * NJ + j)
            cD = lambda d, j: CC(oD + d * NJ + j)
            cFB = lambda n: CC(oFB + n)
            cLG = lambda n: CC(oLG + n)
            cLB = lambda n: CC(oLB + n)
            cGT = lambda d, c: CC(oGT + d * NCH + c)

            SY(lambda sy: sy.dma_start(out=ident_f[:], in_=ident_in))
            SY(lambda sy: sy.dma_start(out=consts[:, oA:oA + 2 * NJ * S].rearrange("p (d j) -> p d j", d=2), in_=alog.rearrange("d p j -> p d j")))
            SY(lambda sy: sy.dma_start(out=consts[:, oDTB:oDTB + 2 * NJ].rearrange("p (d j) -> p d j", d=2), in_=dtb.rearrange("d p j -> p d j")))
            SY(lambda sy: sy.dma_start(out=consts[:, oCW:oCW + 8 * NJ].rearrange("p (d j) -> p d j", d=2), in_=convw.rearrange("d p j -> p d j")))
            SY(lambda sy: sy.dma_start(out=consts[:, oCB:oCB + 2 * NJ].rearrange("p (d j) -> p d j", d=2), in_=convb.rearrange("d p j -> p d j")))
            SY(lambda sy: sy.dma_start(out=consts[:, oD:oD + 2 * NJ].rearrange("p (d j) -> p d j", d=2), in_=dvec.rearrange("d p j -> p d j")))
            SY(lambda sy: sy.dma_start(out=consts[:, oFB:oFB + NKD], in_=fusb))
            SY(lambda sy: sy.dma_start(out=consts[:, oLG:oLG + NKD], in_=lng))
            SY(lambda sy: sy.dma_start(out=consts[:, oLB:oLB + NKD], in_=lnb))
            SY(lambda sy: sy.dma_start(out=consts[:, oGT:oGT + 2 * NCH], in_=gates))

            oK = oGT + 2 * NCH

            def _regs(v):
                v.memset(CC(oK + 0), 1.0)
                v.memset(CC(oK + 1), EPS)
                last = v.memset(CC(oK + 2), EPS * EPS)
                nc.const_aps.register(F32, 1.0, CC(oK + 0))
                nc.const_aps.register(F32, EPS, CC(oK + 1))
                nc.const_aps.register(F32, EPS * EPS, CC(oK + 2))
                return last
            VEC(_regs)

            ACTG(lambda a: a.activation(consts[:, oA:oA + 2 * NJ * S], consts[:, oA:oA + 2 * NJ * S], AF.Exp))

            def _c0(v):
                v.tensor_scalar_mul(consts[:, oA:oA + 2 * NJ * S], consts[:, oA:oA + 2 * NJ * S], -1.0)
                return v.tensor_copy(ident_b[:], ident_f[:])
            VEC(_c0)

            with contextlib.ExitStack() as ESC:
                cast_f = ESC.enter_context(nc.sbuf_tensor("cast_f", [128, 8192], F32))
                cast_b = ESC.enter_context(nc.sbuf_tensor("cast_b", [128, 8192], BF16))
                jobs = []
                for d in range(2):
                    jobs += [(w_xi[d], wxi_h[d], DM, DI), (w_z[d], wz_h[d], DM, DI),
                             (w_out[d], wout_h[d], DI, DM)]
                jobs.append((w_fus, wfus_h, 2 * DM, DM))
                for e in range(E):
                    jobs += [(w12[e], w12_h[e], DM, 2 * HF), (w3[e], w3_h[e], HF, DM)]
                for src, dst, rows, cols in jobs:
                    csz = min(8192, cols)
                    for r0 in range(0, rows, 128):
                        for c0 in range(0, cols, csz):
                            n = min(csz, cols - c0)
                            SY(lambda sy, src=src, r0=r0, c0=c0, n=n:
                               sy.dma_start(out=cast_f[:, 0:n], in_=src[r0:r0 + 128, c0:c0 + n]))
                            VEC(lambda v, n=n: v.tensor_copy(cast_b[:, 0:n], cast_f[:, 0:n]))
                            GP(lambda g, dst=dst, r0=r0, c0=c0, n=n:
                               g.dma_start(out=dst[r0:r0 + 128, c0:c0 + n], in_=cast_b[:, 0:n]))

                SY(lambda sy: sy.dma_start(out=cast_f[0:RANK, 0:2 * DI].rearrange("p (d n) -> p d n", d=2), in_=w_dt.rearrange("d p n -> p d n")))

                def _sdt(v):
                    v.tensor_copy(wdt_sb[0:RANK, :], cast_f[0:RANK, 0:2 * DI])
                    return v.memset(wdt_sb[RANK:65, :], 1.0)
                VEC(_sdt)

                SY(lambda sy: sy.dma_start(
                    out=cast_f[:, 0:2 * NJ * 96].rearrange("p (d j n) -> p d j n", d=2, j=NJ),
                    in_=w_xp.rearrange("d (j p) n -> p d j n", p=128)))
                VEC(lambda v: v.tensor_copy(wxp_sb[:], cast_f[:, 0:2 * NJ * 96]))
                SY(lambda sy: sy.dma_start(
                    out=cast_f[:, 0:NKD * E].rearrange("p (k e) -> p k e", k=NKD),
                    in_=wgate.rearrange("(k p) e -> p k e", p=128)))
                VEC(lambda v: v.tensor_copy(wg_sb[:], cast_f[:, 0:NKD * E]))

            if True:
                pA = ES.enter_context(nc.psum_tensor("pA", [128, 1024], F32))
                pB = ES.enter_context(nc.psum_tensor("pB", [128, 1024], F32))
                pC = ES.enter_context(nc.psum_tensor("pC", [128, 1024], F32))
                pD = ES.enter_context(nc.psum_tensor("pD", [128, 512], F32))
                pTb = ES.enter_context(nc.psum_tensor("pTb", [128, 1024], BF16))
                with contextlib.ExitStack() as ESS:
                    xtok = ESS.enter_context(nc.sbuf_tensor("xtok", [128, QT * DM], F32))
                    sqv = ESS.enter_context(nc.sbuf_tensor("sqv", [128, QT * DM], F32))
                    lnmv = ESS.enter_context(nc.sbuf_tensor("lnmv", [128, 2 * QT], F32))
                    lnr = ESS.enter_context(nc.sbuf_tensor("lnr", [128, QT], F32))
                    lnmv2 = ESS.enter_context(nc.sbuf_tensor("lnmv2", [128, 2 * QT], F32))
                    xn2t = ESS.enter_context(nc.sbuf_tensor("xn2t", [128, QT * DM], BF16))
                    xn2b = ESS.enter_context(nc.sbuf_tensor("xn2b", [128, NKD * CH], BF16))
                    brep = ESS.enter_context(nc.sbuf_tensor("brep", [128, S * SC], BF16))
                    crep = ESS.enter_context(nc.sbuf_tensor("crep", [128, S * SC], BF16))
                    wrot = ESS.enter_context(nc.sbuf_tensor("wrot", [128, NKD * 128], BF16))
                    wzrot = ESS.enter_context(nc.sbuf_tensor("wzrot", [128, NKD * 128], BF16))
                    xi_j = ESS.enter_context(nc.sbuf_tensor("xi_j", [128, CH + 3], BF16))
                    tails = ESS.enter_context(nc.sbuf_tensor("tails", [128, NJ * 3], BF16))
                    accv = ESS.enter_context(nc.sbuf_tensor("accv", [128, CH], F32))
                    xc_j = ESS.enter_context(nc.sbuf_tensor("xc_j", [128, CH], BF16))
                    xcmv = ESS.enter_context(nc.sbuf_tensor("xcmv", [128, 512], BF16))
                    dtaug = ESS.enter_context(nc.sbuf_tensor("dtaug", [65, CH], BF16))
                    bcsb = ESS.enter_context(nc.sbuf_tensor("bcsb", [32, CH], BF16))
                    pensf = ESS.enter_context(nc.sbuf_tensor("pensf", [1, CH], F32))
                    dlj = ESS.enter_context(nc.sbuf_tensor("dlj", [128, 2 * SC], F32))
                    duj = ESS.enter_context(nc.sbuf_tensor("duj", [128, SC], BF16))
                    dAj = ESS.enter_context(nc.sbuf_tensor("dAj", [128, SC], F32))
                    bj = ESS.enter_context(nc.sbuf_tensor("bj", [128, 2 * SC], BF16))
                    tmpj = ESS.enter_context(nc.sbuf_tensor("tmpj", [128, SC], BF16))
                    hj = ESS.enter_context(nc.sbuf_tensor("hj", [128, 2 * SC], BF16))
                    yj = ESS.enter_context(nc.sbuf_tensor("yj", [128, SC], F32))
                    gj = ESS.enter_context(nc.sbuf_tensor("gj", [128, SC], BF16))
                    yfj = ESS.enter_context(nc.sbuf_tensor("yfj", [128, SC], BF16))
                    states = ESS.enter_context(nc.sbuf_tensor("states", [128, NJ * S], F32))
                    for d in range(2):
                        xsrc = xf if d == 0 else xb

                        def _rst(v):
                            v.memset(states[:], 0.0)
                            return v.memset(tails[:], 0.0)
                        VEC(_rst)

                        for c in range(NCH):
                            own = (c == OWN)
                            t0 = c * CH

                            for q in range(NQ):
                                tq0 = t0 + q * QT * 128
                                SY(lambda sy, tq0=tq0, xsrc=xsrc: sy.dma_start(
                                    out=xtok[:].rearrange("p (a m) -> p a m", a=QT),
                                    in_=xsrc[tq0:tq0 + QT * 128, :].rearrange("(a p) m -> p a m", p=128)))

                                def _ln1(v):
                                    last = None
                                    for a in range(QT):
                                        xt = xtok[:, a * DM:(a + 1) * DM]
                                        last = v.tensor_tensor(sqv[:, a * DM:(a + 1) * DM], xt, xt, ALU.mult)
                                    return last
                                VEC(_ln1)

                                def _ln1r(v):
                                    last = None
                                    for a in range(QT):
                                        v.reduce_sum(lnmv[:, a:a + 1], xtok[:, a * DM:(a + 1) * DM], AX.X)
                                        last = v.reduce_sum(lnmv[:, QT + a:QT + a + 1],
                                                            sqv[:, a * DM:(a + 1) * DM], AX.X)
                                    return last
                                VEC(_ln1r)

                                def _lnl(a_):
                                    a_.activation(lnmv2[:, 0:QT], lnmv[:, 0:QT], AF.Copy, scale=1.0 / DM)
                                    a_.activation(lnmv2[:, QT:2 * QT], lnmv[:, QT:2 * QT], AF.Copy, scale=1.0 / DM)
                                    return a_.activation(lnr[:, 0:QT], lnmv[:, 0:QT], AF.Square, scale=1.0 / DM)
                                ACTG(_lnl)

                                VEC(lambda v: v.tensor_tensor(lnmv2[:, QT:2 * QT], lnmv2[:, QT:2 * QT],
                                                              lnr[:, 0:QT], ALU.subtract))

                                def _lnr(a_, C):
                                    C(a_.activation(lnr[:, 0:QT], lnmv2[:, QT:2 * QT], AF.Ln,
                                                    scale=1.0 + EPS, bias=EPS * EPS))
                                    return a_.activation(lnr[:, 0:QT], lnr[:, 0:QT], AF.Exp, scale=-0.5)
                                ACTG(_lnr)

                                def _ap2(v):
                                    last = None
                                    for a in range(QT):
                                        last = v.tensor_scalar(
                                            xn2t[:, a * DM:(a + 1) * DM], xtok[:, a * DM:(a + 1) * DM],
                                            lnmv2[:, a:a + 1], lnr[:, a:a + 1],
                                            ALU.subtract, ALU.mult)
                                    return last
                                VEC(_ap2)

                                for kb in range(NKD):
                                    for a4 in range(0, QT, 4):
                                        na = min(4, QT - a4)

                                        def _tr(t, kb=kb, a4=a4, na=na):
                                            last = None
                                            for i in range(na):
                                                a = a4 + i
                                                last = t.transpose(
                                                    pTb[:, i * 128:(i + 1) * 128],
                                                    xn2t[:, a * DM + kb * 128:a * DM + (kb + 1) * 128],
                                                    ident_b[:])
                                            return last
                                        PE(_tr)

                                        def _ev(v, kb=kb, q=q, a4=a4, na=na):
                                            col = q * QT * 128 + a4 * 128
                                            return v.tensor_scalar(
                                                xn2b[:, kb * CH + col:kb * CH + col + na * 128],
                                                pTb[:, 0:na * 128], cLG(kb), cLB(kb),
                                                ALU.mult, ALU.add)
                                        VEC(_ev)

                                if own and d == 0:
                                    def _lnr1(a_, C):
                                        C(a_.activation(lnr[:, 0:QT], lnmv2[:, QT:2 * QT], AF.Ln, bias=EPS))
                                        return a_.activation(lnr[:, 0:QT], lnr[:, 0:QT], AF.Exp, scale=-0.5)
                                    ACTG(_lnr1)

                                    def _apx(v):
                                        last = None
                                        for a in range(QT):
                                            last = v.tensor_scalar(
                                                xtok[:, a * DM:(a + 1) * DM], xtok[:, a * DM:(a + 1) * DM],
                                                lnmv2[:, a:a + 1], lnr[:, a:a + 1],
                                                ALU.subtract, ALU.mult)
                                        return last
                                    VEC(_apx)

                                    GP(lambda g, q=q: g.dma_start(
                                        out=xn_h[q * QT * 128:(q + 1) * QT * 128, :].rearrange("(a p) m -> p a m", p=128),
                                        in_=xtok[:].rearrange("p (a m) -> p a m", a=QT)))
                                    VEC(lambda v: v.tensor_copy(xn2t[:], xtok[:]))
                                    for kb in range(NKD):
                                        for a4 in range(0, QT, 4):
                                            na = min(4, QT - a4)

                                            def _trx(t, kb=kb, a4=a4, na=na):
                                                last = None
                                                for i in range(na):
                                                    a = a4 + i
                                                    last = t.transpose(
                                                        pTb[:, i * 128:(i + 1) * 128],
                                                        xn2t[:, a * DM + kb * 128:a * DM + (kb + 1) * 128],
                                                        ident_b[:])
                                                return last
                                            PE(_trx)
                                            ACTG(lambda a_, na=na: a_.activation(
                                                xc_j[:, 0:na * 128], pTb[:, 0:na * 128], AF.Copy))
                                            GP(lambda g, kb=kb, q=q, a4=a4, na=na: g.dma_start(
                                                out=xnfm_h[kb * 128:(kb + 1) * 128,
                                                           q * QT * 128 + a4 * 128:q * QT * 128 + (a4 + na) * 128],
                                                in_=xc_j[:, 0:na * 128]))

                            if dbg and own:
                                GP(lambda g: g.dma_start(
                                    out=xn2d.rearrange("(k p) t -> p k t", p=128),
                                    in_=xn2b[:].rearrange("p (k t) -> p k t", k=NKD)))

                            for j in range(NJ):
                                SY(lambda sy, j=j, d=d: sy.dma_start(
                                    out=wrot[:].rearrange("p (k n) -> p k n", k=NKD),
                                    in_=wxi_h[d][:, j * 128:(j + 1) * 128].rearrange("(k p) n -> p k n", p=128)))

                                for mi in range(NMM):
                                    def _mmx(t, mi=mi):
                                        last = None
                                        for kb in range(NKD):
                                            last = t.matmul(
                                                pA[:, 0:MM], wrot[:, kb * 128:(kb + 1) * 128],
                                                xn2b[:, kb * CH + mi * MM:kb * CH + (mi + 1) * MM],
                                                start=(kb == 0), stop=(kb == NKD - 1))
                                        return last
                                    PE(_mmx)
                                    ACTG(lambda a_, mi=mi: a_.activation(
                                        xi_j[:, 3 + mi * MM:3 + (mi + 1) * MM], pA[:, 0:MM], AF.Copy))

                                def _conv(v, C, j=j, c=c, d=d):
                                    C(v.tensor_scalar(xi_j[:, 0:3], tails[:, j * 3:(j + 1) * 3], cGT(d, c), None, ALU.mult))
                                    C(v.tensor_scalar(accv[:], xi_j[:, 0:CH], cCW(d, j, 0), None, ALU.mult))
                                    for tap in range(1, 4):
                                        C(v.scalar_tensor_tensor(accv[:], xi_j[:, tap:tap + CH], cCW(d, j, tap),
                                                                 accv[:], ALU.mult, ALU.add))
                                    return v.tensor_copy(tails[:, j * 3:(j + 1) * 3], xi_j[:, CH:CH + 3])
                                VEC(_conv)
                                if dbg and own:
                                    GP(lambda g, j=j: g.dma_start(
                                        out=xid[j * 128:(j + 1) * 128, :], in_=xi_j[:, 3:CH + 3]))
                                ACTG(lambda a_, j=j, d=d: a_.activation(xc_j[:], accv[:], AF.Silu, bias=cCB(d, j)))
                                GP(lambda g, j=j: g.dma_start(out=xc_h[j * 128:(j + 1) * 128, :], in_=xc_j[:]))

                            SY(lambda sy, t0=t0, d=d: sy.dma_start(out=pensf[:], in_=pen[d:d + 1, t0:t0 + CH]))
                            ACTG(lambda a_: a_.activation(dtaug[64:65, :], pensf[:], AF.Copy))
                            M5 = min(512, CH)
                            for mi in range(CH // M5):
                                for kb in range(NJ):
                                    SY(lambda sy, mi=mi, kb=kb: sy.dma_start(
                                        out=xcmv[:, 0:M5], in_=xc_h[kb * 128:(kb + 1) * 128, mi * M5:(mi + 1) * M5]))

                                    PE(lambda t, mi=mi, kb=kb, d=d: t.matmul(
                                        pD[0:96, 0:M5], wxp_sb[:, (d * NJ + kb) * 96:(d * NJ + kb + 1) * 96],
                                        xcmv[:, 0:M5], start=(kb == 0), stop=(kb == NJ - 1)))

                                def _evp(a_, mi=mi):
                                    a_.activation(dtaug[0:64, mi * M5:(mi + 1) * M5], pD[0:64, 0:M5], AF.Copy)
                                    return a_.activation(bcsb[:, mi * M5:(mi + 1) * M5], pD[64:96, 0:M5], AF.Copy)
                                ACTG(_evp)

                            GP(lambda g: g.dma_start(out=bc_h[:], in_=bcsb[:]))

                            for sc in range(2):
                                s0 = sc * SC

                                def _ldbr(sy, s0=s0, own=own):
                                    for s in range(S):
                                        b_("ld", sy.dma_start(
                                            out=brep[:, s * SC:(s + 1) * SC],
                                            in_=bc_h[s, s0:s0 + SC].partition_broadcast(128)))
                                    if own:
                                        for s in range(S):
                                            b_("ld", sy.dma_start(
                                                out=crep[:, s * SC:(s + 1) * SC],
                                                in_=bc_h[16 + s, s0:s0 + SC].partition_broadcast(128)))
                                SY(_ldbr)

                                for j in range(NJ):
                                    dl = dlj[:, (j % 2) * SC:(j % 2 + 1) * SC]
                                    SY(lambda sy, j=j, s0=s0: sy.dma_start(
                                        out=xc_j[:, 0:SC], in_=xc_h[j * 128:(j + 1) * 128, s0:s0 + SC]))

                                    def _mdt(t, j=j, s0=s0, d=d):
                                        last = None
                                        for h2 in range(0, SC, MM):
                                            last = t.matmul(
                                                pB[:, h2:h2 + MM],
                                                wdt_sb[:, d * DI + j * 128:d * DI + (j + 1) * 128],
                                                dtaug[:, s0 + h2:s0 + h2 + MM], start=True, stop=True)
                                        return last
                                    PE(_mdt)

                                    def _sp(a_, C, j=j, d=d, dl=dl):
                                        C(a_.activation(dl, pB[:, 0:SC], AF.Exp, bias=cDTB(d, j)))
                                        return a_.activation(dl, dl, AF.Ln, bias=1.0)
                                    ACTG(_sp)

                                    VEC(lambda v, dl=dl: v.tensor_tensor(duj[:], dl, xc_j[:, 0:SC], ALU.mult))

                                    for s in range(S):
                                        ACTG(lambda a_, j=j, s=s, d=d, dl=dl: a_.activation(
                                            dAj[:], dl, AF.Exp, scale=cA(d, j, s)))

                                        def _scan(v, C, j=j, s=s, own=own):
                                            bjs = bj[:, (s % 2) * SC:(s % 2 + 1) * SC]
                                            hjs = hj[:, (s % 2) * SC:(s % 2 + 1) * SC]
                                            C(v.tensor_tensor(bjs, duj[:], brep[:, s * SC:(s + 1) * SC], ALU.mult))
                                            C(v.tensor_tensor_scan(hjs, dAj[:], bjs,
                                                                   states[:, j * S + s:j * S + s + 1],
                                                                   ALU.mult, ALU.add))
                                            last = v.tensor_copy(states[:, j * S + s:j * S + s + 1], hjs[:, SC - 1:SC])
                                            if own:
                                                C(v.tensor_tensor(tmpj[:], hjs, crep[:, s * SC:(s + 1) * SC], ALU.mult))
                                                if s == 0:
                                                    last = v.tensor_copy(yj[:], tmpj[:])
                                                else:
                                                    last = v.tensor_tensor(yj[:], yj[:], tmpj[:], ALU.add)
                                            return last
                                        VEC(_scan)

                                    if own:
                                        SY(lambda sy, j=j, d=d: sy.dma_start(
                                            out=wzrot[:].rearrange("p (k n) -> p k n", k=NKD),
                                            in_=wz_h[d][:, j * 128:(j + 1) * 128].rearrange("(k p) n -> p k n", p=128)))

                                        def _mz(t, s0=s0):
                                            last = None
                                            for h2 in range(0, SC, MM):
                                                for kb in range(NKD):
                                                    last = t.matmul(pC[:, h2:h2 + MM], wzrot[:, kb * 128:(kb + 1) * 128],
                                                                    xn2b[:, kb * CH + s0 + h2:kb * CH + s0 + h2 + MM],
                                                                    start=(kb == 0), stop=(kb == NKD - 1))
                                            return last
                                        PE(_mz)
                                        ACTG(lambda a_: a_.activation(gj[:], pC[:, 0:SC], AF.Silu))

                                        def _yf(v, C, j=j, d=d):
                                            C(v.scalar_tensor_tensor(yj[:], xc_j[:, 0:SC], cD(d, j), yj[:],
                                                                     ALU.mult, ALU.add))
                                            if d == 0:
                                                return v.tensor_tensor(yfj[:], yj[:], gj[:], ALU.mult)
                                            return v.tensor_tensor(yfj[:], yj[:, ::-1], gj[:, ::-1], ALU.mult)
                                        VEC(_yf)

                                        def _styf(g, j=j, s0=s0, d=d):
                                            if d == 0:
                                                return g.dma_start(out=yfin_h[d, j * 128:(j + 1) * 128, s0:s0 + SC],
                                                                   in_=yfj[:])
                                            r0 = CH - (s0 + SC)
                                            return g.dma_start(out=yfin_h[d, j * 128:(j + 1) * 128, r0:r0 + SC],
                                                               in_=yfj[:])
                                        GP(_styf)

                MC = min(512, CH)
                NMC = CH // MC
                with contextlib.ExitStack() as ESP:
                    featf = ESP.enter_context(nc.sbuf_tensor("featf", [128, NKD * CH], BF16))
                    featb = ESP.enter_context(nc.sbuf_tensor("featb", [128, NKD * CH], BF16))
                    wstat = ESP.enter_context(nc.sbuf_tensor("wstat", [128, NJ * 128], BF16))
                    xfm = ESP.enter_context(nc.sbuf_tensor("xfm", [128, NKD * MC], BF16))
                    with contextlib.ExitStack() as ESG:
                        scf = ESG.enter_context(nc.sbuf_tensor("scf", [16, CH], F32))
                        sct = ESG.enter_context(nc.sbuf_tensor("sct", [128, 16], F32))
                        gsc = ESG.enter_context(nc.sbuf_tensor("gsc", [128, 16], F32))
                        pbf = ESG.enter_context(nc.sbuf_tensor("pbf", [16, CH], BF16))
                        for mi in range(NMC):
                            SY(lambda sy, mi=mi: sy.dma_start(
                                out=xfm[:].rearrange("p (k n) -> p k n", k=NKD),
                                in_=xnfm_h[:, mi * MC:(mi + 1) * MC].rearrange("(k p) n -> p k n", p=128)))

                            def _msc(t, mi=mi):
                                last = None
                                for kb in range(NKD):
                                    last = t.matmul(pA[0:E, 0:MC], wg_sb[:, kb * E:(kb + 1) * E],
                                                    xfm[:, kb * MC:(kb + 1) * MC],
                                                    start=(kb == 0), stop=(kb == NKD - 1))
                                return last
                            PE(_msc)
                            ACTG(lambda a_, mi=mi: a_.activation(scf[0:E, mi * MC:(mi + 1) * MC], pA[0:E, 0:MC], AF.Copy))

                        for a in range(NT):
                            PE(lambda t, a=a: t.transpose(pD[:, 0:8], scf[0:E, a * 128:(a + 1) * 128], ident_f[0:8, 0:8]))
                            ACTG(lambda a_: a_.activation(sct[:, 0:8], pD[:, 0:8], AF.Copy))

                            VEC(lambda v: v.reduce_max(sct[:, 8:9], sct[:, 0:8], AX.X, negate=True))
                            ACTG(lambda a_: a_.activation(sct[:, 13:14], sct[:, 8:9], AF.Copy))

                            def _g8(v, C):
                                s8 = sct[:, 0:8]
                                C(v.tensor_scalar(sct[:, 9:10], sct[:, 13:14], -1.0, None, ALU.mult))
                                C(v.tensor_scalar(gsc[:, 0:8], s8, sct[:, 9:10], None, ALU.is_ge))
                                C(v.scalar_tensor_tensor(gsc[:, 8:16], gsc[:, 0:8], -1e30, s8, ALU.mult, ALU.add))
                                return v.reduce_max(sct[:, 10:11], gsc[:, 8:16], AX.X)
                            VEC(_g8)

                            def _gl(a_):
                                a_.activation(sct[:, 14:15], sct[:, 10:11], AF.Copy)
                                return a_.activation(gsc[:, 8:16], sct[:, 0:8], AF.Exp, bias=sct[:, 13:14])
                            ACTG(_gl)

                            def _g9(v, C):
                                C(v.tensor_scalar(gsc[:, 0:8], sct[:, 0:8], sct[:, 14:15], None, ALU.is_ge))
                                C(v.tensor_tensor(gsc[:, 8:16], gsc[:, 8:16], gsc[:, 0:8], ALU.mult))
                                return v.reduce_sum(sct[:, 11:12], gsc[:, 8:16], AX.X)
                            VEC(_g9)
                            ACTG(lambda a_: a_.activation(sct[:, 15:16], sct[:, 11:12], AF.Copy))
                            VEC(lambda v: v.reciprocal(sct[:, 12:13], sct[:, 15:16]))
                            ACTG(lambda a_: a_.activation(sct[:, 9:10], sct[:, 12:13], AF.Copy))
                            VEC(lambda v: v.tensor_scalar(gsc[:, 0:8], gsc[:, 8:16], sct[:, 9:10], None, ALU.mult))
                            PE(lambda t, a=a: t.transpose(pD[0:8, 128:256], gsc[:, 0:8], ident_f[:]))
                            ACTG(lambda a_, a=a: a_.activation(pbf[0:E, a * 128:(a + 1) * 128], pD[0:8, 128:256], AF.Copy))
                        GP(lambda g: g.dma_start(out=p_h[:], in_=pbf[0:E, :]))

                    with nc.sbuf_tensor("yfm", [128, NJ * MC], BF16) as yfm:
                        for d in range(2):
                            feat = featf if d == 0 else featb
                            for mi in range(NMC):
                                SY(lambda sy, d=d, mi=mi: sy.dma_start(
                                    out=yfm[:].rearrange("p (k n) -> p k n", k=NJ),
                                    in_=yfin_h[d][:, mi * MC:(mi + 1) * MC].rearrange("(k p) n -> p k n", p=128)))
                                for n in range(NKD):
                                    SY(lambda sy, d=d, n=n: sy.dma_start(
                                        out=wstat[:].rearrange("p (k n) -> p k n", k=NJ),
                                        in_=wout_h[d][:, n * 128:(n + 1) * 128].rearrange("(k p) n -> p k n", p=128)))

                                    def _mf(t, n=n):
                                        last = None
                                        for k in range(NJ):
                                            last = t.matmul(pA[:, 0:MC], wstat[:, k * 128:(k + 1) * 128],
                                                            yfm[:, k * MC:(k + 1) * MC],
                                                            start=(k == 0), stop=(k == NJ - 1))
                                        return last
                                    PE(_mf)
                                    ACTG(lambda a_, n=n, mi=mi, feat=feat: a_.activation(
                                        feat[:, n * CH + mi * MC:n * CH + (mi + 1) * MC], pA[:, 0:MC], AF.Copy))

                    with contextlib.ExitStack() as ESF:
                        prep = ESF.enter_context(nc.sbuf_tensor("prep", [128, E * MC], BF16))
                        hidden = ESF.enter_context(nc.sbuf_tensor("hidden", [128, NJ * MC], BF16))
                        hsil = ESF.enter_context(nc.sbuf_tensor("hsil", [128, MC], F32))
                        oacc = ESF.enter_context(nc.sbuf_tensor("oacc", [128, NKD * MC], F32))
                        otok = ESF.enter_context(nc.sbuf_tensor("otok", [128, DM], F32))
                        xrl = ESF.enter_context(nc.sbuf_tensor("xrl", [128, DM], F32))
                        xnrl = ESF.enter_context(nc.sbuf_tensor("xnrl", [128, DM], F32))
                        for mi in range(NMC):
                            SY(lambda sy, mi=mi: sy.dma_start(
                                out=xfm[:].rearrange("p (k n) -> p k n", k=NKD),
                                in_=xnfm_h[:, mi * MC:(mi + 1) * MC].rearrange("(k p) n -> p k n", p=128)))

                            def _ldp(sy, mi=mi):
                                for e in range(E):
                                    b_("ld", sy.dma_start(out=prep[:, e * MC:(e + 1) * MC],
                                                          in_=p_h[e, mi * MC:(mi + 1) * MC].partition_broadcast(128)))
                            SY(_ldp)

                            for nn in range(NKD):
                                SY(lambda sy, nn=nn: sy.dma_start(
                                    out=wstat[:].rearrange("p (k n) -> p k n", k=NJ),
                                    in_=wfus_h[:, nn * 128:(nn + 1) * 128].rearrange("(k p) n -> p k n", p=128)))

                                def _mfus(t, nn=nn, mi=mi):
                                    for k in range(NKD):
                                        t.matmul(pB[:, 0:MC], wstat[:, k * 128:(k + 1) * 128],
                                                 featf[:, k * CH + mi * MC:k * CH + (mi + 1) * MC],
                                                 start=(k == 0), stop=False)
                                    last = None
                                    for k in range(NKD):
                                        last = t.matmul(pB[:, 0:MC], wstat[:, (NKD + k) * 128:(NKD + k + 1) * 128],
                                                        featb[:, k * CH + mi * MC:k * CH + (mi + 1) * MC],
                                                        start=False, stop=(k == NKD - 1))
                                    return last
                                PE(_mfus)
                                VEC(lambda v, nn=nn: v.tensor_scalar(
                                    oacc[:, nn * MC:(nn + 1) * MC], pB[:, 0:MC], cFB(nn), None, ALU.add))

                            for e in range(E):
                                for nh in range(NJ):
                                    def _lw12(sy, e=e, nh=nh):
                                        b_("ld", sy.dma_start(
                                            out=wstat[:, 0:NKD * 128].rearrange("p (k n) -> p k n", k=NKD),
                                            in_=w12_h[e][:, nh * 128:(nh + 1) * 128].rearrange("(k p) n -> p k n", p=128)))
                                        b_("ld", sy.dma_start(
                                            out=wstat[:, NKD * 128:2 * NKD * 128].rearrange("p (k n) -> p k n", k=NKD),
                                            in_=w12_h[e][:, HF + nh * 128:HF + (nh + 1) * 128].rearrange("(k p) n -> p k n", p=128)))
                                    SY(_lw12)

                                    def _mh(t, mi=mi):
                                        for kb in range(NKD):
                                            t.matmul(pA[:, 0:MC], wstat[:, kb * 128:(kb + 1) * 128],
                                                     xfm[:, kb * MC:(kb + 1) * MC],
                                                     start=(kb == 0), stop=(kb == NKD - 1))
                                        last = None
                                        for kb in range(NKD):
                                            last = t.matmul(pB[:, 0:MC], wstat[:, (NKD + kb) * 128:(NKD + kb + 1) * 128],
                                                            xfm[:, kb * MC:(kb + 1) * MC],
                                                            start=(kb == 0), stop=(kb == NKD - 1))
                                        return last
                                    PE(_mh)
                                    ACTG(lambda a_: a_.activation(hsil[:], pA[:, 0:MC], AF.Silu))

                                    def _hd(v, C, nh=nh, e=e):
                                        C(v.tensor_tensor(hidden[:, nh * MC:(nh + 1) * MC], hsil[:], pB[:, 0:MC], ALU.mult))
                                        return v.tensor_tensor(hidden[:, nh * MC:(nh + 1) * MC],
                                                               hidden[:, nh * MC:(nh + 1) * MC],
                                                               prep[:, e * MC:(e + 1) * MC], ALU.mult)
                                    VEC(_hd)

                                for nn in range(NKD):
                                    SY(lambda sy, e=e, nn=nn: sy.dma_start(
                                        out=wstat[:, 0:NJ * 128].rearrange("p (k n) -> p k n", k=NJ),
                                        in_=w3_h[e][:, nn * 128:(nn + 1) * 128].rearrange("(k p) n -> p k n", p=128)))

                                    def _mw3(t, nn=nn):
                                        last = None
                                        for k in range(NJ):
                                            last = t.matmul(pC[:, 0:MC], wstat[:, k * 128:(k + 1) * 128],
                                                            hidden[:, k * MC:(k + 1) * MC],
                                                            start=(k == 0), stop=(k == NJ - 1))
                                        return last
                                    PE(_mw3)
                                    VEC(lambda v, nn=nn: v.tensor_tensor(
                                        oacc[:, nn * MC:(nn + 1) * MC], oacc[:, nn * MC:(nn + 1) * MC],
                                        pC[:, 0:MC], ALU.add))

                            for tt in range(MC // 128):
                                def _mtr(t, tt=tt):
                                    last = None
                                    for nn in range(NKD):
                                        last = t.transpose(pA[:, nn * 128:(nn + 1) * 128],
                                                           oacc[:, nn * MC + tt * 128:nn * MC + (tt + 1) * 128],
                                                           ident_f[:])
                                    return last
                                PE(_mtr)
                                r0 = mi * MC + tt * 128

                                def _ldr(sy, r0=r0):
                                    b_("ld", sy.dma_start(out=xrl[:], in_=xf[T - CH + r0:T - CH + r0 + 128, :]))
                                    b_("ld", sy.dma_start(out=xnrl[:], in_=xn_h[r0:r0 + 128, :]))
                                SY(_ldr)

                                def _fin(v, C):
                                    C(v.tensor_tensor(otok[:], pA[:, 0:DM], xrl[:], ALU.add))
                                    return v.tensor_tensor(otok[:], otok[:], xnrl[:], ALU.add)
                                VEC(_fin)
                                GP(lambda g, r0=r0: g.dma_start(out=y_out[r0:r0 + 128, :], in_=otok[:]))

        @block.sync
        def _(sy):
            sy.wait_ge(sST, ctr["st"])

    return nc


_BUILD_CACHE = {}


def _get_nc(T, CH):
    key = (T, CH)
    if key not in _BUILD_CACHE:
        _BUILD_CACHE[key] = build(T, CH)
    return _BUILD_CACHE[key]


def make_inmaps(T, CH, x, weights):
    NCH = T // CH
    OWNT = T // 8
    f32 = np.float32

    def pp(vec, nblk):
        return np.ascontiguousarray(vec.reshape(nblk, 128).T.astype(f32))

    wd = {}
    wd["w_xi"] = np.ascontiguousarray(np.stack([
        weights["f_in_w"][:DI].T, weights["b_in_w"][:DI].T]).astype(f32))
    wd["w_z"] = np.ascontiguousarray(np.stack([
        weights["f_in_w"][DI:].T, weights["b_in_w"][DI:].T]).astype(f32))
    wd["w_xp"] = np.ascontiguousarray(np.stack([
        weights["f_xproj_w"].T, weights["b_xproj_w"].T]).astype(f32))
    wd["w_dt"] = np.ascontiguousarray(np.stack([
        weights["f_dt_w"].T, weights["b_dt_w"].T]).astype(f32))
    wd["dtb"] = np.ascontiguousarray(np.stack([
        pp(weights["f_dt_b"], NJ), pp(weights["b_dt_b"], NJ)]))
    cw = []
    for p in ("f", "b"):
        c = weights[p + "_conv_w"][:, 0, :]
        cw.append(np.ascontiguousarray(
            c.reshape(NJ, 128, 4).transpose(1, 0, 2).reshape(128, NJ * 4).astype(f32)))
    wd["convw"] = np.ascontiguousarray(np.stack(cw))
    wd["convb"] = np.ascontiguousarray(np.stack([
        pp(weights["f_conv_b"], NJ), pp(weights["b_conv_b"], NJ)]))
    al = []
    for p in ("f", "b"):
        a = weights[p + "_A_log"]
        al.append(np.ascontiguousarray(
            a.reshape(NJ, 128, S).transpose(1, 0, 2).reshape(128, NJ * S).astype(f32)))
    wd["alog"] = np.ascontiguousarray(np.stack(al))
    wd["dvec"] = np.ascontiguousarray(np.stack([
        pp(weights["f_D"], NJ), pp(weights["b_D"], NJ)]))
    wd["w_out"] = np.ascontiguousarray(np.stack([
        weights["f_out_w"].T, weights["b_out_w"].T]).astype(f32))
    wd["w_fus"] = np.ascontiguousarray(weights["fus_w"].T.astype(f32))
    wd["fusb"] = pp(weights["fus_b"], NKD)
    wd["lng"] = pp(weights["ln_g"], NKD)
    wd["lnb"] = pp(weights["ln_b"], NKD)
    wd["w12"] = np.ascontiguousarray(weights["ffn_w12"].transpose(0, 2, 1).astype(f32))
    wd["w3"] = np.ascontiguousarray(weights["ffn_w3"].transpose(0, 2, 1).astype(f32))
    wd["wgate"] = np.ascontiguousarray(weights["ffn_gate_w"].T.astype(f32))
    wd["ident"] = np.eye(128, dtype=f32)

    xr = x[::-1]
    in_maps = []
    for k in range(8):
        L = OWNT * (k + 1)
        padf = T - L
        padb = OWNT * k
        xfk = np.concatenate([np.zeros((padf, DM), f32), x[:L]], axis=0)
        xbk = np.concatenate([np.zeros((padb, DM), f32), xr[:T - padb]], axis=0)
        penk = np.zeros((2, T), f32)
        penk[0, :padf] = PEN
        penk[1, :padb] = PEN
        gk = np.ones((128, 2 * NCH), f32)
        npadf = padf // CH
        npadb = padb // CH
        gk[:, 0:npadf + 1] = 0.0
        gk[:, NCH:NCH + npadb + 1] = 0.0
        m = dict(xf=np.ascontiguousarray(xfk), xb=np.ascontiguousarray(xbk),
                 pen=penk, gates=gk)
        m.update(wd)
        in_maps.append(m)
    return in_maps


def kernel(L0_attn_x, L0_attn_ln_g, L0_attn_ln_b, L0_attn_fus_w, L0_attn_fus_b,
           L0_attn_f_in_w, L0_attn_f_conv_w, L0_attn_f_conv_b, L0_attn_f_xproj_w,
           L0_attn_f_dt_w, L0_attn_f_dt_b, L0_attn_f_A_log, L0_attn_f_D, L0_attn_f_out_w,
           L0_attn_b_in_w, L0_attn_b_conv_w, L0_attn_b_conv_b, L0_attn_b_xproj_w,
           L0_attn_b_dt_w, L0_attn_b_dt_b, L0_attn_b_A_log, L0_attn_b_D, L0_attn_b_out_w,
           L0_ffn_w12, L0_ffn_w3, L0_ffn_gate_w):
    T, CH = 16384, 2048
    x = np.ascontiguousarray(np.asarray(L0_attn_x, dtype=np.float32)[0])
    weights = dict(
        ln_g=np.asarray(L0_attn_ln_g), ln_b=np.asarray(L0_attn_ln_b),
        fus_w=np.asarray(L0_attn_fus_w), fus_b=np.asarray(L0_attn_fus_b),
        f_in_w=np.asarray(L0_attn_f_in_w), f_conv_w=np.asarray(L0_attn_f_conv_w),
        f_conv_b=np.asarray(L0_attn_f_conv_b), f_xproj_w=np.asarray(L0_attn_f_xproj_w),
        f_dt_w=np.asarray(L0_attn_f_dt_w), f_dt_b=np.asarray(L0_attn_f_dt_b),
        f_A_log=np.asarray(L0_attn_f_A_log), f_D=np.asarray(L0_attn_f_D),
        f_out_w=np.asarray(L0_attn_f_out_w),
        b_in_w=np.asarray(L0_attn_b_in_w), b_conv_w=np.asarray(L0_attn_b_conv_w),
        b_conv_b=np.asarray(L0_attn_b_conv_b), b_xproj_w=np.asarray(L0_attn_b_xproj_w),
        b_dt_w=np.asarray(L0_attn_b_dt_w), b_dt_b=np.asarray(L0_attn_b_dt_b),
        b_A_log=np.asarray(L0_attn_b_A_log), b_D=np.asarray(L0_attn_b_D),
        b_out_w=np.asarray(L0_attn_b_out_w),
        ffn_w12=np.asarray(L0_ffn_w12), ffn_w3=np.asarray(L0_ffn_w3),
        ffn_gate_w=np.asarray(L0_ffn_gate_w),
    )
    nc = _get_nc(T, CH)
    in_maps = make_inmaps(T, CH, x, weights)
    res = run_bass_kernel_spmd(nc, in_maps, core_ids=list(range(8)))
    out = np.concatenate([res.results[k]["y_out"] for k in range(8)], axis=0)
    return out.reshape(1, T, DM).astype(np.float32)


# version 08
# speedup vs baseline: 2.0067x; 2.0067x over previous
import contextlib
import ml_dtypes
import numpy as np
import concourse.bass as bass
import concourse.mybir as mybir
from concourse.bass_utils import run_bass_kernel_spmd

F32 = mybir.dt.float32
BF16 = mybir.dt.bfloat16
AF = mybir.ActivationFunctionType
ALU = mybir.AluOpType
AX = mybir.AxisListType

DM, DI, S, RANK, E, HF = 1024, 2048, 16, 64, 8, 2048
NJ, NKD = DI // 128, DM // 128
EPS = 1e-5
PEN = -30000.0


def build(T, CH, dbg=False):
    NCH = T // CH
    NT = CH // 128
    QT = max(1, NT // 8)
    NQ = NT // QT
    SC = CH // 2
    MM = min(512, CH)
    NMM = CH // MM
    OWN = NCH - 1
    nc = bass.Bass("TRN2", target_bir_lowering=False)
    dram = nc.dram_tensor

    xf = dram("xf", [T, DM], F32, kind="ExternalInput").ap()
    xb = dram("xb", [T, DM], F32, kind="ExternalInput").ap()
    pen = dram("pen", [2, T], BF16, kind="ExternalInput").ap()
    gates = dram("gates", [128, 2 * NCH], F32, kind="ExternalInput").ap()
    w_xi = dram("w_xi", [2, DM, DI], F32, kind="ExternalInput").ap()
    w_z = dram("w_z", [2, DM, DI], F32, kind="ExternalInput").ap()
    w_xp = dram("w_xp", [2, DI, 96], F32, kind="ExternalInput").ap()
    w_dt = dram("w_dt", [2, RANK, DI], F32, kind="ExternalInput").ap()
    dtb = dram("dtb", [2, 128, NJ], F32, kind="ExternalInput").ap()
    convw = dram("convw", [2, 128, NJ * 4], F32, kind="ExternalInput").ap()
    convb = dram("convb", [2, 128, NJ], F32, kind="ExternalInput").ap()
    alog = dram("alog", [2, 128, NJ * S], F32, kind="ExternalInput").ap()
    dvec = dram("dvec", [2, 128, NJ], F32, kind="ExternalInput").ap()
    w_out = dram("w_out", [2, DI, DM], F32, kind="ExternalInput").ap()
    w_fus = dram("w_fus", [2 * DM, DM], F32, kind="ExternalInput").ap()
    fusb = dram("fusb", [128, NKD], F32, kind="ExternalInput").ap()
    lng = dram("lng", [128, NKD], F32, kind="ExternalInput").ap()
    lnb = dram("lnb", [128, NKD], F32, kind="ExternalInput").ap()
    w12 = dram("w12", [E, DM, 2 * HF], F32, kind="ExternalInput").ap()
    w3 = dram("w3", [E, HF, DM], F32, kind="ExternalInput").ap()
    wgate = dram("wgate", [DM, E], F32, kind="ExternalInput").ap()
    ident_in = dram("ident", [128, 128], F32, kind="ExternalInput").ap()
    cfg = dram("cfg", [1, 4], mybir.dt.int32, kind="ExternalInput").ap()

    y_out = dram("y_out", [CH, DM], F32, kind="ExternalOutput").ap()

    IK = "ExternalOutput" if dbg else "Internal"
    wxi_h = dram("wxi_h", [2, DM, DI], BF16, kind="Internal").ap()
    wz_h = dram("wz_h", [2, DM, DI], BF16, kind="Internal").ap()
    wout_h = dram("wout_h", [2, DI, DM], BF16, kind="Internal").ap()
    wfus_h = dram("wfus_h", [2 * DM, DM], BF16, kind="Internal").ap()
    w12_h = dram("w12_h", [E, DM, 2 * HF], BF16, kind="Internal").ap()
    w3_h = dram("w3_h", [E, HF, DM], BF16, kind="Internal").ap()
    bc_h = dram("bc_h", [32, CH], BF16, kind=IK).ap()
    xc_h = dram("xc_h", [DI, CH], BF16, kind=IK).ap()
    xn_h = dram("xn_h", [CH, DM], F32, kind=IK).ap()
    xnfm_h = dram("xnfm_h", [DM, CH], BF16, kind=IK).ap()
    yfin_h = dram("yfin_h", [2, DI, CH], BF16, kind=IK).ap()
    p_h = dram("p_h", [E, CH], BF16, kind=IK).ap()
    xid = dram("xid", [DI, CH], BF16, kind=IK).ap() if dbg else None
    xn2d = dram("xn2d", [DM, CH], BF16, kind=IK).ap() if dbg else None

    ctr = dict(ld=0, st=0, pe=0, act=0, vec=0)

    with contextlib.ExitStack() as ES:
        sLD = ES.enter_context(nc.semaphore("sLD"))
        sST = ES.enter_context(nc.semaphore("sST"))
        sPE = ES.enter_context(nc.semaphore("sPE"))
        sACT = ES.enter_context(nc.semaphore("sACT"))
        sVEC = ES.enter_context(nc.semaphore("sVEC"))
        block = ES.enter_context(nc.Block())
        SEM = dict(ld=sLD, st=sST, pe=sPE, act=sACT, vec=sVEC)

        waited = {e: dict(ld=0, st=0, pe=0, act=0, vec=0) for e in
                  ("ld", "st", "pe", "act", "vec")}

        def wall(eng, me):
            w_ = waited[me]
            for k, sem in SEM.items():
                if ctr[k] > w_[k]:
                    eng.wait_ge(sem, ctr[k])
                    w_[k] = ctr[k]

        def b_(key, inst):
            n = 16 if key in ("ld", "st") else 1
            ctr[key] += n
            inst.then_inc(SEM[key], n)
            return inst

        import inspect as _insp

        def _wants_chain(fn):
            try:
                return len(_insp.signature(fn).parameters) >= 2 and \
                    "C" in _insp.signature(fn).parameters
            except (TypeError, ValueError):
                return False

        def _mk_chain(eng, key):
            def C(inst):
                ctr[key] += 1
                inst.then_inc(SEM[key], 1)
                eng.wait_ge(SEM[key], ctr[key])
                return inst
            return C

        def _group(block_deco, key):
            def wrap(fn):
                @block_deco
                def _(eng):
                    wall(eng, key)
                    if _wants_chain(fn):
                        r = fn(eng, _mk_chain(eng, key))
                    else:
                        r = fn(eng)
                    if r is not None:
                        b_(key, r)
            return wrap

        def SY(fn):
            _group(block.sync, "ld")(fn)

        def GP(fn):
            _group(block.gpsimd, "st")(fn)

        def PE(fn):
            _group(block.tensor, "pe")(fn)

        def ACTG(fn):
            _group(block.scalar, "act")(fn)

        def VEC(fn):
            _group(block.vector, "vec")(fn)

        NCON = 2 * NJ * S + 14 * NJ + 3 * NKD + 2 * NCH + 16
        if True:
            ident_b = ES.enter_context(nc.sbuf_tensor("ident_b", [128, 128], BF16))
            ident_f = ES.enter_context(nc.sbuf_tensor("ident_f", [128, 128], F32))
            consts = ES.enter_context(nc.sbuf_tensor("consts", [128, NCON], F32))
            wdt_sb = ES.enter_context(nc.sbuf_tensor("wdt_sb", [65, 2 * DI], BF16))
            wxp_sb = ES.enter_context(nc.sbuf_tensor("wxp_sb", [128, 2 * NJ * 96], BF16))
            wg_sb = ES.enter_context(nc.sbuf_tensor("wg_sb", [128, NKD * E], BF16))
            oA = 0
            oDTB = oA + 2 * NJ * S
            oCW = oDTB + 2 * NJ
            oCB = oCW + 2 * NJ * 4
            oD = oCB + 2 * NJ
            oFB = oD + 2 * NJ
            oLG = oFB + NKD
            oLB = oLG + NKD
            oGT = oLB + NKD

            def CC(i):
                return consts[:, i:i + 1]

            cA = lambda d, j, s: CC(oA + d * NJ * S + j * S + s)
            cDTB = lambda d, j: CC(oDTB + d * NJ + j)
            cCW = lambda d, j, tap: CC(oCW + d * NJ * 4 + j * 4 + tap)
            cCB = lambda d, j: CC(oCB + d * NJ + j)
            cD = lambda d, j: CC(oD + d * NJ + j)
            cFB = lambda n: CC(oFB + n)
            cLG = lambda n: CC(oLG + n)
            cLB = lambda n: CC(oLB + n)
            cGT = lambda d, c: CC(oGT + d * NCH + c)

            SY(lambda sy: sy.dma_start(out=ident_f[:], in_=ident_in))
            SY(lambda sy: sy.dma_start(out=consts[:, oA:oA + 2 * NJ * S].rearrange("p (d j) -> p d j", d=2), in_=alog.rearrange("d p j -> p d j")))
            SY(lambda sy: sy.dma_start(out=consts[:, oDTB:oDTB + 2 * NJ].rearrange("p (d j) -> p d j", d=2), in_=dtb.rearrange("d p j -> p d j")))
            SY(lambda sy: sy.dma_start(out=consts[:, oCW:oCW + 8 * NJ].rearrange("p (d j) -> p d j", d=2), in_=convw.rearrange("d p j -> p d j")))
            SY(lambda sy: sy.dma_start(out=consts[:, oCB:oCB + 2 * NJ].rearrange("p (d j) -> p d j", d=2), in_=convb.rearrange("d p j -> p d j")))
            SY(lambda sy: sy.dma_start(out=consts[:, oD:oD + 2 * NJ].rearrange("p (d j) -> p d j", d=2), in_=dvec.rearrange("d p j -> p d j")))
            SY(lambda sy: sy.dma_start(out=consts[:, oFB:oFB + NKD], in_=fusb))
            SY(lambda sy: sy.dma_start(out=consts[:, oLG:oLG + NKD], in_=lng))
            SY(lambda sy: sy.dma_start(out=consts[:, oLB:oLB + NKD], in_=lnb))
            SY(lambda sy: sy.dma_start(out=consts[:, oGT:oGT + 2 * NCH], in_=gates))

            oK = oGT + 2 * NCH

            def _regs(v):
                v.memset(CC(oK + 0), 1.0)
                v.memset(CC(oK + 1), EPS)
                last = v.memset(CC(oK + 2), EPS * EPS)
                nc.const_aps.register(F32, 1.0, CC(oK + 0))
                nc.const_aps.register(F32, EPS, CC(oK + 1))
                nc.const_aps.register(F32, EPS * EPS, CC(oK + 2))
                return last
            VEC(_regs)

            ACTG(lambda a: a.activation(consts[:, oA:oA + 2 * NJ * S], consts[:, oA:oA + 2 * NJ * S], AF.Exp))

            def _c0(v):
                v.tensor_scalar_mul(consts[:, oA:oA + 2 * NJ * S], consts[:, oA:oA + 2 * NJ * S], -1.0)
                return v.tensor_copy(ident_b[:], ident_f[:])
            VEC(_c0)

            cfg_sb = ES.enter_context(nc.sbuf_tensor("cfg_sb", [1, 512], mybir.dt.int32))
            scr_sb = ES.enter_context(nc.sbuf_tensor("scr_sb", [1, 64], F32))
            dumcol = [8]
            SY(lambda sy: sy.dma_start(out=cfg_sb[0:1, 0:4], in_=cfg))
            skipreg = {}
            _edecos = [("ld", block.sync), ("st", block.gpsimd), ("pe", block.tensor),
                       ("act", block.scalar), ("vec", block.vector)]
            for _k, _deco in _edecos:
                @_deco
                def _(eng, _k=_k):
                    wall(eng, _k)
                    pid = eng.partition_id()
                    r0 = eng.to_reg(NCH - 1 - pid)
                    r1 = eng.to_reg(pid)
                    skipreg[_k] = (r0, r1)

            with contextlib.ExitStack() as ESC:
                cast_f = ESC.enter_context(nc.sbuf_tensor("cast_f", [128, 8192], F32))
                cast_b = ESC.enter_context(nc.sbuf_tensor("cast_b", [128, 8192], BF16))
                jobs = []
                for d in range(2):
                    jobs += [(w_xi[d], wxi_h[d], DM, DI), (w_z[d], wz_h[d], DM, DI),
                             (w_out[d], wout_h[d], DI, DM)]
                jobs.append((w_fus, wfus_h, 2 * DM, DM))
                for e in range(E):
                    jobs += [(w12[e], w12_h[e], DM, 2 * HF), (w3[e], w3_h[e], HF, DM)]
                for src, dst, rows, cols in jobs:
                    csz = min(8192, cols)
                    for r0 in range(0, rows, 128):
                        for c0 in range(0, cols, csz):
                            n = min(csz, cols - c0)
                            SY(lambda sy, src=src, r0=r0, c0=c0, n=n:
                               sy.dma_start(out=cast_f[:, 0:n], in_=src[r0:r0 + 128, c0:c0 + n]))
                            VEC(lambda v, n=n: v.tensor_copy(cast_b[:, 0:n], cast_f[:, 0:n]))
                            GP(lambda g, dst=dst, r0=r0, c0=c0, n=n:
                               g.dma_start(out=dst[r0:r0 + 128, c0:c0 + n], in_=cast_b[:, 0:n]))

                SY(lambda sy: sy.dma_start(out=cast_f[0:RANK, 0:2 * DI].rearrange("p (d n) -> p d n", d=2), in_=w_dt.rearrange("d p n -> p d n")))

                def _sdt(v):
                    v.tensor_copy(wdt_sb[0:RANK, :], cast_f[0:RANK, 0:2 * DI])
                    return v.memset(wdt_sb[RANK:65, :], 1.0)
                VEC(_sdt)

                SY(lambda sy: sy.dma_start(
                    out=cast_f[:, 0:2 * NJ * 96].rearrange("p (d j n) -> p d j n", d=2, j=NJ),
                    in_=w_xp.rearrange("d (j p) n -> p d j n", p=128)))
                VEC(lambda v: v.tensor_copy(wxp_sb[:], cast_f[:, 0:2 * NJ * 96]))
                SY(lambda sy: sy.dma_start(
                    out=cast_f[:, 0:NKD * E].rearrange("p (k e) -> p k e", k=NKD),
                    in_=wgate.rearrange("(k p) e -> p k e", p=128)))
                VEC(lambda v: v.tensor_copy(wg_sb[:], cast_f[:, 0:NKD * E]))

            if True:
                pA = ES.enter_context(nc.psum_tensor("pA", [128, 1024], F32))
                pB = ES.enter_context(nc.psum_tensor("pB", [128, 1024], F32))
                pC = ES.enter_context(nc.psum_tensor("pC", [128, 1024], F32))
                pD = ES.enter_context(nc.psum_tensor("pD", [128, 512], F32))
                pTb = ES.enter_context(nc.psum_tensor("pTb", [128, 1024], BF16))
                with contextlib.ExitStack() as ESS:
                    xtok = ESS.enter_context(nc.sbuf_tensor("xtok", [128, QT * DM], F32))
                    sqv = ESS.enter_context(nc.sbuf_tensor("sqv", [128, QT * DM], F32))
                    lnmv = ESS.enter_context(nc.sbuf_tensor("lnmv", [128, 2 * QT], F32))
                    lnr = ESS.enter_context(nc.sbuf_tensor("lnr", [128, QT], F32))
                    lnmv2 = ESS.enter_context(nc.sbuf_tensor("lnmv2", [128, 2 * QT], F32))
                    xn2t = ESS.enter_context(nc.sbuf_tensor("xn2t", [128, QT * DM], BF16))
                    xn2b = ESS.enter_context(nc.sbuf_tensor("xn2b", [128, NKD * CH], BF16))
                    brep = ESS.enter_context(nc.sbuf_tensor("brep", [128, S * SC], BF16))
                    crep = ESS.enter_context(nc.sbuf_tensor("crep", [128, S * SC], BF16))
                    wrot = ESS.enter_context(nc.sbuf_tensor("wrot", [128, NKD * 128], BF16))
                    wzrot = ESS.enter_context(nc.sbuf_tensor("wzrot", [128, NKD * 128], BF16))
                    xi_j = ESS.enter_context(nc.sbuf_tensor("xi_j", [128, CH + 3], BF16))
                    tails = ESS.enter_context(nc.sbuf_tensor("tails", [128, NJ * 3], BF16))
                    accv = ESS.enter_context(nc.sbuf_tensor("accv", [128, CH], F32))
                    xc_j = ESS.enter_context(nc.sbuf_tensor("xc_j", [128, CH], BF16))
                    xcmv = ESS.enter_context(nc.sbuf_tensor("xcmv", [128, 512], BF16))
                    dtaug = ESS.enter_context(nc.sbuf_tensor("dtaug", [65, CH], BF16))
                    bcsb = ESS.enter_context(nc.sbuf_tensor("bcsb", [32, CH], BF16))
                    pensf = ESS.enter_context(nc.sbuf_tensor("pensf", [1, CH], BF16))
                    dlj = ESS.enter_context(nc.sbuf_tensor("dlj", [128, 2 * SC], F32))
                    duj = ESS.enter_context(nc.sbuf_tensor("duj", [128, SC], BF16))
                    dAj = ESS.enter_context(nc.sbuf_tensor("dAj", [128, 2 * SC], F32))
                    bj = ESS.enter_context(nc.sbuf_tensor("bj", [128, 2 * SC], BF16))
                    tmpj = ESS.enter_context(nc.sbuf_tensor("tmpj", [128, SC], BF16))
                    hj = ESS.enter_context(nc.sbuf_tensor("hj", [128, 2 * SC], BF16))
                    yj = ESS.enter_context(nc.sbuf_tensor("yj", [128, SC], F32))
                    gj = ESS.enter_context(nc.sbuf_tensor("gj", [128, SC], BF16))
                    yfj = ESS.enter_context(nc.sbuf_tensor("yfj", [128, SC], BF16))
                    states = ESS.enter_context(nc.sbuf_tensor("states", [128, NJ * S], F32))
                    for d in range(2):
                        xsrc = xf if d == 0 else xb

                        def _rst(v):
                            v.memset(states[:], 0.0)
                            return v.memset(tails[:], 0.0)
                        VEC(_rst)

                        for c in range(NCH):
                            own = (c == OWN)
                            t0 = c * CH

                            skippable = (c < OWN)
                            guards = {}
                            snap = dict(ctr)
                            if skippable:
                                for _k, _deco in _edecos:
                                    @_deco
                                    def _(eng, _k=_k, c=c, d=d):
                                        g = eng.If_lt(skipreg[_k][d], c + 1)
                                        g.__enter__()
                                        guards[_k] = g

                            for q in range(NQ):
                                tq0 = t0 + q * QT * 128
                                SY(lambda sy, tq0=tq0, xsrc=xsrc: sy.dma_start(
                                    out=xtok[:].rearrange("p (a m) -> p a m", a=QT),
                                    in_=xsrc[tq0:tq0 + QT * 128, :].rearrange("(a p) m -> p a m", p=128)))

                                def _ln1(v):
                                    last = None
                                    for a in range(QT):
                                        xt = xtok[:, a * DM:(a + 1) * DM]
                                        last = v.tensor_tensor(sqv[:, a * DM:(a + 1) * DM], xt, xt, ALU.mult)
                                    return last
                                VEC(_ln1)

                                def _ln1r(v):
                                    last = None
                                    for a in range(QT):
                                        v.reduce_sum(lnmv[:, a:a + 1], xtok[:, a * DM:(a + 1) * DM], AX.X)
                                        last = v.reduce_sum(lnmv[:, QT + a:QT + a + 1],
                                                            sqv[:, a * DM:(a + 1) * DM], AX.X)
                                    return last
                                VEC(_ln1r)

                                def _lnl(a_):
                                    a_.activation(lnmv2[:, 0:QT], lnmv[:, 0:QT], AF.Copy, scale=1.0 / DM)
                                    a_.activation(lnmv2[:, QT:2 * QT], lnmv[:, QT:2 * QT], AF.Copy, scale=1.0 / DM)
                                    return a_.activation(lnr[:, 0:QT], lnmv[:, 0:QT], AF.Square, scale=1.0 / DM)
                                ACTG(_lnl)

                                VEC(lambda v: v.tensor_tensor(lnmv2[:, QT:2 * QT], lnmv2[:, QT:2 * QT],
                                                              lnr[:, 0:QT], ALU.subtract))

                                def _lnr(a_, C):
                                    C(a_.activation(lnr[:, 0:QT], lnmv2[:, QT:2 * QT], AF.Ln,
                                                    scale=1.0 + EPS, bias=EPS * EPS))
                                    return a_.activation(lnr[:, 0:QT], lnr[:, 0:QT], AF.Exp, scale=-0.5)
                                ACTG(_lnr)

                                def _ap2(v):
                                    last = None
                                    for a in range(QT):
                                        last = v.tensor_scalar(
                                            xn2t[:, a * DM:(a + 1) * DM], xtok[:, a * DM:(a + 1) * DM],
                                            lnmv2[:, a:a + 1], lnr[:, a:a + 1],
                                            ALU.subtract, ALU.mult)
                                    return last
                                VEC(_ap2)

                                for kb in range(NKD):
                                    for a4 in range(0, QT, 4):
                                        na = min(4, QT - a4)

                                        def _tr(t, kb=kb, a4=a4, na=na):
                                            last = None
                                            for i in range(na):
                                                a = a4 + i
                                                last = t.transpose(
                                                    pTb[:, i * 128:(i + 1) * 128],
                                                    xn2t[:, a * DM + kb * 128:a * DM + (kb + 1) * 128],
                                                    ident_b[:])
                                            return last
                                        PE(_tr)

                                        def _ev(v, kb=kb, q=q, a4=a4, na=na):
                                            col = q * QT * 128 + a4 * 128
                                            return v.tensor_scalar(
                                                xn2b[:, kb * CH + col:kb * CH + col + na * 128],
                                                pTb[:, 0:na * 128], cLG(kb), cLB(kb),
                                                ALU.mult, ALU.add)
                                        VEC(_ev)

                                if own and d == 0:
                                    def _lnr1(a_, C):
                                        C(a_.activation(lnr[:, 0:QT], lnmv2[:, QT:2 * QT], AF.Ln, bias=EPS))
                                        return a_.activation(lnr[:, 0:QT], lnr[:, 0:QT], AF.Exp, scale=-0.5)
                                    ACTG(_lnr1)

                                    def _apx(v):
                                        last = None
                                        for a in range(QT):
                                            last = v.tensor_scalar(
                                                xtok[:, a * DM:(a + 1) * DM], xtok[:, a * DM:(a + 1) * DM],
                                                lnmv2[:, a:a + 1], lnr[:, a:a + 1],
                                                ALU.subtract, ALU.mult)
                                        return last
                                    VEC(_apx)

                                    GP(lambda g, q=q: g.dma_start(
                                        out=xn_h[q * QT * 128:(q + 1) * QT * 128, :].rearrange("(a p) m -> p a m", p=128),
                                        in_=xtok[:].rearrange("p (a m) -> p a m", a=QT)))
                                    VEC(lambda v: v.tensor_copy(xn2t[:], xtok[:]))
                                    for kb in range(NKD):
                                        for a4 in range(0, QT, 4):
                                            na = min(4, QT - a4)

                                            def _trx(t, kb=kb, a4=a4, na=na):
                                                last = None
                                                for i in range(na):
                                                    a = a4 + i
                                                    last = t.transpose(
                                                        pTb[:, i * 128:(i + 1) * 128],
                                                        xn2t[:, a * DM + kb * 128:a * DM + (kb + 1) * 128],
                                                        ident_b[:])
                                                return last
                                            PE(_trx)
                                            ACTG(lambda a_, na=na: a_.activation(
                                                xc_j[:, 0:na * 128], pTb[:, 0:na * 128], AF.Copy))
                                            GP(lambda g, kb=kb, q=q, a4=a4, na=na: g.dma_start(
                                                out=xnfm_h[kb * 128:(kb + 1) * 128,
                                                           q * QT * 128 + a4 * 128:q * QT * 128 + (a4 + na) * 128],
                                                in_=xc_j[:, 0:na * 128]))

                            if dbg and own:
                                GP(lambda g: g.dma_start(
                                    out=xn2d.rearrange("(k p) t -> p k t", p=128),
                                    in_=xn2b[:].rearrange("p (k t) -> p k t", k=NKD)))

                            for j in range(NJ):
                                SY(lambda sy, j=j, d=d: sy.dma_start(
                                    out=wrot[:].rearrange("p (k n) -> p k n", k=NKD),
                                    in_=wxi_h[d][:, j * 128:(j + 1) * 128].rearrange("(k p) n -> p k n", p=128)))

                                for mi in range(NMM):
                                    def _mmx(t, mi=mi):
                                        last = None
                                        for kb in range(NKD):
                                            last = t.matmul(
                                                pA[:, 0:MM], wrot[:, kb * 128:(kb + 1) * 128],
                                                xn2b[:, kb * CH + mi * MM:kb * CH + (mi + 1) * MM],
                                                start=(kb == 0), stop=(kb == NKD - 1))
                                        return last
                                    PE(_mmx)
                                    ACTG(lambda a_, mi=mi: a_.activation(
                                        xi_j[:, 3 + mi * MM:3 + (mi + 1) * MM], pA[:, 0:MM], AF.Copy))

                                def _conv(v, C, j=j, c=c, d=d):
                                    C(v.tensor_scalar(xi_j[:, 0:3], tails[:, j * 3:(j + 1) * 3], cGT(d, c), None, ALU.mult))
                                    C(v.tensor_scalar(accv[:], xi_j[:, 0:CH], cCW(d, j, 0), None, ALU.mult))
                                    for tap in range(1, 4):
                                        C(v.scalar_tensor_tensor(accv[:], xi_j[:, tap:tap + CH], cCW(d, j, tap),
                                                                 accv[:], ALU.mult, ALU.add))
                                    return v.tensor_copy(tails[:, j * 3:(j + 1) * 3], xi_j[:, CH:CH + 3])
                                VEC(_conv)
                                if dbg and own:
                                    GP(lambda g, j=j: g.dma_start(
                                        out=xid[j * 128:(j + 1) * 128, :], in_=xi_j[:, 3:CH + 3]))
                                ACTG(lambda a_, j=j, d=d: a_.activation(xc_j[:], accv[:], AF.Silu, bias=cCB(d, j)))
                                GP(lambda g, j=j: g.dma_start(out=xc_h[j * 128:(j + 1) * 128, :], in_=xc_j[:]))

                            SY(lambda sy, t0=t0, d=d: sy.dma_start(out=pensf[:], in_=pen[d:d + 1, t0:t0 + CH]))
                            ACTG(lambda a_: a_.activation(dtaug[64:65, :], pensf[:], AF.Copy))
                            M5 = min(512, CH)
                            for mi in range(CH // M5):
                                for kb in range(NJ):
                                    SY(lambda sy, mi=mi, kb=kb: sy.dma_start(
                                        out=xcmv[:, 0:M5], in_=xc_h[kb * 128:(kb + 1) * 128, mi * M5:(mi + 1) * M5]))

                                    PE(lambda t, mi=mi, kb=kb, d=d: t.matmul(
                                        pD[0:96, 0:M5], wxp_sb[:, (d * NJ + kb) * 96:(d * NJ + kb + 1) * 96],
                                        xcmv[:, 0:M5], start=(kb == 0), stop=(kb == NJ - 1)))

                                def _evp(a_, mi=mi):
                                    a_.activation(dtaug[0:64, mi * M5:(mi + 1) * M5], pD[0:64, 0:M5], AF.Copy)
                                    return a_.activation(bcsb[:, mi * M5:(mi + 1) * M5], pD[64:96, 0:M5], AF.Copy)
                                ACTG(_evp)

                            GP(lambda g: g.dma_start(out=bc_h[:], in_=bcsb[:]))

                            for sc in range(2):
                                s0 = sc * SC

                                def _ldbr(sy, s0=s0, own=own):
                                    for s in range(S):
                                        b_("ld", sy.dma_start(
                                            out=brep[:, s * SC:(s + 1) * SC],
                                            in_=bc_h[s, s0:s0 + SC].partition_broadcast(128)))
                                    if own:
                                        for s in range(S):
                                            b_("ld", sy.dma_start(
                                                out=crep[:, s * SC:(s + 1) * SC],
                                                in_=bc_h[16 + s, s0:s0 + SC].partition_broadcast(128)))
                                SY(_ldbr)

                                for j in range(NJ):
                                    dl = dlj[:, (j % 2) * SC:(j % 2 + 1) * SC]
                                    SY(lambda sy, j=j, s0=s0: sy.dma_start(
                                        out=xc_j[:, 0:SC], in_=xc_h[j * 128:(j + 1) * 128, s0:s0 + SC]))

                                    def _mdt(t, j=j, s0=s0, d=d):
                                        last = None
                                        ms = min(MM, SC)
                                        for h2 in range(0, SC, ms):
                                            last = t.matmul(
                                                pB[:, h2:h2 + ms],
                                                wdt_sb[:, d * DI + j * 128:d * DI + (j + 1) * 128],
                                                dtaug[:, s0 + h2:s0 + h2 + ms], start=True, stop=True)
                                        return last
                                    PE(_mdt)

                                    def _sp(a_, C, j=j, d=d, dl=dl):
                                        C(a_.activation(dl, pB[:, 0:SC], AF.Exp, bias=cDTB(d, j)))
                                        return a_.activation(dl, dl, AF.Ln, bias=1.0)
                                    ACTG(_sp)

                                    VEC(lambda v, dl=dl: v.tensor_tensor(duj[:], dl, xc_j[:, 0:SC], ALU.mult))

                                    act_hist = {}
                                    bcnt = {}
                                    scnt = {}
                                    sp_cnt = ctr["act"]

                                    VEC(lambda v: v.tensor_tensor(bj[:, 0:SC], duj[:], brep[:, 0:SC], ALU.mult))
                                    bcnt[0] = ctr["vec"]

                                    for s in range(S):
                                        @block.scalar
                                        def _(a_, j=j, s=s, d=d, dl=dl):
                                            if s == 0:
                                                a_.wait_ge(sACT, sp_cnt)
                                            if s >= 2:
                                                a_.wait_ge(sVEC, scnt[s - 2])
                                            inst = a_.activation(dAj[:, (s % 2) * SC:(s % 2 + 1) * SC],
                                                                 dl, AF.Exp, scale=cA(d, j, s))
                                            ctr["act"] += 1
                                            inst.then_inc(sACT, 1)
                                            act_hist[s] = ctr["act"]

                                        @block.vector
                                        def _(v, j=j, s=s, own=own):
                                            v.wait_ge(sACT, act_hist[s])
                                            v.wait_ge(sVEC, bcnt[s])
                                            hjs = hj[:, (s % 2) * SC:(s % 2 + 1) * SC]
                                            i1 = v.tensor_tensor_scan(
                                                hjs, dAj[:, (s % 2) * SC:(s % 2 + 1) * SC],
                                                bj[:, (s % 2) * SC:(s % 2 + 1) * SC],
                                                states[:, j * S + s:j * S + s + 1], ALU.mult, ALU.add)
                                            ctr["vec"] += 1
                                            i1.then_inc(sVEC, 1)
                                            scnt[s] = ctr["vec"]
                                            if s + 1 < S:
                                                i2 = v.tensor_tensor(
                                                    bj[:, ((s + 1) % 2) * SC:((s + 1) % 2 + 1) * SC],
                                                    duj[:], brep[:, (s + 1) * SC:(s + 2) * SC], ALU.mult)
                                                ctr["vec"] += 1
                                                i2.then_inc(sVEC, 1)
                                                bcnt[s + 1] = ctr["vec"]
                                            v.wait_ge(sVEC, scnt[s])
                                            i3 = v.tensor_copy(states[:, j * S + s:j * S + s + 1], hjs[:, SC - 1:SC])
                                            ctr["vec"] += 1
                                            i3.then_inc(sVEC, 1)
                                            if own:
                                                i4 = v.tensor_tensor(tmpj[:], hjs, crep[:, s * SC:(s + 1) * SC], ALU.mult)
                                                ctr["vec"] += 1
                                                i4.then_inc(sVEC, 1)
                                                v.wait_ge(sVEC, ctr["vec"])
                                                if s == 0:
                                                    i5 = v.tensor_copy(yj[:], tmpj[:])
                                                else:
                                                    i5 = v.tensor_tensor(yj[:], yj[:], tmpj[:], ALU.add)
                                                ctr["vec"] += 1
                                                i5.then_inc(sVEC, 1)

                                    if own:
                                        SY(lambda sy, j=j, d=d: sy.dma_start(
                                            out=wzrot[:].rearrange("p (k n) -> p k n", k=NKD),
                                            in_=wz_h[d][:, j * 128:(j + 1) * 128].rearrange("(k p) n -> p k n", p=128)))

                                        def _mz(t, s0=s0):
                                            last = None
                                            ms = min(MM, SC)
                                            for h2 in range(0, SC, ms):
                                                for kb in range(NKD):
                                                    last = t.matmul(pC[:, h2:h2 + ms], wzrot[:, kb * 128:(kb + 1) * 128],
                                                                    xn2b[:, kb * CH + s0 + h2:kb * CH + s0 + h2 + ms],
                                                                    start=(kb == 0), stop=(kb == NKD - 1))
                                            return last
                                        PE(_mz)
                                        ACTG(lambda a_: a_.activation(gj[:], pC[:, 0:SC], AF.Silu))

                                        def _yf(v, C, j=j, d=d):
                                            C(v.scalar_tensor_tensor(yj[:], xc_j[:, 0:SC], cD(d, j), yj[:],
                                                                     ALU.mult, ALU.add))
                                            if d == 0:
                                                return v.tensor_tensor(yfj[:], yj[:], gj[:], ALU.mult)
                                            return v.tensor_tensor(yfj[:], yj[:, ::-1], gj[:, ::-1], ALU.mult)
                                        VEC(_yf)

                                        def _styf(g, j=j, s0=s0, d=d):
                                            if d == 0:
                                                return g.dma_start(out=yfin_h[d, j * 128:(j + 1) * 128, s0:s0 + SC],
                                                                   in_=yfj[:])
                                            r0 = CH - (s0 + SC)
                                            return g.dma_start(out=yfin_h[d, j * 128:(j + 1) * 128, r0:r0 + SC],
                                                               in_=yfj[:])
                                        GP(_styf)

                            if skippable:
                                deltas = {k: ctr[k] - snap[k] for k in ctr}
                                for _k, _deco in _edecos:
                                    @_deco
                                    def _(eng, _k=_k, c=c, d=d, snap=snap):
                                        guards[_k].__exit__(None, None, None)
                                        with eng.Else():
                                            if deltas[_k] > 0:
                                                eng.wait_ge(SEM[_k], snap[_k])
                                                rem = deltas[_k]
                                                pi = 0
                                                if _k == "pe":
                                                    eng.wait_ge(SEM["act"], snap["act"])
                                                while rem > 0:
                                                    piece = min(240, rem)
                                                    if _k in ("ld", "st"):
                                                        col = dumcol[0]
                                                        dumcol[0] += 1
                                                        assert col < 512
                                                        eng.dma_start(
                                                            out=cfg_sb[0:1, col:col + 1],
                                                            in_=cfg_sb[0:1, 0:1]).then_inc(SEM[_k], piece)
                                                    elif _k == "pe":
                                                        eng.matmul(pD[0:8, 0:8], ident_b[0:8, 0:8],
                                                                   ident_b[0:8, 0:8], start=True,
                                                                   stop=True).then_inc(SEM[_k], piece)
                                                    elif _k == "act":
                                                        assert pi < 32
                                                        eng.activation(scr_sb[0:1, pi:pi + 1],
                                                                       consts[0:1, oK:oK + 1],
                                                                       AF.Copy).then_inc(SEM[_k], piece)
                                                    else:
                                                        assert pi < 32
                                                        eng.memset(scr_sb[0:1, 32 + pi:33 + pi],
                                                                   0.0).then_inc(SEM[_k], piece)
                                                    pi += 1
                                                    rem -= piece
                                for _e in waited:
                                    for _k in waited[_e]:
                                        waited[_e][_k] = 0

                MC = min(512, CH)
                NMC = CH // MC
                with contextlib.ExitStack() as ESP:
                    featf = ESP.enter_context(nc.sbuf_tensor("featf", [128, NKD * CH], BF16))
                    featb = ESP.enter_context(nc.sbuf_tensor("featb", [128, NKD * CH], BF16))
                    wstat = ESP.enter_context(nc.sbuf_tensor("wstat", [128, NJ * 128], BF16))
                    xfm = ESP.enter_context(nc.sbuf_tensor("xfm", [128, NKD * MC], BF16))
                    with contextlib.ExitStack() as ESG:
                        scf = ESG.enter_context(nc.sbuf_tensor("scf", [16, CH], F32))
                        sct = ESG.enter_context(nc.sbuf_tensor("sct", [128, 16], F32))
                        gsc = ESG.enter_context(nc.sbuf_tensor("gsc", [128, 16], F32))
                        pbf = ESG.enter_context(nc.sbuf_tensor("pbf", [16, CH], BF16))
                        for mi in range(NMC):
                            SY(lambda sy, mi=mi: sy.dma_start(
                                out=xfm[:].rearrange("p (k n) -> p k n", k=NKD),
                                in_=xnfm_h[:, mi * MC:(mi + 1) * MC].rearrange("(k p) n -> p k n", p=128)))

                            def _msc(t, mi=mi):
                                last = None
                                for kb in range(NKD):
                                    last = t.matmul(pA[0:E, 0:MC], wg_sb[:, kb * E:(kb + 1) * E],
                                                    xfm[:, kb * MC:(kb + 1) * MC],
                                                    start=(kb == 0), stop=(kb == NKD - 1))
                                return last
                            PE(_msc)
                            ACTG(lambda a_, mi=mi: a_.activation(scf[0:E, mi * MC:(mi + 1) * MC], pA[0:E, 0:MC], AF.Copy))

                        for a in range(NT):
                            PE(lambda t, a=a: t.transpose(pD[:, 0:8], scf[0:E, a * 128:(a + 1) * 128], ident_f[0:8, 0:8]))
                            ACTG(lambda a_: a_.activation(sct[:, 0:8], pD[:, 0:8], AF.Copy))

                            VEC(lambda v: v.reduce_max(sct[:, 8:9], sct[:, 0:8], AX.X, negate=True))
                            ACTG(lambda a_: a_.activation(sct[:, 13:14], sct[:, 8:9], AF.Copy))

                            def _g8(v, C):
                                s8 = sct[:, 0:8]
                                C(v.tensor_scalar(sct[:, 9:10], sct[:, 13:14], -1.0, None, ALU.mult))
                                C(v.tensor_scalar(gsc[:, 0:8], s8, sct[:, 9:10], None, ALU.is_ge))
                                C(v.scalar_tensor_tensor(gsc[:, 8:16], gsc[:, 0:8], -1e30, s8, ALU.mult, ALU.add))
                                return v.reduce_max(sct[:, 10:11], gsc[:, 8:16], AX.X)
                            VEC(_g8)

                            def _gl(a_):
                                a_.activation(sct[:, 14:15], sct[:, 10:11], AF.Copy)
                                return a_.activation(gsc[:, 8:16], sct[:, 0:8], AF.Exp, bias=sct[:, 13:14])
                            ACTG(_gl)

                            def _g9(v, C):
                                C(v.tensor_scalar(gsc[:, 0:8], sct[:, 0:8], sct[:, 14:15], None, ALU.is_ge))
                                C(v.tensor_tensor(gsc[:, 8:16], gsc[:, 8:16], gsc[:, 0:8], ALU.mult))
                                return v.reduce_sum(sct[:, 11:12], gsc[:, 8:16], AX.X)
                            VEC(_g9)
                            ACTG(lambda a_: a_.activation(sct[:, 15:16], sct[:, 11:12], AF.Copy))
                            VEC(lambda v: v.reciprocal(sct[:, 12:13], sct[:, 15:16]))
                            ACTG(lambda a_: a_.activation(sct[:, 9:10], sct[:, 12:13], AF.Copy))
                            VEC(lambda v: v.tensor_scalar(gsc[:, 0:8], gsc[:, 8:16], sct[:, 9:10], None, ALU.mult))
                            PE(lambda t, a=a: t.transpose(pD[0:8, 128:256], gsc[:, 0:8], ident_f[:]))
                            ACTG(lambda a_, a=a: a_.activation(pbf[0:E, a * 128:(a + 1) * 128], pD[0:8, 128:256], AF.Copy))
                        GP(lambda g: g.dma_start(out=p_h[:], in_=pbf[0:E, :]))

                    with nc.sbuf_tensor("yfm", [128, NJ * MC], BF16) as yfm:
                        for d in range(2):
                            feat = featf if d == 0 else featb
                            for mi in range(NMC):
                                SY(lambda sy, d=d, mi=mi: sy.dma_start(
                                    out=yfm[:].rearrange("p (k n) -> p k n", k=NJ),
                                    in_=yfin_h[d][:, mi * MC:(mi + 1) * MC].rearrange("(k p) n -> p k n", p=128)))
                                for n in range(NKD):
                                    SY(lambda sy, d=d, n=n: sy.dma_start(
                                        out=wstat[:].rearrange("p (k n) -> p k n", k=NJ),
                                        in_=wout_h[d][:, n * 128:(n + 1) * 128].rearrange("(k p) n -> p k n", p=128)))

                                    def _mf(t, n=n):
                                        last = None
                                        for k in range(NJ):
                                            last = t.matmul(pA[:, 0:MC], wstat[:, k * 128:(k + 1) * 128],
                                                            yfm[:, k * MC:(k + 1) * MC],
                                                            start=(k == 0), stop=(k == NJ - 1))
                                        return last
                                    PE(_mf)
                                    ACTG(lambda a_, n=n, mi=mi, feat=feat: a_.activation(
                                        feat[:, n * CH + mi * MC:n * CH + (mi + 1) * MC], pA[:, 0:MC], AF.Copy))

                    with contextlib.ExitStack() as ESF:
                        prep = ESF.enter_context(nc.sbuf_tensor("prep", [128, E * MC], BF16))
                        hidden = ESF.enter_context(nc.sbuf_tensor("hidden", [128, NJ * MC], BF16))
                        hsil = ESF.enter_context(nc.sbuf_tensor("hsil", [128, MC], F32))
                        oacc = ESF.enter_context(nc.sbuf_tensor("oacc", [128, NKD * MC], F32))
                        otok = ESF.enter_context(nc.sbuf_tensor("otok", [128, DM], F32))
                        xrl = ESF.enter_context(nc.sbuf_tensor("xrl", [128, DM], F32))
                        xnrl = ESF.enter_context(nc.sbuf_tensor("xnrl", [128, DM], F32))
                        for mi in range(NMC):
                            SY(lambda sy, mi=mi: sy.dma_start(
                                out=xfm[:].rearrange("p (k n) -> p k n", k=NKD),
                                in_=xnfm_h[:, mi * MC:(mi + 1) * MC].rearrange("(k p) n -> p k n", p=128)))

                            def _ldp(sy, mi=mi):
                                for e in range(E):
                                    b_("ld", sy.dma_start(out=prep[:, e * MC:(e + 1) * MC],
                                                          in_=p_h[e, mi * MC:(mi + 1) * MC].partition_broadcast(128)))
                            SY(_ldp)

                            for nn in range(NKD):
                                SY(lambda sy, nn=nn: sy.dma_start(
                                    out=wstat[:].rearrange("p (k n) -> p k n", k=NJ),
                                    in_=wfus_h[:, nn * 128:(nn + 1) * 128].rearrange("(k p) n -> p k n", p=128)))

                                def _mfus(t, nn=nn, mi=mi):
                                    for k in range(NKD):
                                        t.matmul(pB[:, 0:MC], wstat[:, k * 128:(k + 1) * 128],
                                                 featf[:, k * CH + mi * MC:k * CH + (mi + 1) * MC],
                                                 start=(k == 0), stop=False)
                                    last = None
                                    for k in range(NKD):
                                        last = t.matmul(pB[:, 0:MC], wstat[:, (NKD + k) * 128:(NKD + k + 1) * 128],
                                                        featb[:, k * CH + mi * MC:k * CH + (mi + 1) * MC],
                                                        start=False, stop=(k == NKD - 1))
                                    return last
                                PE(_mfus)
                                VEC(lambda v, nn=nn: v.tensor_scalar(
                                    oacc[:, nn * MC:(nn + 1) * MC], pB[:, 0:MC], cFB(nn), None, ALU.add))

                            for e in range(E):
                                for nh in range(NJ):
                                    def _lw12(sy, e=e, nh=nh):
                                        b_("ld", sy.dma_start(
                                            out=wstat[:, 0:NKD * 128].rearrange("p (k n) -> p k n", k=NKD),
                                            in_=w12_h[e][:, nh * 128:(nh + 1) * 128].rearrange("(k p) n -> p k n", p=128)))
                                        b_("ld", sy.dma_start(
                                            out=wstat[:, NKD * 128:2 * NKD * 128].rearrange("p (k n) -> p k n", k=NKD),
                                            in_=w12_h[e][:, HF + nh * 128:HF + (nh + 1) * 128].rearrange("(k p) n -> p k n", p=128)))
                                    SY(_lw12)

                                    def _mh(t, mi=mi):
                                        for kb in range(NKD):
                                            t.matmul(pA[:, 0:MC], wstat[:, kb * 128:(kb + 1) * 128],
                                                     xfm[:, kb * MC:(kb + 1) * MC],
                                                     start=(kb == 0), stop=(kb == NKD - 1))
                                        last = None
                                        for kb in range(NKD):
                                            last = t.matmul(pB[:, 0:MC], wstat[:, (NKD + kb) * 128:(NKD + kb + 1) * 128],
                                                            xfm[:, kb * MC:(kb + 1) * MC],
                                                            start=(kb == 0), stop=(kb == NKD - 1))
                                        return last
                                    PE(_mh)
                                    ACTG(lambda a_: a_.activation(hsil[:], pA[:, 0:MC], AF.Silu))

                                    def _hd(v, C, nh=nh, e=e):
                                        C(v.tensor_tensor(hidden[:, nh * MC:(nh + 1) * MC], hsil[:], pB[:, 0:MC], ALU.mult))
                                        return v.tensor_tensor(hidden[:, nh * MC:(nh + 1) * MC],
                                                               hidden[:, nh * MC:(nh + 1) * MC],
                                                               prep[:, e * MC:(e + 1) * MC], ALU.mult)
                                    VEC(_hd)

                                for nn in range(NKD):
                                    SY(lambda sy, e=e, nn=nn: sy.dma_start(
                                        out=wstat[:, 0:NJ * 128].rearrange("p (k n) -> p k n", k=NJ),
                                        in_=w3_h[e][:, nn * 128:(nn + 1) * 128].rearrange("(k p) n -> p k n", p=128)))

                                    def _mw3(t, nn=nn):
                                        last = None
                                        for k in range(NJ):
                                            last = t.matmul(pC[:, 0:MC], wstat[:, k * 128:(k + 1) * 128],
                                                            hidden[:, k * MC:(k + 1) * MC],
                                                            start=(k == 0), stop=(k == NJ - 1))
                                        return last
                                    PE(_mw3)
                                    VEC(lambda v, nn=nn: v.tensor_tensor(
                                        oacc[:, nn * MC:(nn + 1) * MC], oacc[:, nn * MC:(nn + 1) * MC],
                                        pC[:, 0:MC], ALU.add))

                            for tt in range(MC // 128):
                                def _mtr(t, tt=tt):
                                    last = None
                                    for nn in range(NKD):
                                        last = t.transpose(pA[:, nn * 128:(nn + 1) * 128],
                                                           oacc[:, nn * MC + tt * 128:nn * MC + (tt + 1) * 128],
                                                           ident_f[:])
                                    return last
                                PE(_mtr)
                                r0 = mi * MC + tt * 128

                                def _ldr(sy, r0=r0):
                                    b_("ld", sy.dma_start(out=xrl[:], in_=xf[T - CH + r0:T - CH + r0 + 128, :]))
                                    b_("ld", sy.dma_start(out=xnrl[:], in_=xn_h[r0:r0 + 128, :]))
                                SY(_ldr)

                                def _fin(v, C):
                                    C(v.tensor_tensor(otok[:], pA[:, 0:DM], xrl[:], ALU.add))
                                    return v.tensor_tensor(otok[:], otok[:], xnrl[:], ALU.add)
                                VEC(_fin)
                                GP(lambda g, r0=r0: g.dma_start(out=y_out[r0:r0 + 128, :], in_=otok[:]))

        @block.sync
        def _(sy):
            sy.wait_ge(sST, ctr["st"])

    return nc


_BUILD_CACHE = {}


def _get_nc(T, CH):
    key = (T, CH)
    if key not in _BUILD_CACHE:
        _BUILD_CACHE[key] = build(T, CH)
    return _BUILD_CACHE[key]


def make_inmaps(T, CH, x, weights):
    NCH = T // CH
    OWNT = T // 8
    f32 = np.float32

    def pp(vec, nblk):
        return np.ascontiguousarray(vec.reshape(nblk, 128).T.astype(f32))

    wd = {}
    wd["w_xi"] = np.ascontiguousarray(np.stack([
        weights["f_in_w"][:DI].T, weights["b_in_w"][:DI].T]).astype(f32))
    wd["w_z"] = np.ascontiguousarray(np.stack([
        weights["f_in_w"][DI:].T, weights["b_in_w"][DI:].T]).astype(f32))
    wd["w_xp"] = np.ascontiguousarray(np.stack([
        weights["f_xproj_w"].T, weights["b_xproj_w"].T]).astype(f32))
    wd["w_dt"] = np.ascontiguousarray(np.stack([
        weights["f_dt_w"].T, weights["b_dt_w"].T]).astype(f32))
    wd["dtb"] = np.ascontiguousarray(np.stack([
        pp(weights["f_dt_b"], NJ), pp(weights["b_dt_b"], NJ)]))
    cw = []
    for p in ("f", "b"):
        c = weights[p + "_conv_w"][:, 0, :]
        cw.append(np.ascontiguousarray(
            c.reshape(NJ, 128, 4).transpose(1, 0, 2).reshape(128, NJ * 4).astype(f32)))
    wd["convw"] = np.ascontiguousarray(np.stack(cw))
    wd["convb"] = np.ascontiguousarray(np.stack([
        pp(weights["f_conv_b"], NJ), pp(weights["b_conv_b"], NJ)]))
    al = []
    for p in ("f", "b"):
        a = weights[p + "_A_log"]
        al.append(np.ascontiguousarray(
            a.reshape(NJ, 128, S).transpose(1, 0, 2).reshape(128, NJ * S).astype(f32)))
    wd["alog"] = np.ascontiguousarray(np.stack(al))
    wd["dvec"] = np.ascontiguousarray(np.stack([
        pp(weights["f_D"], NJ), pp(weights["b_D"], NJ)]))
    wd["w_out"] = np.ascontiguousarray(np.stack([
        weights["f_out_w"].T, weights["b_out_w"].T]).astype(f32))
    wd["w_fus"] = np.ascontiguousarray(weights["fus_w"].T.astype(f32))
    wd["fusb"] = pp(weights["fus_b"], NKD)
    wd["lng"] = pp(weights["ln_g"], NKD)
    wd["lnb"] = pp(weights["ln_b"], NKD)
    wd["w12"] = np.ascontiguousarray(weights["ffn_w12"].transpose(0, 2, 1).astype(f32))
    wd["w3"] = np.ascontiguousarray(weights["ffn_w3"].transpose(0, 2, 1).astype(f32))
    wd["wgate"] = np.ascontiguousarray(weights["ffn_gate_w"].T.astype(f32))
    wd["ident"] = np.eye(128, dtype=f32)

    xr = x[::-1]
    in_maps = []
    for k in range(8):
        L = OWNT * (k + 1)
        padf = T - L
        padb = OWNT * k
        xfk = np.concatenate([np.zeros((padf, DM), f32), x[:L]], axis=0)
        xbk = np.concatenate([np.zeros((padb, DM), f32), xr[:T - padb]], axis=0)
        penk = np.zeros((2, T), f32)
        penk[0, :padf] = PEN
        penk[1, :padb] = PEN
        penk = penk.astype(ml_dtypes.bfloat16)
        gk = np.ones((128, 2 * NCH), f32)
        npadf = padf // CH
        npadb = padb // CH
        gk[:, 0:npadf + 1] = 0.0
        gk[:, NCH:NCH + npadb + 1] = 0.0
        m = dict(xf=np.ascontiguousarray(xfk), xb=np.ascontiguousarray(xbk),
                 pen=penk, gates=gk,
                 cfg=np.array([[npadf, npadb, 0, 0]], dtype=np.int32))
        m.update(wd)
        in_maps.append(m)
    return in_maps


def kernel(L0_attn_x, L0_attn_ln_g, L0_attn_ln_b, L0_attn_fus_w, L0_attn_fus_b,
           L0_attn_f_in_w, L0_attn_f_conv_w, L0_attn_f_conv_b, L0_attn_f_xproj_w,
           L0_attn_f_dt_w, L0_attn_f_dt_b, L0_attn_f_A_log, L0_attn_f_D, L0_attn_f_out_w,
           L0_attn_b_in_w, L0_attn_b_conv_w, L0_attn_b_conv_b, L0_attn_b_xproj_w,
           L0_attn_b_dt_w, L0_attn_b_dt_b, L0_attn_b_A_log, L0_attn_b_D, L0_attn_b_out_w,
           L0_ffn_w12, L0_ffn_w3, L0_ffn_gate_w):
    T, CH = 16384, 2048
    x = np.ascontiguousarray(np.asarray(L0_attn_x, dtype=np.float32)[0])
    weights = dict(
        ln_g=np.asarray(L0_attn_ln_g), ln_b=np.asarray(L0_attn_ln_b),
        fus_w=np.asarray(L0_attn_fus_w), fus_b=np.asarray(L0_attn_fus_b),
        f_in_w=np.asarray(L0_attn_f_in_w), f_conv_w=np.asarray(L0_attn_f_conv_w),
        f_conv_b=np.asarray(L0_attn_f_conv_b), f_xproj_w=np.asarray(L0_attn_f_xproj_w),
        f_dt_w=np.asarray(L0_attn_f_dt_w), f_dt_b=np.asarray(L0_attn_f_dt_b),
        f_A_log=np.asarray(L0_attn_f_A_log), f_D=np.asarray(L0_attn_f_D),
        f_out_w=np.asarray(L0_attn_f_out_w),
        b_in_w=np.asarray(L0_attn_b_in_w), b_conv_w=np.asarray(L0_attn_b_conv_w),
        b_conv_b=np.asarray(L0_attn_b_conv_b), b_xproj_w=np.asarray(L0_attn_b_xproj_w),
        b_dt_w=np.asarray(L0_attn_b_dt_w), b_dt_b=np.asarray(L0_attn_b_dt_b),
        b_A_log=np.asarray(L0_attn_b_A_log), b_D=np.asarray(L0_attn_b_D),
        b_out_w=np.asarray(L0_attn_b_out_w),
        ffn_w12=np.asarray(L0_ffn_w12), ffn_w3=np.asarray(L0_ffn_w3),
        ffn_gate_w=np.asarray(L0_ffn_gate_w),
    )
    nc = _get_nc(T, CH)
    in_maps = make_inmaps(T, CH, x, weights)
    res = run_bass_kernel_spmd(nc, in_maps, core_ids=list(range(8)))
    out = np.concatenate([res.results[k]["y_out"] for k in range(8)], axis=0)
    return out.reshape(1, T, DM).astype(np.float32)


# version 09
# speedup vs baseline: 2.3267x; 1.1595x over previous
import contextlib
import ml_dtypes
import numpy as np
import concourse.bass as bass
import concourse.mybir as mybir
from concourse.bass_utils import run_bass_kernel_spmd

F32 = mybir.dt.float32
BF16 = mybir.dt.bfloat16
AF = mybir.ActivationFunctionType
ALU = mybir.AluOpType
AX = mybir.AxisListType

DM, DI, S, RANK, E, HF = 1024, 2048, 16, 64, 8, 2048
NJ, NKD = DI // 128, DM // 128
EPS = 1e-5
PEN = -30000.0


def build(T, CH, dbg=False):
    NCH = T // CH
    NT = CH // 128
    QT = max(1, NT // 8)
    NQ = NT // QT
    SC = CH // 2
    MM = min(512, CH)
    NMM = CH // MM
    OWN = NCH - 1
    nc = bass.Bass("TRN2", target_bir_lowering=False)
    dram = nc.dram_tensor

    xf = dram("xf", [T, DM], F32, kind="ExternalInput").ap()
    xb = dram("xb", [T, DM], F32, kind="ExternalInput").ap()
    pen = dram("pen", [2, T], BF16, kind="ExternalInput").ap()
    gates = dram("gates", [128, 2 * NCH], F32, kind="ExternalInput").ap()
    w_xi = dram("w_xi", [2, DM, DI], F32, kind="ExternalInput").ap()
    w_z = dram("w_z", [2, DM, DI], F32, kind="ExternalInput").ap()
    w_xp = dram("w_xp", [2, DI, 96], F32, kind="ExternalInput").ap()
    w_dt = dram("w_dt", [2, RANK, DI], F32, kind="ExternalInput").ap()
    dtb = dram("dtb", [2, 128, NJ], F32, kind="ExternalInput").ap()
    convw = dram("convw", [2, 128, NJ * 4], F32, kind="ExternalInput").ap()
    convb = dram("convb", [2, 128, NJ], F32, kind="ExternalInput").ap()
    alog = dram("alog", [2, 128, NJ * S], F32, kind="ExternalInput").ap()
    dvec = dram("dvec", [2, 128, NJ], F32, kind="ExternalInput").ap()
    w_out = dram("w_out", [2, DI, DM], F32, kind="ExternalInput").ap()
    w_fus = dram("w_fus", [2 * DM, DM], F32, kind="ExternalInput").ap()
    fusb = dram("fusb", [128, NKD], F32, kind="ExternalInput").ap()
    lng = dram("lng", [128, NKD], F32, kind="ExternalInput").ap()
    lnb = dram("lnb", [128, NKD], F32, kind="ExternalInput").ap()
    w12 = dram("w12", [E, DM, 2 * HF], F32, kind="ExternalInput").ap()
    w3 = dram("w3", [E, HF, DM], F32, kind="ExternalInput").ap()
    wgate = dram("wgate", [DM, E], F32, kind="ExternalInput").ap()
    ident_in = dram("ident", [128, 128], F32, kind="ExternalInput").ap()
    cfg = dram("cfg", [1, 4], mybir.dt.int32, kind="ExternalInput").ap()

    y_out = dram("y_out", [CH, DM], F32, kind="ExternalOutput").ap()

    IK = "ExternalOutput" if dbg else "Internal"
    wxi_h = dram("wxi_h", [2, DM, DI], BF16, kind="Internal").ap()
    wz_h = dram("wz_h", [2, DM, DI], BF16, kind="Internal").ap()
    wout_h = dram("wout_h", [2, DI, DM], BF16, kind="Internal").ap()
    wfus_h = dram("wfus_h", [2 * DM, DM], BF16, kind="Internal").ap()
    w12_h = dram("w12_h", [E, DM, 2 * HF], BF16, kind="Internal").ap()
    w3_h = dram("w3_h", [E, HF, DM], BF16, kind="Internal").ap()
    bc_h = dram("bc_h", [32, CH], BF16, kind=IK).ap()
    xc_h = dram("xc_h", [DI, CH], BF16, kind=IK).ap()
    xn_h = dram("xn_h", [CH, DM], F32, kind=IK).ap()
    xnfm_h = dram("xnfm_h", [DM, CH], BF16, kind=IK).ap()
    yfin_h = dram("yfin_h", [2, DI, CH], BF16, kind=IK).ap()
    p_h = dram("p_h", [E, CH], BF16, kind=IK).ap()
    xid = dram("xid", [DI, CH], BF16, kind=IK).ap() if dbg else None
    xn2d = dram("xn2d", [DM, CH], BF16, kind=IK).ap() if dbg else None

    ctr = dict(ld=0, st=0, pe=0, act=0, vec=0, ad=0)

    with contextlib.ExitStack() as ES:
        sLD = ES.enter_context(nc.semaphore("sLD"))
        sST = ES.enter_context(nc.semaphore("sST"))
        sPE = ES.enter_context(nc.semaphore("sPE"))
        sACT = ES.enter_context(nc.semaphore("sACT"))
        sVEC = ES.enter_context(nc.semaphore("sVEC"))
        sAD = ES.enter_context(nc.semaphore("sAD"))
        block = ES.enter_context(nc.Block())
        SEM = dict(ld=sLD, st=sST, pe=sPE, act=sACT, vec=sVEC, ad=sAD)

        waited = {e: dict(ld=0, st=0, pe=0, act=0, vec=0, ad=0) for e in
                  ("ld", "st", "pe", "act", "vec")}

        def wall(eng, me):
            w_ = waited[me]
            for k, sem in SEM.items():
                if ctr[k] > w_[k]:
                    eng.wait_ge(sem, ctr[k])
                    w_[k] = ctr[k]

        def b_(key, inst):
            n = 16 if key in ("ld", "st", "ad") else 1
            ctr[key] += n
            inst.then_inc(SEM[key], n)
            return inst

        import inspect as _insp

        def _wants_chain(fn):
            try:
                return len(_insp.signature(fn).parameters) >= 2 and \
                    "C" in _insp.signature(fn).parameters
            except (TypeError, ValueError):
                return False

        def _mk_chain(eng, key):
            def C(inst):
                ctr[key] += 1
                inst.then_inc(SEM[key], 1)
                eng.wait_ge(SEM[key], ctr[key])
                return inst
            return C

        def _group(block_deco, key):
            def wrap(fn):
                @block_deco
                def _(eng):
                    wall(eng, key)
                    if _wants_chain(fn):
                        r = fn(eng, _mk_chain(eng, key))
                    else:
                        r = fn(eng)
                    if r is not None:
                        b_(key, r)
            return wrap

        def SY(fn):
            _group(block.sync, "ld")(fn)

        def GP(fn):
            _group(block.gpsimd, "st")(fn)

        def PE(fn):
            _group(block.tensor, "pe")(fn)

        def ACTG(fn):
            _group(block.scalar, "act")(fn)

        def VEC(fn):
            _group(block.vector, "vec")(fn)

        NCON = 2 * NJ * S + 14 * NJ + 3 * NKD + 2 * NCH + 16
        if True:
            ident_b = ES.enter_context(nc.sbuf_tensor("ident_b", [128, 128], BF16))
            ident_f = ES.enter_context(nc.sbuf_tensor("ident_f", [128, 128], F32))
            consts = ES.enter_context(nc.sbuf_tensor("consts", [128, NCON], F32))
            wdt_sb = ES.enter_context(nc.sbuf_tensor("wdt_sb", [65, 2 * DI], BF16))
            wxp_sb = ES.enter_context(nc.sbuf_tensor("wxp_sb", [128, 2 * NJ * 96], BF16))
            wg_sb = ES.enter_context(nc.sbuf_tensor("wg_sb", [128, NKD * E], BF16))
            oA = 0
            oDTB = oA + 2 * NJ * S
            oCW = oDTB + 2 * NJ
            oCB = oCW + 2 * NJ * 4
            oD = oCB + 2 * NJ
            oFB = oD + 2 * NJ
            oLG = oFB + NKD
            oLB = oLG + NKD
            oGT = oLB + NKD

            def CC(i):
                return consts[:, i:i + 1]

            cA = lambda d, j, s: CC(oA + d * NJ * S + j * S + s)
            cDTB = lambda d, j: CC(oDTB + d * NJ + j)
            cCW = lambda d, j, tap: CC(oCW + d * NJ * 4 + j * 4 + tap)
            cCB = lambda d, j: CC(oCB + d * NJ + j)
            cD = lambda d, j: CC(oD + d * NJ + j)
            cFB = lambda n: CC(oFB + n)
            cLG = lambda n: CC(oLG + n)
            cLB = lambda n: CC(oLB + n)
            cGT = lambda d, c: CC(oGT + d * NCH + c)

            SY(lambda sy: sy.dma_start(out=ident_f[:], in_=ident_in))
            SY(lambda sy: sy.dma_start(out=consts[:, oA:oA + 2 * NJ * S].rearrange("p (d j) -> p d j", d=2), in_=alog.rearrange("d p j -> p d j")))
            SY(lambda sy: sy.dma_start(out=consts[:, oDTB:oDTB + 2 * NJ].rearrange("p (d j) -> p d j", d=2), in_=dtb.rearrange("d p j -> p d j")))
            SY(lambda sy: sy.dma_start(out=consts[:, oCW:oCW + 8 * NJ].rearrange("p (d j) -> p d j", d=2), in_=convw.rearrange("d p j -> p d j")))
            SY(lambda sy: sy.dma_start(out=consts[:, oCB:oCB + 2 * NJ].rearrange("p (d j) -> p d j", d=2), in_=convb.rearrange("d p j -> p d j")))
            SY(lambda sy: sy.dma_start(out=consts[:, oD:oD + 2 * NJ].rearrange("p (d j) -> p d j", d=2), in_=dvec.rearrange("d p j -> p d j")))
            SY(lambda sy: sy.dma_start(out=consts[:, oFB:oFB + NKD], in_=fusb))
            SY(lambda sy: sy.dma_start(out=consts[:, oLG:oLG + NKD], in_=lng))
            SY(lambda sy: sy.dma_start(out=consts[:, oLB:oLB + NKD], in_=lnb))
            SY(lambda sy: sy.dma_start(out=consts[:, oGT:oGT + 2 * NCH], in_=gates))

            oK = oGT + 2 * NCH

            def _regs(v):
                v.memset(CC(oK + 0), 1.0)
                v.memset(CC(oK + 1), EPS)
                last = v.memset(CC(oK + 2), EPS * EPS)
                nc.const_aps.register(F32, 1.0, CC(oK + 0))
                nc.const_aps.register(F32, EPS, CC(oK + 1))
                nc.const_aps.register(F32, EPS * EPS, CC(oK + 2))
                return last
            VEC(_regs)

            ACTG(lambda a: a.activation(consts[:, oA:oA + 2 * NJ * S], consts[:, oA:oA + 2 * NJ * S], AF.Exp))

            def _c0(v):
                v.tensor_scalar_mul(consts[:, oA:oA + 2 * NJ * S], consts[:, oA:oA + 2 * NJ * S], -1.0)
                return v.tensor_copy(ident_b[:], ident_f[:])
            VEC(_c0)

            cfg_sb = ES.enter_context(nc.sbuf_tensor("cfg_sb", [1, 512], mybir.dt.int32))
            scr_sb = ES.enter_context(nc.sbuf_tensor("scr_sb", [1, 64], F32))
            dumcol = [8]
            SY(lambda sy: sy.dma_start(out=cfg_sb[0:1, 0:4], in_=cfg))
            skipreg = {}
            _edecos = [("ld", block.sync), ("st", block.gpsimd), ("pe", block.tensor),
                       ("act", block.scalar), ("vec", block.vector)]
            for _k, _deco in _edecos:
                @_deco
                def _(eng, _k=_k):
                    wall(eng, _k)
                    pid = eng.partition_id()
                    r0 = eng.to_reg(NCH - 1 - pid)
                    r1 = eng.to_reg(pid)
                    skipreg[_k] = (r0, r1)

            with contextlib.ExitStack() as ESC:
                cast_f = ESC.enter_context(nc.sbuf_tensor("cast_f", [128, 8192], F32))
                cast_b = ESC.enter_context(nc.sbuf_tensor("cast_b", [128, 8192], BF16))
                jobs = []
                for d in range(2):
                    jobs += [(w_xi[d], wxi_h[d], DM, DI), (w_z[d], wz_h[d], DM, DI),
                             (w_out[d], wout_h[d], DI, DM)]
                jobs.append((w_fus, wfus_h, 2 * DM, DM))
                for e in range(E):
                    jobs += [(w12[e], w12_h[e], DM, 2 * HF), (w3[e], w3_h[e], HF, DM)]
                for src, dst, rows, cols in jobs:
                    csz = min(8192, cols)
                    for r0 in range(0, rows, 128):
                        for c0 in range(0, cols, csz):
                            n = min(csz, cols - c0)
                            SY(lambda sy, src=src, r0=r0, c0=c0, n=n:
                               sy.dma_start(out=cast_f[:, 0:n], in_=src[r0:r0 + 128, c0:c0 + n]))
                            VEC(lambda v, n=n: v.tensor_copy(cast_b[:, 0:n], cast_f[:, 0:n]))
                            GP(lambda g, dst=dst, r0=r0, c0=c0, n=n:
                               g.dma_start(out=dst[r0:r0 + 128, c0:c0 + n], in_=cast_b[:, 0:n]))

                SY(lambda sy: sy.dma_start(out=cast_f[0:RANK, 0:2 * DI].rearrange("p (d n) -> p d n", d=2), in_=w_dt.rearrange("d p n -> p d n")))

                def _sdt(v):
                    v.tensor_copy(wdt_sb[0:RANK, :], cast_f[0:RANK, 0:2 * DI])
                    return v.memset(wdt_sb[RANK:65, :], 1.0)
                VEC(_sdt)

                SY(lambda sy: sy.dma_start(
                    out=cast_f[:, 0:2 * NJ * 96].rearrange("p (d j n) -> p d j n", d=2, j=NJ),
                    in_=w_xp.rearrange("d (j p) n -> p d j n", p=128)))
                VEC(lambda v: v.tensor_copy(wxp_sb[:], cast_f[:, 0:2 * NJ * 96]))
                SY(lambda sy: sy.dma_start(
                    out=cast_f[:, 0:NKD * E].rearrange("p (k e) -> p k e", k=NKD),
                    in_=wgate.rearrange("(k p) e -> p k e", p=128)))
                VEC(lambda v: v.tensor_copy(wg_sb[:], cast_f[:, 0:NKD * E]))

            if True:
                pA = ES.enter_context(nc.psum_tensor("pA", [128, 1024], F32))
                pB = ES.enter_context(nc.psum_tensor("pB", [128, 1024], F32))
                pC = ES.enter_context(nc.psum_tensor("pC", [128, 1024], F32))
                pD = ES.enter_context(nc.psum_tensor("pD", [128, 512], F32))
                pTb = ES.enter_context(nc.psum_tensor("pTb", [128, 1024], BF16))
                with contextlib.ExitStack() as ESS:
                    xtok = ESS.enter_context(nc.sbuf_tensor("xtok", [128, QT * DM], F32))
                    sqv = ESS.enter_context(nc.sbuf_tensor("sqv", [128, QT * DM], F32))
                    lnmv = ESS.enter_context(nc.sbuf_tensor("lnmv", [128, 2 * QT], F32))
                    lnr = ESS.enter_context(nc.sbuf_tensor("lnr", [128, QT], F32))
                    lnmv2 = ESS.enter_context(nc.sbuf_tensor("lnmv2", [128, 2 * QT], F32))
                    xn2t = ESS.enter_context(nc.sbuf_tensor("xn2t", [128, QT * DM], BF16))
                    xn2b = ESS.enter_context(nc.sbuf_tensor("xn2b", [128, NKD * CH], BF16))
                    brep = ESS.enter_context(nc.sbuf_tensor("brep", [128, S * SC], BF16))
                    crep = ESS.enter_context(nc.sbuf_tensor("crep", [128, S * SC], BF16))
                    wrot = ESS.enter_context(nc.sbuf_tensor("wrot", [128, NKD * 128], BF16))
                    wzrot = ESS.enter_context(nc.sbuf_tensor("wzrot", [128, NKD * 128], BF16))
                    xi_j = ESS.enter_context(nc.sbuf_tensor("xi_j", [128, CH + 3], BF16))
                    tails = ESS.enter_context(nc.sbuf_tensor("tails", [128, NJ * 3], BF16))
                    accv = ESS.enter_context(nc.sbuf_tensor("accv", [128, CH], F32))
                    xc_j = ESS.enter_context(nc.sbuf_tensor("xc_j", [128, CH], BF16))
                    xcmv = ESS.enter_context(nc.sbuf_tensor("xcmv", [128, 512], BF16))
                    dtaug = ESS.enter_context(nc.sbuf_tensor("dtaug", [65, CH], BF16))
                    bcsb = ESS.enter_context(nc.sbuf_tensor("bcsb", [32, CH], BF16))
                    pensf = ESS.enter_context(nc.sbuf_tensor("pensf", [1, CH], BF16))
                    dlj = ESS.enter_context(nc.sbuf_tensor("dlj", [128, 2 * SC], F32))
                    duj = ESS.enter_context(nc.sbuf_tensor("duj", [128, SC], BF16))
                    dAj = ESS.enter_context(nc.sbuf_tensor("dAj", [128, 2 * SC], F32))
                    bj = ESS.enter_context(nc.sbuf_tensor("bj", [128, 2 * SC], BF16))
                    tmpj = ESS.enter_context(nc.sbuf_tensor("tmpj", [128, SC], BF16))
                    hj = ESS.enter_context(nc.sbuf_tensor("hj", [128, 2 * SC], BF16))
                    yj = ESS.enter_context(nc.sbuf_tensor("yj", [128, SC], F32))
                    gj = ESS.enter_context(nc.sbuf_tensor("gj", [128, SC], BF16))
                    yfj = ESS.enter_context(nc.sbuf_tensor("yfj", [128, SC], BF16))
                    states = ESS.enter_context(nc.sbuf_tensor("states", [128, NJ * S], F32))
                    for d in range(2):
                        xsrc = xf if d == 0 else xb

                        def _rst(v):
                            v.memset(states[:], 0.0)
                            return v.memset(tails[:], 0.0)
                        VEC(_rst)

                        for c in range(NCH):
                            own = (c == OWN)
                            t0 = c * CH

                            skippable = (c < OWN)
                            guards = {}
                            snap = dict(ctr)
                            if skippable:
                                for _k, _deco in _edecos:
                                    @_deco
                                    def _(eng, _k=_k, c=c, d=d):
                                        g = eng.If_lt(skipreg[_k][d], c + 1)
                                        g.__enter__()
                                        guards[_k] = g

                            for q in range(NQ):
                                tq0 = t0 + q * QT * 128
                                SY(lambda sy, tq0=tq0, xsrc=xsrc: sy.dma_start(
                                    out=xtok[:].rearrange("p (a m) -> p a m", a=QT),
                                    in_=xsrc[tq0:tq0 + QT * 128, :].rearrange("(a p) m -> p a m", p=128)))

                                def _ln1(v):
                                    last = None
                                    for a in range(QT):
                                        xt = xtok[:, a * DM:(a + 1) * DM]
                                        last = v.tensor_tensor(sqv[:, a * DM:(a + 1) * DM], xt, xt, ALU.mult)
                                    return last
                                VEC(_ln1)

                                def _ln1r(v):
                                    last = None
                                    for a in range(QT):
                                        v.reduce_sum(lnmv[:, a:a + 1], xtok[:, a * DM:(a + 1) * DM], AX.X)
                                        last = v.reduce_sum(lnmv[:, QT + a:QT + a + 1],
                                                            sqv[:, a * DM:(a + 1) * DM], AX.X)
                                    return last
                                VEC(_ln1r)

                                def _lnl(a_):
                                    a_.activation(lnmv2[:, 0:QT], lnmv[:, 0:QT], AF.Copy, scale=1.0 / DM)
                                    a_.activation(lnmv2[:, QT:2 * QT], lnmv[:, QT:2 * QT], AF.Copy, scale=1.0 / DM)
                                    return a_.activation(lnr[:, 0:QT], lnmv[:, 0:QT], AF.Square, scale=1.0 / DM)
                                ACTG(_lnl)

                                VEC(lambda v: v.tensor_tensor(lnmv2[:, QT:2 * QT], lnmv2[:, QT:2 * QT],
                                                              lnr[:, 0:QT], ALU.subtract))

                                def _lnr(a_, C):
                                    C(a_.activation(lnr[:, 0:QT], lnmv2[:, QT:2 * QT], AF.Ln,
                                                    scale=1.0 + EPS, bias=EPS * EPS))
                                    return a_.activation(lnr[:, 0:QT], lnr[:, 0:QT], AF.Exp, scale=-0.5)
                                ACTG(_lnr)

                                def _ap2(v):
                                    last = None
                                    for a in range(QT):
                                        last = v.tensor_scalar(
                                            xn2t[:, a * DM:(a + 1) * DM], xtok[:, a * DM:(a + 1) * DM],
                                            lnmv2[:, a:a + 1], lnr[:, a:a + 1],
                                            ALU.subtract, ALU.mult)
                                    return last
                                VEC(_ap2)

                                for kb in range(NKD):
                                    for a4 in range(0, QT, 4):
                                        na = min(4, QT - a4)

                                        def _tr(t, kb=kb, a4=a4, na=na):
                                            last = None
                                            for i in range(na):
                                                a = a4 + i
                                                last = t.transpose(
                                                    pTb[:, i * 128:(i + 1) * 128],
                                                    xn2t[:, a * DM + kb * 128:a * DM + (kb + 1) * 128],
                                                    ident_b[:])
                                            return last
                                        PE(_tr)

                                        def _ev(v, kb=kb, q=q, a4=a4, na=na):
                                            col = q * QT * 128 + a4 * 128
                                            return v.tensor_scalar(
                                                xn2b[:, kb * CH + col:kb * CH + col + na * 128],
                                                pTb[:, 0:na * 128], cLG(kb), cLB(kb),
                                                ALU.mult, ALU.add)
                                        VEC(_ev)

                                if own and d == 0:
                                    def _lnr1(a_, C):
                                        C(a_.activation(lnr[:, 0:QT], lnmv2[:, QT:2 * QT], AF.Ln, bias=EPS))
                                        return a_.activation(lnr[:, 0:QT], lnr[:, 0:QT], AF.Exp, scale=-0.5)
                                    ACTG(_lnr1)

                                    def _apx(v):
                                        last = None
                                        for a in range(QT):
                                            last = v.tensor_scalar(
                                                xtok[:, a * DM:(a + 1) * DM], xtok[:, a * DM:(a + 1) * DM],
                                                lnmv2[:, a:a + 1], lnr[:, a:a + 1],
                                                ALU.subtract, ALU.mult)
                                        return last
                                    VEC(_apx)

                                    GP(lambda g, q=q: g.dma_start(
                                        out=xn_h[q * QT * 128:(q + 1) * QT * 128, :].rearrange("(a p) m -> p a m", p=128),
                                        in_=xtok[:].rearrange("p (a m) -> p a m", a=QT)))
                                    VEC(lambda v: v.tensor_copy(xn2t[:], xtok[:]))
                                    for kb in range(NKD):
                                        for a4 in range(0, QT, 4):
                                            na = min(4, QT - a4)

                                            def _trx(t, kb=kb, a4=a4, na=na):
                                                last = None
                                                for i in range(na):
                                                    a = a4 + i
                                                    last = t.transpose(
                                                        pTb[:, i * 128:(i + 1) * 128],
                                                        xn2t[:, a * DM + kb * 128:a * DM + (kb + 1) * 128],
                                                        ident_b[:])
                                                return last
                                            PE(_trx)
                                            ACTG(lambda a_, na=na: a_.activation(
                                                xc_j[:, 0:na * 128], pTb[:, 0:na * 128], AF.Copy))
                                            GP(lambda g, kb=kb, q=q, a4=a4, na=na: g.dma_start(
                                                out=xnfm_h[kb * 128:(kb + 1) * 128,
                                                           q * QT * 128 + a4 * 128:q * QT * 128 + (a4 + na) * 128],
                                                in_=xc_j[:, 0:na * 128]))

                            if dbg and own:
                                GP(lambda g: g.dma_start(
                                    out=xn2d.rearrange("(k p) t -> p k t", p=128),
                                    in_=xn2b[:].rearrange("p (k t) -> p k t", k=NKD)))

                            for j in range(NJ):
                                SY(lambda sy, j=j, d=d: sy.dma_start(
                                    out=wrot[:].rearrange("p (k n) -> p k n", k=NKD),
                                    in_=wxi_h[d][:, j * 128:(j + 1) * 128].rearrange("(k p) n -> p k n", p=128)))

                                for mi in range(NMM):
                                    def _mmx(t, mi=mi):
                                        last = None
                                        for kb in range(NKD):
                                            last = t.matmul(
                                                pA[:, 0:MM], wrot[:, kb * 128:(kb + 1) * 128],
                                                xn2b[:, kb * CH + mi * MM:kb * CH + (mi + 1) * MM],
                                                start=(kb == 0), stop=(kb == NKD - 1))
                                        return last
                                    PE(_mmx)
                                    ACTG(lambda a_, mi=mi: a_.activation(
                                        xi_j[:, 3 + mi * MM:3 + (mi + 1) * MM], pA[:, 0:MM], AF.Copy))

                                def _conv(v, C, j=j, c=c, d=d):
                                    C(v.tensor_scalar(xi_j[:, 0:3], tails[:, j * 3:(j + 1) * 3], cGT(d, c), None, ALU.mult))
                                    C(v.tensor_scalar(accv[:], xi_j[:, 0:CH], cCW(d, j, 0), None, ALU.mult))
                                    for tap in range(1, 4):
                                        C(v.scalar_tensor_tensor(accv[:], xi_j[:, tap:tap + CH], cCW(d, j, tap),
                                                                 accv[:], ALU.mult, ALU.add))
                                    return v.tensor_copy(tails[:, j * 3:(j + 1) * 3], xi_j[:, CH:CH + 3])
                                VEC(_conv)
                                if dbg and own:
                                    GP(lambda g, j=j: g.dma_start(
                                        out=xid[j * 128:(j + 1) * 128, :], in_=xi_j[:, 3:CH + 3]))
                                ACTG(lambda a_, j=j, d=d: a_.activation(xc_j[:], accv[:], AF.Silu, bias=cCB(d, j)))
                                GP(lambda g, j=j: g.dma_start(out=xc_h[j * 128:(j + 1) * 128, :], in_=xc_j[:]))

                            SY(lambda sy, t0=t0, d=d: sy.dma_start(out=pensf[:], in_=pen[d:d + 1, t0:t0 + CH]))
                            ACTG(lambda a_: a_.activation(dtaug[64:65, :], pensf[:], AF.Copy))
                            M5 = min(512, CH)
                            for mi in range(CH // M5):
                                for kb in range(NJ):
                                    SY(lambda sy, mi=mi, kb=kb: sy.dma_start(
                                        out=xcmv[:, 0:M5], in_=xc_h[kb * 128:(kb + 1) * 128, mi * M5:(mi + 1) * M5]))

                                    PE(lambda t, mi=mi, kb=kb, d=d: t.matmul(
                                        pD[0:96, 0:M5], wxp_sb[:, (d * NJ + kb) * 96:(d * NJ + kb + 1) * 96],
                                        xcmv[:, 0:M5], start=(kb == 0), stop=(kb == NJ - 1)))

                                def _evp(a_, mi=mi):
                                    a_.activation(dtaug[0:64, mi * M5:(mi + 1) * M5], pD[0:64, 0:M5], AF.Copy)
                                    return a_.activation(bcsb[:, mi * M5:(mi + 1) * M5], pD[64:96, 0:M5], AF.Copy)
                                ACTG(_evp)

                            GP(lambda g: g.dma_start(out=bc_h[:], in_=bcsb[:]))

                            for sc in range(2):
                                s0 = sc * SC

                                def _ldbr(sy, s0=s0, own=own):
                                    for s in range(S):
                                        b_("ld", sy.dma_start(
                                            out=brep[:, s * SC:(s + 1) * SC],
                                            in_=bc_h[s, s0:s0 + SC].partition_broadcast(128)))
                                    if own:
                                        for s in range(S):
                                            b_("ld", sy.dma_start(
                                                out=crep[:, s * SC:(s + 1) * SC],
                                                in_=bc_h[16 + s, s0:s0 + SC].partition_broadcast(128)))
                                SY(_ldbr)

                                for j in range(NJ):
                                    dl = dlj[:, (j % 2) * SC:(j % 2 + 1) * SC]
                                    SY(lambda sy, j=j, s0=s0: sy.dma_start(
                                        out=xc_j[:, 0:SC], in_=xc_h[j * 128:(j + 1) * 128, s0:s0 + SC]))

                                    def _mdt(t, j=j, s0=s0, d=d):
                                        last = None
                                        ms = min(MM, SC)
                                        for h2 in range(0, SC, ms):
                                            last = t.matmul(
                                                pB[:, h2:h2 + ms],
                                                wdt_sb[:, d * DI + j * 128:d * DI + (j + 1) * 128],
                                                dtaug[:, s0 + h2:s0 + h2 + ms], start=True, stop=True)
                                        return last
                                    PE(_mdt)

                                    def _sp(a_, C, j=j, d=d, dl=dl):
                                        C(a_.activation(dl, pB[:, 0:SC], AF.Exp, bias=cDTB(d, j)))
                                        return a_.activation(dl, dl, AF.Ln, bias=1.0)
                                    ACTG(_sp)

                                    VEC(lambda v, dl=dl: v.tensor_tensor(duj[:], dl, xc_j[:, 0:SC], ALU.mult))

                                    act_hist = {}
                                    bcnt = {}
                                    scnt = {}
                                    sp_cnt = ctr["act"]

                                    VEC(lambda v: v.tensor_tensor(bj[:, 0:SC], duj[:], brep[:, 0:SC], ALU.mult))
                                    bcnt[0] = ctr["vec"]

                                    for s in range(S):
                                        @block.scalar
                                        def _(a_, j=j, s=s, d=d, dl=dl):
                                            if s == 0:
                                                a_.wait_ge(sACT, sp_cnt)
                                            if s >= 2:
                                                a_.wait_ge(sVEC, scnt[s - 2])
                                            inst = a_.activation(dAj[:, (s % 2) * SC:(s % 2 + 1) * SC],
                                                                 dl, AF.Exp, scale=cA(d, j, s))
                                            ctr["act"] += 1
                                            inst.then_inc(sACT, 1)
                                            act_hist[s] = ctr["act"]

                                        @block.vector
                                        def _(v, j=j, s=s, own=own):
                                            v.wait_ge(sACT, act_hist[s])
                                            v.wait_ge(sVEC, bcnt[s])
                                            hjs = hj[:, (s % 2) * SC:(s % 2 + 1) * SC]
                                            i1 = v.tensor_tensor_scan(
                                                hjs, dAj[:, (s % 2) * SC:(s % 2 + 1) * SC],
                                                bj[:, (s % 2) * SC:(s % 2 + 1) * SC],
                                                states[:, j * S + s:j * S + s + 1], ALU.mult, ALU.add)
                                            ctr["vec"] += 1
                                            i1.then_inc(sVEC, 1)
                                            scnt[s] = ctr["vec"]
                                            if s + 1 < S:
                                                i2 = v.tensor_tensor(
                                                    bj[:, ((s + 1) % 2) * SC:((s + 1) % 2 + 1) * SC],
                                                    duj[:], brep[:, (s + 1) * SC:(s + 2) * SC], ALU.mult)
                                                ctr["vec"] += 1
                                                i2.then_inc(sVEC, 1)
                                                bcnt[s + 1] = ctr["vec"]
                                            v.wait_ge(sVEC, scnt[s])
                                            i3 = v.tensor_copy(states[:, j * S + s:j * S + s + 1], hjs[:, SC - 1:SC])
                                            ctr["vec"] += 1
                                            i3.then_inc(sVEC, 1)
                                            if own:
                                                i4 = v.tensor_tensor(tmpj[:], hjs, crep[:, s * SC:(s + 1) * SC], ALU.mult)
                                                ctr["vec"] += 1
                                                i4.then_inc(sVEC, 1)
                                                v.wait_ge(sVEC, ctr["vec"])
                                                if s == 0:
                                                    i5 = v.tensor_copy(yj[:], tmpj[:])
                                                else:
                                                    i5 = v.tensor_tensor(yj[:], yj[:], tmpj[:], ALU.add)
                                                ctr["vec"] += 1
                                                i5.then_inc(sVEC, 1)

                                    if own:
                                        SY(lambda sy, j=j, d=d: sy.dma_start(
                                            out=wzrot[:].rearrange("p (k n) -> p k n", k=NKD),
                                            in_=wz_h[d][:, j * 128:(j + 1) * 128].rearrange("(k p) n -> p k n", p=128)))

                                        def _mz(t, s0=s0):
                                            last = None
                                            ms = min(MM, SC)
                                            for h2 in range(0, SC, ms):
                                                for kb in range(NKD):
                                                    last = t.matmul(pC[:, h2:h2 + ms], wzrot[:, kb * 128:(kb + 1) * 128],
                                                                    xn2b[:, kb * CH + s0 + h2:kb * CH + s0 + h2 + ms],
                                                                    start=(kb == 0), stop=(kb == NKD - 1))
                                            return last
                                        PE(_mz)
                                        ACTG(lambda a_: a_.activation(gj[:], pC[:, 0:SC], AF.Silu))

                                        def _yf(v, C, j=j, d=d):
                                            C(v.scalar_tensor_tensor(yj[:], xc_j[:, 0:SC], cD(d, j), yj[:],
                                                                     ALU.mult, ALU.add))
                                            if d == 0:
                                                return v.tensor_tensor(yfj[:], yj[:], gj[:], ALU.mult)
                                            return v.tensor_tensor(yfj[:], yj[:, ::-1], gj[:, ::-1], ALU.mult)
                                        VEC(_yf)

                                        def _styf(g, j=j, s0=s0, d=d):
                                            if d == 0:
                                                return g.dma_start(out=yfin_h[d, j * 128:(j + 1) * 128, s0:s0 + SC],
                                                                   in_=yfj[:])
                                            r0 = CH - (s0 + SC)
                                            return g.dma_start(out=yfin_h[d, j * 128:(j + 1) * 128, r0:r0 + SC],
                                                               in_=yfj[:])
                                        GP(_styf)

                            if skippable:
                                deltas = {k: ctr[k] - snap[k] for k in ctr}
                                for _k, _deco in _edecos:
                                    @_deco
                                    def _(eng, _k=_k, c=c, d=d, snap=snap):
                                        guards[_k].__exit__(None, None, None)
                                        with eng.Else():
                                            if deltas[_k] > 0:
                                                eng.wait_ge(SEM[_k], snap[_k])
                                                rem = deltas[_k]
                                                pi = 0
                                                if _k == "pe":
                                                    eng.wait_ge(SEM["act"], snap["act"])
                                                while rem > 0:
                                                    piece = min(240, rem)
                                                    if _k in ("ld", "st"):
                                                        col = dumcol[0]
                                                        dumcol[0] += 1
                                                        assert col < 512
                                                        eng.dma_start(
                                                            out=cfg_sb[0:1, col:col + 1],
                                                            in_=cfg_sb[0:1, 0:1]).then_inc(SEM[_k], piece)
                                                    elif _k == "pe":
                                                        eng.matmul(pD[0:8, 0:8], ident_b[0:8, 0:8],
                                                                   ident_b[0:8, 0:8], start=True,
                                                                   stop=True).then_inc(SEM[_k], piece)
                                                    elif _k == "act":
                                                        assert pi < 32
                                                        eng.activation(scr_sb[0:1, pi:pi + 1],
                                                                       consts[0:1, oK:oK + 1],
                                                                       AF.Copy).then_inc(SEM[_k], piece)
                                                    else:
                                                        assert pi < 32
                                                        eng.memset(scr_sb[0:1, 32 + pi:33 + pi],
                                                                   0.0).then_inc(SEM[_k], piece)
                                                    pi += 1
                                                    rem -= piece
                                for _e in waited:
                                    for _k in waited[_e]:
                                        waited[_e][_k] = 0

                MC = min(512, CH)
                NMC = CH // MC
                with contextlib.ExitStack() as ESP:
                    wsA = ESP.enter_context(nc.sbuf_tensor("wsA", [128, 2 * NKD * 128], BF16))
                    wsB = ESP.enter_context(nc.sbuf_tensor("wsB", [128, 2 * NKD * 128], BF16))
                    oaccf = ESP.enter_context(nc.sbuf_tensor("oaccf", [128, NKD * CH], F32))
                    with contextlib.ExitStack() as ESG:
                        xfm = ESG.enter_context(nc.sbuf_tensor("xfm", [128, NKD * MC], BF16))
                        scf = ESG.enter_context(nc.sbuf_tensor("scf", [16, CH], F32))
                        sct = ESG.enter_context(nc.sbuf_tensor("sct", [128, 16], F32))
                        gsc = ESG.enter_context(nc.sbuf_tensor("gsc", [128, 16], F32))
                        pbf = ESG.enter_context(nc.sbuf_tensor("pbf", [16, CH], BF16))
                        for mi in range(NMC):
                            SY(lambda sy, mi=mi: sy.dma_start(
                                out=xfm[:].rearrange("p (k n) -> p k n", k=NKD),
                                in_=xnfm_h[:, mi * MC:(mi + 1) * MC].rearrange("(k p) n -> p k n", p=128)))

                            def _msc(t, mi=mi):
                                last = None
                                for kb in range(NKD):
                                    last = t.matmul(pA[0:E, 0:MC], wg_sb[:, kb * E:(kb + 1) * E],
                                                    xfm[:, kb * MC:(kb + 1) * MC],
                                                    start=(kb == 0), stop=(kb == NKD - 1))
                                return last
                            PE(_msc)
                            ACTG(lambda a_, mi=mi: a_.activation(scf[0:E, mi * MC:(mi + 1) * MC], pA[0:E, 0:MC], AF.Copy))

                        for a in range(NT):
                            PE(lambda t, a=a: t.transpose(pD[:, 0:8], scf[0:E, a * 128:(a + 1) * 128], ident_f[0:8, 0:8]))
                            ACTG(lambda a_: a_.activation(sct[:, 0:8], pD[:, 0:8], AF.Copy))

                            VEC(lambda v: v.reduce_max(sct[:, 8:9], sct[:, 0:8], AX.X, negate=True))
                            ACTG(lambda a_: a_.activation(sct[:, 13:14], sct[:, 8:9], AF.Copy))

                            def _g8(v, C):
                                s8 = sct[:, 0:8]
                                C(v.tensor_scalar(sct[:, 9:10], sct[:, 13:14], -1.0, None, ALU.mult))
                                C(v.tensor_scalar(gsc[:, 0:8], s8, sct[:, 9:10], None, ALU.is_ge))
                                C(v.scalar_tensor_tensor(gsc[:, 8:16], gsc[:, 0:8], -1e30, s8, ALU.mult, ALU.add))
                                return v.reduce_max(sct[:, 10:11], gsc[:, 8:16], AX.X)
                            VEC(_g8)

                            def _gl(a_):
                                a_.activation(sct[:, 14:15], sct[:, 10:11], AF.Copy)
                                return a_.activation(gsc[:, 8:16], sct[:, 0:8], AF.Exp, bias=sct[:, 13:14])
                            ACTG(_gl)

                            def _g9(v, C):
                                C(v.tensor_scalar(gsc[:, 0:8], sct[:, 0:8], sct[:, 14:15], None, ALU.is_ge))
                                C(v.tensor_tensor(gsc[:, 8:16], gsc[:, 8:16], gsc[:, 0:8], ALU.mult))
                                return v.reduce_sum(sct[:, 11:12], gsc[:, 8:16], AX.X)
                            VEC(_g9)
                            ACTG(lambda a_: a_.activation(sct[:, 15:16], sct[:, 11:12], AF.Copy))
                            VEC(lambda v: v.reciprocal(sct[:, 12:13], sct[:, 15:16]))
                            ACTG(lambda a_: a_.activation(sct[:, 9:10], sct[:, 12:13], AF.Copy))
                            VEC(lambda v: v.tensor_scalar(gsc[:, 0:8], gsc[:, 8:16], sct[:, 9:10], None, ALU.mult))
                            PE(lambda t, a=a: t.transpose(pD[0:8, 128:256], gsc[:, 0:8], ident_f[:]))
                            ACTG(lambda a_, a=a: a_.activation(pbf[0:E, a * 128:(a + 1) * 128], pD[0:8, 128:256], AF.Copy))
                        GP(lambda g: g.dma_start(out=p_h[:], in_=pbf[0:E, :]))

                    with contextlib.ExitStack() as ESFF:
                        featf = ESFF.enter_context(nc.sbuf_tensor("featf", [128, NKD * CH], BF16))
                        featb = ESFF.enter_context(nc.sbuf_tensor("featb", [128, NKD * CH], BF16))
                        yfm = ESFF.enter_context(nc.sbuf_tensor("yfm", [128, NJ * MC], BF16))
                        wci = [0]
                        for d in range(2):
                            feat = featf if d == 0 else featb
                            for mi in range(NMC):
                                SY(lambda sy, d=d, mi=mi: sy.dma_start(
                                    out=yfm[:].rearrange("p (k n) -> p k n", k=NJ),
                                    in_=yfin_h[d][:, mi * MC:(mi + 1) * MC].rearrange("(k p) n -> p k n", p=128)))
                                for n in range(NKD):
                                    wb = wsA if wci[0] % 2 == 0 else wsB
                                    wci[0] += 1
                                    SY(lambda sy, d=d, n=n, wb=wb: sy.dma_start(
                                        out=wb[:, 0:NKD * 128].rearrange("p (k n) -> p k n", k=NKD),
                                        in_=wout_h[d][0:DI // 2, n * 128:(n + 1) * 128].rearrange("(k p) n -> p k n", p=128)))
                                    GP(lambda g, d=d, n=n, wb=wb: g.dma_start(
                                        out=wb[:, NKD * 128:2 * NKD * 128].rearrange("p (k n) -> p k n", k=NKD),
                                        in_=wout_h[d][DI // 2:DI, n * 128:(n + 1) * 128].rearrange("(k p) n -> p k n", p=128)))

                                    def _mf(t, n=n, wb=wb):
                                        last = None
                                        for k in range(NJ):
                                            last = t.matmul(pA[:, 0:MC], wb[:, k * 128:(k + 1) * 128],
                                                            yfm[:, k * MC:(k + 1) * MC],
                                                            start=(k == 0), stop=(k == NJ - 1))
                                        return last
                                    PE(_mf)
                                    ACTG(lambda a_, n=n, mi=mi, feat=feat: a_.activation(
                                        feat[:, n * CH + mi * MC:n * CH + (mi + 1) * MC], pA[:, 0:MC], AF.Copy))

                        for nn in range(NKD):
                            wb = wsA if wci[0] % 2 == 0 else wsB
                            wci[0] += 1
                            SY(lambda sy, nn=nn, wb=wb: sy.dma_start(
                                out=wb[:, 0:NKD * 128].rearrange("p (k n) -> p k n", k=NKD),
                                in_=wfus_h[0:DM, nn * 128:(nn + 1) * 128].rearrange("(k p) n -> p k n", p=128)))
                            GP(lambda g, nn=nn, wb=wb: g.dma_start(
                                out=wb[:, NKD * 128:2 * NKD * 128].rearrange("p (k n) -> p k n", k=NKD),
                                in_=wfus_h[DM:2 * DM, nn * 128:(nn + 1) * 128].rearrange("(k p) n -> p k n", p=128)))
                            for mi in range(NMC):
                                def _mfus(t, nn=nn, mi=mi, wb=wb):
                                    for k in range(NKD):
                                        t.matmul(pB[:, 0:MC], wb[:, k * 128:(k + 1) * 128],
                                                 featf[:, k * CH + mi * MC:k * CH + (mi + 1) * MC],
                                                 start=(k == 0), stop=False)
                                    last = None
                                    for k in range(NKD):
                                        last = t.matmul(pB[:, 0:MC], wb[:, (NKD + k) * 128:(NKD + k + 1) * 128],
                                                        featb[:, k * CH + mi * MC:k * CH + (mi + 1) * MC],
                                                        start=False, stop=(k == NKD - 1))
                                    return last
                                PE(_mfus)
                                VEC(lambda v, nn=nn, mi=mi: v.tensor_scalar(
                                    oaccf[:, nn * CH + mi * MC:nn * CH + (mi + 1) * MC],
                                    pB[:, 0:MC], cFB(nn), None, ALU.add))

                    MP = min(2 * MC, CH)
                    NMP = CH // MP
                    HH = MP // MC
                    with contextlib.ExitStack() as ESE:
                        ws3 = ESE.enter_context(nc.sbuf_tensor("ws3", [128, 4 * NJ * 128], BF16))
                        xfm_p = ESE.enter_context(nc.sbuf_tensor("xfm_p", [128, NKD * MP], BF16))
                        prep_ep = ESE.enter_context(nc.sbuf_tensor("prep_ep", [128, MP], BF16))
                        hidden_p = ESE.enter_context(nc.sbuf_tensor("hidden_p", [128, NJ * MP], BF16))
                        hsil2 = ESE.enter_context(nc.sbuf_tensor("hsil2", [128, MP], F32))
                        otok = ESE.enter_context(nc.sbuf_tensor("otok", [128, DM], F32))
                        xrl = ESE.enter_context(nc.sbuf_tensor("xrl", [128, DM], F32))
                        xnrl = ESE.enter_context(nc.sbuf_tensor("xnrl", [128, DM], F32))
                        pe_mm = {}
                        pe_w3s = {}
                        act_sil = {}
                        vec_hid = {}
                        ii = [0]
                        pe_start = ctr["pe"]
                        act_start = ctr["act"]
                        vec_start = ctr["vec"]
                        pe_w3_prev = [pe_start]
                        last_sy = [ctr["ld"]]
                        last_gp = [ctr["st"]]
                        last_ad = [ctr["ad"]]
                        for mp in range(NMP):
                            @block.sync
                            def _(sy, mp=mp, pin=last_sy[0]):
                                sy.wait_ge(sLD, pin)
                                sy.wait_ge(sPE, ctr["pe"])
                                b_("ld", sy.dma_start(
                                    out=xfm_p[:].rearrange("p (k n) -> p k n", k=NKD),
                                    in_=xnfm_h[:, mp * MP:(mp + 1) * MP].rearrange("(k p) n -> p k n", p=128)))
                            last_sy[0] = ctr["ld"]
                            for e in range(E):
                                @block.sync
                                def _(sy, e=e, mp=mp, pin=last_sy[0]):
                                    sy.wait_ge(sLD, pin)
                                    sy.wait_ge(sVEC, ctr["vec"])
                                    b_("ld", sy.dma_start(
                                        out=prep_ep[:],
                                        in_=p_h[e, mp * MP:(mp + 1) * MP].partition_broadcast(128)))
                                last_sy[0] = ctr["ld"]
                                ld_prep = ctr["ld"]
                                ld_w12 = {}
                                ld_w3 = {}
                                for nh in range(NJ):
                                    wb = wsA if ii[0] % 2 == 0 else wsB

                                    @block.sync
                                    def _(sy, e=e, nh=nh, wb=wb, i0=ii[0], pin=last_sy[0]):
                                        sy.wait_ge(sLD, pin)
                                        sy.wait_ge(sPE, pe_mm.get(i0 - 2, pe_start))
                                        b_("ld", sy.dma_start(
                                            out=wb[:, 0:NKD * 128].rearrange("p (k n) -> p k n", k=NKD),
                                            in_=w12_h[e][:, nh * 128:(nh + 1) * 128].rearrange("(k p) n -> p k n", p=128)))
                                    last_sy[0] = ctr["ld"]

                                    @block.gpsimd
                                    def _(g, e=e, nh=nh, wb=wb, i0=ii[0], pin=last_gp[0]):
                                        g.wait_ge(sST, pin)
                                        g.wait_ge(sPE, pe_mm.get(i0 - 2, pe_start))
                                        b_("st", g.dma_start(
                                            out=wb[:, NKD * 128:2 * NKD * 128].rearrange("p (k n) -> p k n", k=NKD),
                                            in_=w12_h[e][:, HF + nh * 128:HF + (nh + 1) * 128].rearrange("(k p) n -> p k n", p=128)))
                                    last_gp[0] = ctr["st"]
                                    ld_w12[ii[0]] = (ctr["ld"], ctr["st"])

                                    if nh < 4:
                                        @block.scalar
                                        def _(a, e=e, nn=nh, pin=last_ad[0]):
                                            a.wait_ge(sAD, pin)
                                            a.wait_ge(sPE, pe_w3_prev[0])
                                            b_("ad", a.dma_start(
                                                out=ws3[:, (nn % 4) * NJ * 128:(nn % 4 + 1) * NJ * 128].rearrange(
                                                    "p (k n) -> p k n", k=NJ),
                                                in_=w3_h[e][:, nn * 128:(nn + 1) * 128].rearrange("(k p) n -> p k n", p=128)))
                                        last_ad[0] = ctr["ad"]
                                        ld_w3[nh] = ctr["ad"]

                                    @block.tensor
                                    def _(t, wb=wb, i0=ii[0]):
                                        lw, sw = ld_w12[i0]
                                        t.wait_ge(sLD, lw)
                                        t.wait_ge(sST, sw)
                                        t.wait_ge(sACT, act_sil.get(i0 - 1, act_start))
                                        t.wait_ge(sVEC, vec_hid.get(i0 - 1, vec_start))
                                        for hh in range(HH):
                                            for kb in range(NKD):
                                                t.matmul(pA[:, hh * MC:(hh + 1) * MC],
                                                         wb[:, kb * 128:(kb + 1) * 128],
                                                         xfm_p[:, kb * MP + hh * MC:kb * MP + (hh + 1) * MC],
                                                         start=(kb == 0), stop=(kb == NKD - 1))
                                        last = None
                                        for hh in range(HH):
                                            for kb in range(NKD):
                                                last = t.matmul(pB[:, hh * MC:(hh + 1) * MC],
                                                                wb[:, (NKD + kb) * 128:(NKD + kb + 1) * 128],
                                                                xfm_p[:, kb * MP + hh * MC:kb * MP + (hh + 1) * MC],
                                                                start=(kb == 0), stop=(kb == NKD - 1))
                                        ctr["pe"] += 1
                                        last.then_inc(sPE, 1)
                                    pe_mm[ii[0]] = ctr["pe"]

                                    @block.scalar
                                    def _(a, i0=ii[0]):
                                        a.wait_ge(sPE, pe_mm[i0])
                                        a.wait_ge(sVEC, vec_hid.get(i0 - 1, vec_start))
                                        inst = a.activation(hsil2[:], pA[:, 0:MP], AF.Silu)
                                        ctr["act"] += 1
                                        inst.then_inc(sACT, 1)
                                    act_sil[ii[0]] = ctr["act"]

                                    @block.vector
                                    def _(v, nh=nh, i0=ii[0], ld_prep=ld_prep, vself=ctr["vec"]):
                                        v.wait_ge(sACT, act_sil[i0])
                                        v.wait_ge(sLD, ld_prep)
                                        v.wait_ge(sVEC, vself)
                                        hslice = hidden_p[:, nh * MP:(nh + 1) * MP]
                                        i1 = v.tensor_tensor(hslice, hsil2[:], pB[:, 0:MP], ALU.mult)
                                        ctr["vec"] += 1
                                        i1.then_inc(sVEC, 1)
                                        v.wait_ge(sVEC, ctr["vec"])
                                        i2 = v.tensor_tensor(hslice, hslice, prep_ep[:], ALU.mult)
                                        ctr["vec"] += 1
                                        i2.then_inc(sVEC, 1)
                                    vec_hid[ii[0]] = ctr["vec"]
                                    ii[0] += 1

                                vec_hid_last = ctr["vec"]
                                vec_oacc = {}
                                for nn in range(NKD):
                                    if nn >= 4:
                                        @block.scalar
                                        def _(a, e=e, nn=nn, pin=last_ad[0]):
                                            a.wait_ge(sAD, pin)
                                            a.wait_ge(sPE, pe_w3s[nn - 4])
                                            b_("ad", a.dma_start(
                                                out=ws3[:, (nn % 4) * NJ * 128:(nn % 4 + 1) * NJ * 128].rearrange(
                                                    "p (k n) -> p k n", k=NJ),
                                                in_=w3_h[e][:, nn * 128:(nn + 1) * 128].rearrange("(k p) n -> p k n", p=128)))
                                        last_ad[0] = ctr["ad"]
                                        ld_w3[nn] = ctr["ad"]

                                    @block.tensor
                                    def _(t, nn=nn, vh=vec_hid_last):
                                        t.wait_ge(sAD, ld_w3[nn])
                                        t.wait_ge(sVEC, vh)
                                        if nn >= 1:
                                            t.wait_ge(sVEC, vec_oacc[nn - 1])
                                        last = None
                                        for hh in range(HH):
                                            for k in range(NJ):
                                                last = t.matmul(pC[:, hh * MC:(hh + 1) * MC],
                                                                ws3[:, (nn % 4) * NJ * 128 + k * 128:
                                                                    (nn % 4) * NJ * 128 + (k + 1) * 128],
                                                                hidden_p[:, k * MP + hh * MC:k * MP + (hh + 1) * MC],
                                                                start=(k == 0), stop=(k == NJ - 1))
                                        ctr["pe"] += 1
                                        last.then_inc(sPE, 1)
                                    pe_w3s[nn] = ctr["pe"]

                                    @block.vector
                                    def _(v, nn=nn, mp=mp, vself=ctr["vec"]):
                                        v.wait_ge(sPE, pe_w3s[nn])
                                        v.wait_ge(sVEC, vself)
                                        col = nn * CH + mp * MP
                                        i1 = v.tensor_tensor(oaccf[:, col:col + MP],
                                                             oaccf[:, col:col + MP], pC[:, 0:MP], ALU.add)
                                        ctr["vec"] += 1
                                        i1.then_inc(sVEC, 1)
                                    vec_oacc[nn] = ctr["vec"]
                                pe_w3_prev[0] = ctr["pe"]

                        for mi in range(NMC):
                            for tt in range(MC // 128):
                                def _mtr(t, mi=mi, tt=tt):
                                    last = None
                                    for nn in range(NKD):
                                        c0 = nn * CH + mi * MC + tt * 128
                                        last = t.transpose(pA[:, nn * 128:(nn + 1) * 128],
                                                           oaccf[:, c0:c0 + 128], ident_f[:])
                                    return last
                                PE(_mtr)
                                r0 = mi * MC + tt * 128

                                def _ldr(sy, r0=r0):
                                    b_("ld", sy.dma_start(out=xrl[:], in_=xf[T - CH + r0:T - CH + r0 + 128, :]))
                                    b_("ld", sy.dma_start(out=xnrl[:], in_=xn_h[r0:r0 + 128, :]))
                                SY(_ldr)

                                def _fin(v, C):
                                    C(v.tensor_tensor(otok[:], pA[:, 0:DM], xrl[:], ALU.add))
                                    return v.tensor_tensor(otok[:], otok[:], xnrl[:], ALU.add)
                                VEC(_fin)
                                GP(lambda g, r0=r0: g.dma_start(out=y_out[r0:r0 + 128, :], in_=otok[:]))

        @block.sync
        def _(sy):
            sy.wait_ge(sST, ctr["st"])

    return nc


_BUILD_CACHE = {}


def _get_nc(T, CH):
    key = (T, CH)
    if key not in _BUILD_CACHE:
        _BUILD_CACHE[key] = build(T, CH)
    return _BUILD_CACHE[key]


def make_inmaps(T, CH, x, weights):
    NCH = T // CH
    OWNT = T // 8
    f32 = np.float32

    def pp(vec, nblk):
        return np.ascontiguousarray(vec.reshape(nblk, 128).T.astype(f32))

    wd = {}
    wd["w_xi"] = np.ascontiguousarray(np.stack([
        weights["f_in_w"][:DI].T, weights["b_in_w"][:DI].T]).astype(f32))
    wd["w_z"] = np.ascontiguousarray(np.stack([
        weights["f_in_w"][DI:].T, weights["b_in_w"][DI:].T]).astype(f32))
    wd["w_xp"] = np.ascontiguousarray(np.stack([
        weights["f_xproj_w"].T, weights["b_xproj_w"].T]).astype(f32))
    wd["w_dt"] = np.ascontiguousarray(np.stack([
        weights["f_dt_w"].T, weights["b_dt_w"].T]).astype(f32))
    wd["dtb"] = np.ascontiguousarray(np.stack([
        pp(weights["f_dt_b"], NJ), pp(weights["b_dt_b"], NJ)]))
    cw = []
    for p in ("f", "b"):
        c = weights[p + "_conv_w"][:, 0, :]
        cw.append(np.ascontiguousarray(
            c.reshape(NJ, 128, 4).transpose(1, 0, 2).reshape(128, NJ * 4).astype(f32)))
    wd["convw"] = np.ascontiguousarray(np.stack(cw))
    wd["convb"] = np.ascontiguousarray(np.stack([
        pp(weights["f_conv_b"], NJ), pp(weights["b_conv_b"], NJ)]))
    al = []
    for p in ("f", "b"):
        a = weights[p + "_A_log"]
        al.append(np.ascontiguousarray(
            a.reshape(NJ, 128, S).transpose(1, 0, 2).reshape(128, NJ * S).astype(f32)))
    wd["alog"] = np.ascontiguousarray(np.stack(al))
    wd["dvec"] = np.ascontiguousarray(np.stack([
        pp(weights["f_D"], NJ), pp(weights["b_D"], NJ)]))
    wd["w_out"] = np.ascontiguousarray(np.stack([
        weights["f_out_w"].T, weights["b_out_w"].T]).astype(f32))
    wd["w_fus"] = np.ascontiguousarray(weights["fus_w"].T.astype(f32))
    wd["fusb"] = pp(weights["fus_b"], NKD)
    wd["lng"] = pp(weights["ln_g"], NKD)
    wd["lnb"] = pp(weights["ln_b"], NKD)
    wd["w12"] = np.ascontiguousarray(weights["ffn_w12"].transpose(0, 2, 1).astype(f32))
    wd["w3"] = np.ascontiguousarray(weights["ffn_w3"].transpose(0, 2, 1).astype(f32))
    wd["wgate"] = np.ascontiguousarray(weights["ffn_gate_w"].T.astype(f32))
    wd["ident"] = np.eye(128, dtype=f32)

    xr = x[::-1]
    in_maps = []
    for k in range(8):
        L = OWNT * (k + 1)
        padf = T - L
        padb = OWNT * k
        xfk = np.concatenate([np.zeros((padf, DM), f32), x[:L]], axis=0)
        xbk = np.concatenate([np.zeros((padb, DM), f32), xr[:T - padb]], axis=0)
        penk = np.zeros((2, T), f32)
        penk[0, :padf] = PEN
        penk[1, :padb] = PEN
        penk = penk.astype(ml_dtypes.bfloat16)
        gk = np.ones((128, 2 * NCH), f32)
        npadf = padf // CH
        npadb = padb // CH
        gk[:, 0:npadf + 1] = 0.0
        gk[:, NCH:NCH + npadb + 1] = 0.0
        m = dict(xf=np.ascontiguousarray(xfk), xb=np.ascontiguousarray(xbk),
                 pen=penk, gates=gk,
                 cfg=np.array([[npadf, npadb, 0, 0]], dtype=np.int32))
        m.update(wd)
        in_maps.append(m)
    return in_maps


def kernel(L0_attn_x, L0_attn_ln_g, L0_attn_ln_b, L0_attn_fus_w, L0_attn_fus_b,
           L0_attn_f_in_w, L0_attn_f_conv_w, L0_attn_f_conv_b, L0_attn_f_xproj_w,
           L0_attn_f_dt_w, L0_attn_f_dt_b, L0_attn_f_A_log, L0_attn_f_D, L0_attn_f_out_w,
           L0_attn_b_in_w, L0_attn_b_conv_w, L0_attn_b_conv_b, L0_attn_b_xproj_w,
           L0_attn_b_dt_w, L0_attn_b_dt_b, L0_attn_b_A_log, L0_attn_b_D, L0_attn_b_out_w,
           L0_ffn_w12, L0_ffn_w3, L0_ffn_gate_w):
    T, CH = 16384, 2048
    x = np.ascontiguousarray(np.asarray(L0_attn_x, dtype=np.float32)[0])
    weights = dict(
        ln_g=np.asarray(L0_attn_ln_g), ln_b=np.asarray(L0_attn_ln_b),
        fus_w=np.asarray(L0_attn_fus_w), fus_b=np.asarray(L0_attn_fus_b),
        f_in_w=np.asarray(L0_attn_f_in_w), f_conv_w=np.asarray(L0_attn_f_conv_w),
        f_conv_b=np.asarray(L0_attn_f_conv_b), f_xproj_w=np.asarray(L0_attn_f_xproj_w),
        f_dt_w=np.asarray(L0_attn_f_dt_w), f_dt_b=np.asarray(L0_attn_f_dt_b),
        f_A_log=np.asarray(L0_attn_f_A_log), f_D=np.asarray(L0_attn_f_D),
        f_out_w=np.asarray(L0_attn_f_out_w),
        b_in_w=np.asarray(L0_attn_b_in_w), b_conv_w=np.asarray(L0_attn_b_conv_w),
        b_conv_b=np.asarray(L0_attn_b_conv_b), b_xproj_w=np.asarray(L0_attn_b_xproj_w),
        b_dt_w=np.asarray(L0_attn_b_dt_w), b_dt_b=np.asarray(L0_attn_b_dt_b),
        b_A_log=np.asarray(L0_attn_b_A_log), b_D=np.asarray(L0_attn_b_D),
        b_out_w=np.asarray(L0_attn_b_out_w),
        ffn_w12=np.asarray(L0_ffn_w12), ffn_w3=np.asarray(L0_ffn_w3),
        ffn_gate_w=np.asarray(L0_ffn_gate_w),
    )
    nc = _get_nc(T, CH)
    in_maps = make_inmaps(T, CH, x, weights)
    res = run_bass_kernel_spmd(nc, in_maps, core_ids=list(range(8)))
    out = np.concatenate([res.results[k]["y_out"] for k in range(8)], axis=0)
    return out.reshape(1, T, DM).astype(np.float32)


# version 10
# speedup vs baseline: 2.5090x; 1.0783x over previous
import contextlib
import ml_dtypes
import numpy as np
import concourse.bass as bass
import concourse.mybir as mybir
from concourse.bass_utils import run_bass_kernel_spmd

F32 = mybir.dt.float32
BF16 = mybir.dt.bfloat16
AF = mybir.ActivationFunctionType
ALU = mybir.AluOpType
AX = mybir.AxisListType

DM, DI, S, RANK, E, HF = 1024, 2048, 16, 64, 8, 2048
NJ, NKD = DI // 128, DM // 128
EPS = 1e-5
PEN = -30000.0


def build(T, CH, dbg=False):
    NCH = T // CH
    NT = CH // 128
    QT = max(1, NT // 8)
    NQ = NT // QT
    SC = CH // 2
    MM = min(512, CH)
    NMM = CH // MM
    OWN = NCH - 1
    nc = bass.Bass("TRN2", target_bir_lowering=False)
    dram = nc.dram_tensor

    xf = dram("xf", [T, DM], F32, kind="ExternalInput").ap()
    xb = dram("xb", [T, DM], F32, kind="ExternalInput").ap()
    pen = dram("pen", [2, T], BF16, kind="ExternalInput").ap()
    gates = dram("gates", [128, 2 * NCH], F32, kind="ExternalInput").ap()
    w_xi = dram("w_xi", [2, DM, DI], F32, kind="ExternalInput").ap()
    w_z = dram("w_z", [2, DM, DI], F32, kind="ExternalInput").ap()
    w_xp = dram("w_xp", [2, DI, 96], F32, kind="ExternalInput").ap()
    w_dt = dram("w_dt", [2, RANK, DI], F32, kind="ExternalInput").ap()
    dtb = dram("dtb", [2, 128, NJ], F32, kind="ExternalInput").ap()
    convw = dram("convw", [2, 128, NJ * 4], F32, kind="ExternalInput").ap()
    convb = dram("convb", [2, 128, NJ], F32, kind="ExternalInput").ap()
    alog = dram("alog", [2, 128, NJ * S], F32, kind="ExternalInput").ap()
    dvec = dram("dvec", [2, 128, NJ], F32, kind="ExternalInput").ap()
    w_out = dram("w_out", [2, DI, DM], F32, kind="ExternalInput").ap()
    w_fus = dram("w_fus", [2 * DM, DM], F32, kind="ExternalInput").ap()
    fusb = dram("fusb", [128, NKD], F32, kind="ExternalInput").ap()
    lng = dram("lng", [128, NKD], F32, kind="ExternalInput").ap()
    lnb = dram("lnb", [128, NKD], F32, kind="ExternalInput").ap()
    w12 = dram("w12", [E, DM, 2 * HF], F32, kind="ExternalInput").ap()
    w3 = dram("w3", [E, HF, DM], F32, kind="ExternalInput").ap()
    wgate = dram("wgate", [DM, E], F32, kind="ExternalInput").ap()
    ident_in = dram("ident", [128, 128], F32, kind="ExternalInput").ap()
    cfg = dram("cfg", [1, 4], mybir.dt.int32, kind="ExternalInput").ap()

    y_out = dram("y_out", [CH, DM], F32, kind="ExternalOutput").ap()

    IK = "ExternalOutput" if dbg else "Internal"
    wxi_h = dram("wxi_h", [2, DM, DI], BF16, kind="Internal").ap()
    wz_h = dram("wz_h", [2, DM, DI], BF16, kind="Internal").ap()
    wout_h = dram("wout_h", [2, DI, DM], BF16, kind="Internal").ap()
    wfus_h = dram("wfus_h", [2 * DM, DM], BF16, kind="Internal").ap()
    w12_h = dram("w12_h", [E, DM, 2 * HF], BF16, kind="Internal").ap()
    w3_h = dram("w3_h", [E, HF, DM], BF16, kind="Internal").ap()
    bc_h = dram("bc_h", [32, CH], BF16, kind=IK).ap()
    xc_h = dram("xc_h", [DI, CH], BF16, kind=IK).ap()
    xn_h = dram("xn_h", [CH, DM], F32, kind=IK).ap()
    xnfm_h = dram("xnfm_h", [DM, CH], BF16, kind=IK).ap()
    yfin_h = dram("yfin_h", [2, DI, CH], BF16, kind=IK).ap()
    p_h = dram("p_h", [E, CH], BF16, kind=IK).ap()
    xid = dram("xid", [DI, CH], BF16, kind=IK).ap() if dbg else None
    xn2d = dram("xn2d", [DM, CH], BF16, kind=IK).ap() if dbg else None

    ctr = dict(ld=0, st=0, pe=0, act=0, vec=0, ad=0)

    with contextlib.ExitStack() as ES:
        sLD = ES.enter_context(nc.semaphore("sLD"))
        sST = ES.enter_context(nc.semaphore("sST"))
        sPE = ES.enter_context(nc.semaphore("sPE"))
        sACT = ES.enter_context(nc.semaphore("sACT"))
        sVEC = ES.enter_context(nc.semaphore("sVEC"))
        sAD = ES.enter_context(nc.semaphore("sAD"))
        block = ES.enter_context(nc.Block())
        SEM = dict(ld=sLD, st=sST, pe=sPE, act=sACT, vec=sVEC, ad=sAD)

        waited = {e: dict(ld=0, st=0, pe=0, act=0, vec=0, ad=0) for e in
                  ("ld", "st", "pe", "act", "vec")}

        def wall(eng, me):
            w_ = waited[me]
            for k, sem in SEM.items():
                if ctr[k] > w_[k]:
                    eng.wait_ge(sem, ctr[k])
                    w_[k] = ctr[k]

        def b_(key, inst):
            n = 16 if key in ("ld", "st", "ad") else 1
            ctr[key] += n
            inst.then_inc(SEM[key], n)
            return inst

        import inspect as _insp

        def _wants_chain(fn):
            try:
                return len(_insp.signature(fn).parameters) >= 2 and \
                    "C" in _insp.signature(fn).parameters
            except (TypeError, ValueError):
                return False

        def _mk_chain(eng, key):
            def C(inst):
                ctr[key] += 1
                inst.then_inc(SEM[key], 1)
                eng.wait_ge(SEM[key], ctr[key])
                return inst
            return C

        def _group(block_deco, key):
            def wrap(fn):
                @block_deco
                def _(eng):
                    wall(eng, key)
                    if _wants_chain(fn):
                        r = fn(eng, _mk_chain(eng, key))
                    else:
                        r = fn(eng)
                    if r is not None:
                        b_(key, r)
            return wrap

        def SY(fn):
            _group(block.sync, "ld")(fn)

        def GP(fn):
            _group(block.gpsimd, "st")(fn)

        def PE(fn):
            _group(block.tensor, "pe")(fn)

        def ACTG(fn):
            _group(block.scalar, "act")(fn)

        def VEC(fn):
            _group(block.vector, "vec")(fn)

        NCON = 2 * NJ * S + 14 * NJ + 3 * NKD + 2 * NCH + 16
        if True:
            ident_b = ES.enter_context(nc.sbuf_tensor("ident_b", [128, 128], BF16))
            ident_f = ES.enter_context(nc.sbuf_tensor("ident_f", [128, 128], F32))
            consts = ES.enter_context(nc.sbuf_tensor("consts", [128, NCON], F32))
            wdt_sb = ES.enter_context(nc.sbuf_tensor("wdt_sb", [65, 2 * DI], BF16))
            wxp_sb = ES.enter_context(nc.sbuf_tensor("wxp_sb", [128, 2 * NJ * 96], BF16))
            wg_sb = ES.enter_context(nc.sbuf_tensor("wg_sb", [128, NKD * E], BF16))
            oA = 0
            oDTB = oA + 2 * NJ * S
            oCW = oDTB + 2 * NJ
            oCB = oCW + 2 * NJ * 4
            oD = oCB + 2 * NJ
            oFB = oD + 2 * NJ
            oLG = oFB + NKD
            oLB = oLG + NKD
            oGT = oLB + NKD

            def CC(i):
                return consts[:, i:i + 1]

            cA = lambda d, j, s: CC(oA + d * NJ * S + j * S + s)
            cDTB = lambda d, j: CC(oDTB + d * NJ + j)
            cCW = lambda d, j, tap: CC(oCW + d * NJ * 4 + j * 4 + tap)
            cCB = lambda d, j: CC(oCB + d * NJ + j)
            cD = lambda d, j: CC(oD + d * NJ + j)
            cFB = lambda n: CC(oFB + n)
            cLG = lambda n: CC(oLG + n)
            cLB = lambda n: CC(oLB + n)
            cGT = lambda d, c: CC(oGT + d * NCH + c)

            SY(lambda sy: sy.dma_start(out=ident_f[:], in_=ident_in))
            SY(lambda sy: sy.dma_start(out=consts[:, oA:oA + 2 * NJ * S].rearrange("p (d j) -> p d j", d=2), in_=alog.rearrange("d p j -> p d j")))
            SY(lambda sy: sy.dma_start(out=consts[:, oDTB:oDTB + 2 * NJ].rearrange("p (d j) -> p d j", d=2), in_=dtb.rearrange("d p j -> p d j")))
            SY(lambda sy: sy.dma_start(out=consts[:, oCW:oCW + 8 * NJ].rearrange("p (d j) -> p d j", d=2), in_=convw.rearrange("d p j -> p d j")))
            SY(lambda sy: sy.dma_start(out=consts[:, oCB:oCB + 2 * NJ].rearrange("p (d j) -> p d j", d=2), in_=convb.rearrange("d p j -> p d j")))
            SY(lambda sy: sy.dma_start(out=consts[:, oD:oD + 2 * NJ].rearrange("p (d j) -> p d j", d=2), in_=dvec.rearrange("d p j -> p d j")))
            SY(lambda sy: sy.dma_start(out=consts[:, oFB:oFB + NKD], in_=fusb))
            SY(lambda sy: sy.dma_start(out=consts[:, oLG:oLG + NKD], in_=lng))
            SY(lambda sy: sy.dma_start(out=consts[:, oLB:oLB + NKD], in_=lnb))
            SY(lambda sy: sy.dma_start(out=consts[:, oGT:oGT + 2 * NCH], in_=gates))

            oK = oGT + 2 * NCH

            def _regs(v):
                v.memset(CC(oK + 0), 1.0)
                v.memset(CC(oK + 1), EPS)
                last = v.memset(CC(oK + 2), EPS * EPS)
                nc.const_aps.register(F32, 1.0, CC(oK + 0))
                nc.const_aps.register(F32, EPS, CC(oK + 1))
                nc.const_aps.register(F32, EPS * EPS, CC(oK + 2))
                return last
            VEC(_regs)

            ACTG(lambda a: a.activation(consts[:, oA:oA + 2 * NJ * S], consts[:, oA:oA + 2 * NJ * S], AF.Exp))

            def _c0(v):
                v.tensor_scalar_mul(consts[:, oA:oA + 2 * NJ * S], consts[:, oA:oA + 2 * NJ * S], -1.0)
                return v.tensor_copy(ident_b[:], ident_f[:])
            VEC(_c0)

            cfg_sb = ES.enter_context(nc.sbuf_tensor("cfg_sb", [1, 512], mybir.dt.int32))
            scr_sb = ES.enter_context(nc.sbuf_tensor("scr_sb", [1, 64], F32))
            dumcol = [8]
            SY(lambda sy: sy.dma_start(out=cfg_sb[0:1, 0:4], in_=cfg))
            skipreg = {}
            _edecos = [("ld", block.sync), ("st", block.gpsimd), ("pe", block.tensor),
                       ("act", block.scalar), ("vec", block.vector)]
            for _k, _deco in _edecos:
                @_deco
                def _(eng, _k=_k):
                    wall(eng, _k)
                    pid = eng.partition_id()
                    r0 = eng.to_reg(NCH - 1 - pid)
                    r1 = eng.to_reg(pid)
                    skipreg[_k] = (r0, r1)

            with contextlib.ExitStack() as ESC:
                NBUF, CSZ = 4, 4096
                cast_f = ESC.enter_context(nc.sbuf_tensor("cast_f", [128, NBUF * CSZ], F32))
                cast_b = ESC.enter_context(nc.sbuf_tensor("cast_b", [128, NBUF * CSZ], BF16))
                jobs = []
                for d in range(2):
                    jobs += [(w_xi[d], wxi_h[d], DM, DI), (w_z[d], wz_h[d], DM, DI),
                             (w_out[d], wout_h[d], DI, DM)]
                jobs.append((w_fus, wfus_h, 2 * DM, DM))
                for e in range(E):
                    jobs += [(w12[e], w12_h[e], DM, 2 * HF), (w3[e], w3_h[e], HF, DM)]
                tiles = []
                for src, dst, rows, cols in jobs:
                    csz = min(CSZ, cols)
                    for r0 in range(0, rows, 128):
                        for c0 in range(0, cols, csz):
                            tiles.append((src, dst, r0, c0, min(csz, cols - c0)))
                NTL = len(tiles)
                pre = dict(ctr)
                ld_base, vec_base, st_base = ctr["ld"], ctr["vec"], ctr["st"]

                def wall_pre(eng, me):
                    w_ = waited[me]
                    for k, sem in SEM.items():
                        if pre[k] > w_[k]:
                            eng.wait_ge(sem, pre[k])
                            w_[k] = pre[k]

                @block.sync
                def _(sy):
                    wall_pre(sy, "ld")
                    for i, (src, dst, r0, c0, n) in enumerate(tiles):
                        if i >= NBUF:
                            sy.wait_ge(sST, st_base + (i - NBUF + 1) * 16)
                        b = (i % NBUF) * CSZ
                        sy.dma_start(out=cast_f[:, b:b + n],
                                     in_=src[r0:r0 + 128, c0:c0 + n]).then_inc(sLD, 16)

                @block.vector
                def _(v):
                    wall_pre(v, "vec")
                    for i, (src, dst, r0, c0, n) in enumerate(tiles):
                        v.wait_ge(sLD, ld_base + (i + 1) * 16)
                        b = (i % NBUF) * CSZ
                        v.tensor_copy(cast_b[:, b:b + n], cast_f[:, b:b + n]).then_inc(sVEC, 1)

                @block.gpsimd
                def _(g):
                    wall_pre(g, "st")
                    for i, (src, dst, r0, c0, n) in enumerate(tiles):
                        g.wait_ge(sVEC, vec_base + i + 1)
                        b = (i % NBUF) * CSZ
                        g.dma_start(out=dst[r0:r0 + 128, c0:c0 + n],
                                    in_=cast_b[:, b:b + n]).then_inc(sST, 16)

                ctr["ld"] += NTL * 16
                ctr["vec"] += NTL
                ctr["st"] += NTL * 16
                waited["ld"]["st"] = max(waited["ld"]["st"], st_base + (NTL - NBUF + 1) * 16) if NTL >= NBUF else waited["ld"]["st"]
                waited["vec"]["ld"] = max(waited["vec"]["ld"], ld_base + NTL * 16)
                waited["st"]["vec"] = max(waited["st"]["vec"], vec_base + NTL)

                SY(lambda sy: sy.dma_start(out=cast_f[0:RANK, 0:2 * DI].rearrange("p (d n) -> p d n", d=2), in_=w_dt.rearrange("d p n -> p d n")))

                def _sdt(v):
                    v.tensor_copy(wdt_sb[0:RANK, :], cast_f[0:RANK, 0:2 * DI])
                    return v.memset(wdt_sb[RANK:65, :], 1.0)
                VEC(_sdt)

                SY(lambda sy: sy.dma_start(
                    out=cast_f[:, 0:2 * NJ * 96].rearrange("p (d j n) -> p d j n", d=2, j=NJ),
                    in_=w_xp.rearrange("d (j p) n -> p d j n", p=128)))
                VEC(lambda v: v.tensor_copy(wxp_sb[:], cast_f[:, 0:2 * NJ * 96]))
                SY(lambda sy: sy.dma_start(
                    out=cast_f[:, 0:NKD * E].rearrange("p (k e) -> p k e", k=NKD),
                    in_=wgate.rearrange("(k p) e -> p k e", p=128)))
                VEC(lambda v: v.tensor_copy(wg_sb[:], cast_f[:, 0:NKD * E]))

            if True:
                pA = ES.enter_context(nc.psum_tensor("pA", [128, 1024], F32))
                pB = ES.enter_context(nc.psum_tensor("pB", [128, 1024], F32))
                pC = ES.enter_context(nc.psum_tensor("pC", [128, 1024], F32))
                pD = ES.enter_context(nc.psum_tensor("pD", [128, 512], F32))
                pTb = ES.enter_context(nc.psum_tensor("pTb", [128, 1024], BF16))
                with contextlib.ExitStack() as ESS:
                    xtok = ESS.enter_context(nc.sbuf_tensor("xtok", [128, QT * DM], F32))
                    sqv = ESS.enter_context(nc.sbuf_tensor("sqv", [128, QT * DM], F32))
                    lnmv = ESS.enter_context(nc.sbuf_tensor("lnmv", [128, 2 * QT], F32))
                    lnr = ESS.enter_context(nc.sbuf_tensor("lnr", [128, QT], F32))
                    lnmv2 = ESS.enter_context(nc.sbuf_tensor("lnmv2", [128, 2 * QT], F32))
                    xn2t = ESS.enter_context(nc.sbuf_tensor("xn2t", [128, QT * DM], BF16))
                    xn2b = ESS.enter_context(nc.sbuf_tensor("xn2b", [128, NKD * CH], BF16))
                    brep = ESS.enter_context(nc.sbuf_tensor("brep", [128, S * SC], BF16))
                    crep = ESS.enter_context(nc.sbuf_tensor("crep", [128, S * SC], BF16))
                    wrot = ESS.enter_context(nc.sbuf_tensor("wrot", [128, NKD * 128], BF16))
                    wzrot = ESS.enter_context(nc.sbuf_tensor("wzrot", [128, NKD * 128], BF16))
                    xi_j = ESS.enter_context(nc.sbuf_tensor("xi_j", [128, CH + 3], BF16))
                    tails = ESS.enter_context(nc.sbuf_tensor("tails", [128, NJ * 3], BF16))
                    accv = ESS.enter_context(nc.sbuf_tensor("accv", [128, CH], F32))
                    xc_j = ESS.enter_context(nc.sbuf_tensor("xc_j", [128, CH], BF16))
                    xcmv = ESS.enter_context(nc.sbuf_tensor("xcmv", [128, 512], BF16))
                    dtaug = ESS.enter_context(nc.sbuf_tensor("dtaug", [65, CH], BF16))
                    bcsb = ESS.enter_context(nc.sbuf_tensor("bcsb", [32, CH], BF16))
                    pensf = ESS.enter_context(nc.sbuf_tensor("pensf", [1, CH], BF16))
                    dlj = ESS.enter_context(nc.sbuf_tensor("dlj", [128, 2 * SC], F32))
                    duj = ESS.enter_context(nc.sbuf_tensor("duj", [128, SC], BF16))
                    dAj = ESS.enter_context(nc.sbuf_tensor("dAj", [128, 2 * SC], F32))
                    bj = ESS.enter_context(nc.sbuf_tensor("bj", [128, 2 * SC], BF16))
                    tmpj = ESS.enter_context(nc.sbuf_tensor("tmpj", [128, SC], BF16))
                    hj = ESS.enter_context(nc.sbuf_tensor("hj", [128, 2 * SC], BF16))
                    yj = ESS.enter_context(nc.sbuf_tensor("yj", [128, SC], F32))
                    gj = ESS.enter_context(nc.sbuf_tensor("gj", [128, SC], BF16))
                    yfj = ESS.enter_context(nc.sbuf_tensor("yfj", [128, SC], BF16))
                    states = ESS.enter_context(nc.sbuf_tensor("states", [128, NJ * S], F32))
                    for d in range(2):
                        xsrc = xf if d == 0 else xb

                        def _rst(v):
                            v.memset(states[:], 0.0)
                            return v.memset(tails[:], 0.0)
                        VEC(_rst)

                        for c in range(NCH):
                            own = (c == OWN)
                            t0 = c * CH

                            skippable = (c < OWN)
                            guards = {}
                            snap = dict(ctr)
                            if skippable:
                                for _k, _deco in _edecos:
                                    @_deco
                                    def _(eng, _k=_k, c=c, d=d):
                                        g = eng.If_lt(skipreg[_k][d], c + 1)
                                        g.__enter__()
                                        guards[_k] = g

                            for q in range(NQ):
                                tq0 = t0 + q * QT * 128
                                SY(lambda sy, tq0=tq0, xsrc=xsrc: sy.dma_start(
                                    out=xtok[:].rearrange("p (a m) -> p a m", a=QT),
                                    in_=xsrc[tq0:tq0 + QT * 128, :].rearrange("(a p) m -> p a m", p=128)))

                                def _ln1(v):
                                    last = None
                                    for a in range(QT):
                                        xt = xtok[:, a * DM:(a + 1) * DM]
                                        last = v.tensor_tensor(sqv[:, a * DM:(a + 1) * DM], xt, xt, ALU.mult)
                                    return last
                                VEC(_ln1)

                                def _ln1r(v):
                                    last = None
                                    for a in range(QT):
                                        v.reduce_sum(lnmv[:, a:a + 1], xtok[:, a * DM:(a + 1) * DM], AX.X)
                                        last = v.reduce_sum(lnmv[:, QT + a:QT + a + 1],
                                                            sqv[:, a * DM:(a + 1) * DM], AX.X)
                                    return last
                                VEC(_ln1r)

                                def _lnl(a_):
                                    a_.activation(lnmv2[:, 0:QT], lnmv[:, 0:QT], AF.Copy, scale=1.0 / DM)
                                    return a_.activation(lnmv2[:, QT:2 * QT], lnmv[:, QT:2 * QT], AF.Copy, scale=1.0 / DM)
                                ACTG(_lnl)

                                def _lnv(v, C):
                                    C(v.tensor_tensor(lnr[:, 0:QT], lnmv2[:, 0:QT], lnmv2[:, 0:QT], ALU.mult))
                                    return v.tensor_tensor(lnmv2[:, QT:2 * QT], lnmv2[:, QT:2 * QT],
                                                           lnr[:, 0:QT], ALU.subtract)
                                VEC(_lnv)

                                def _lnr(a_, C):
                                    C(a_.activation(lnr[:, 0:QT], lnmv2[:, QT:2 * QT], AF.Ln,
                                                    scale=1.0 + EPS, bias=EPS * EPS))
                                    return a_.activation(lnr[:, 0:QT], lnr[:, 0:QT], AF.Exp, scale=-0.5)
                                ACTG(_lnr)

                                def _ap2(v):
                                    last = None
                                    for a in range(QT):
                                        last = v.tensor_scalar(
                                            xn2t[:, a * DM:(a + 1) * DM], xtok[:, a * DM:(a + 1) * DM],
                                            lnmv2[:, a:a + 1], lnr[:, a:a + 1],
                                            ALU.subtract, ALU.mult)
                                    return last
                                VEC(_ap2)

                                for kb in range(NKD):
                                    for a4 in range(0, QT, 4):
                                        na = min(4, QT - a4)

                                        def _tr(t, kb=kb, a4=a4, na=na):
                                            last = None
                                            for i in range(na):
                                                a = a4 + i
                                                last = t.transpose(
                                                    pTb[:, i * 128:(i + 1) * 128],
                                                    xn2t[:, a * DM + kb * 128:a * DM + (kb + 1) * 128],
                                                    ident_b[:])
                                            return last
                                        PE(_tr)

                                        def _ev(v, kb=kb, q=q, a4=a4, na=na):
                                            col = q * QT * 128 + a4 * 128
                                            return v.tensor_scalar(
                                                xn2b[:, kb * CH + col:kb * CH + col + na * 128],
                                                pTb[:, 0:na * 128], cLG(kb), cLB(kb),
                                                ALU.mult, ALU.add)
                                        VEC(_ev)

                                if own and d == 0:
                                    def _lnr1(a_, C):
                                        C(a_.activation(lnr[:, 0:QT], lnmv2[:, QT:2 * QT], AF.Ln, bias=EPS))
                                        return a_.activation(lnr[:, 0:QT], lnr[:, 0:QT], AF.Exp, scale=-0.5)
                                    ACTG(_lnr1)

                                    def _apx(v):
                                        last = None
                                        for a in range(QT):
                                            last = v.tensor_scalar(
                                                xtok[:, a * DM:(a + 1) * DM], xtok[:, a * DM:(a + 1) * DM],
                                                lnmv2[:, a:a + 1], lnr[:, a:a + 1],
                                                ALU.subtract, ALU.mult)
                                        return last
                                    VEC(_apx)

                                    GP(lambda g, q=q: g.dma_start(
                                        out=xn_h[q * QT * 128:(q + 1) * QT * 128, :].rearrange("(a p) m -> p a m", p=128),
                                        in_=xtok[:].rearrange("p (a m) -> p a m", a=QT)))
                                    VEC(lambda v: v.tensor_copy(xn2t[:], xtok[:]))
                                    for kb in range(NKD):
                                        for a4 in range(0, QT, 4):
                                            na = min(4, QT - a4)

                                            def _trx(t, kb=kb, a4=a4, na=na):
                                                last = None
                                                for i in range(na):
                                                    a = a4 + i
                                                    last = t.transpose(
                                                        pTb[:, i * 128:(i + 1) * 128],
                                                        xn2t[:, a * DM + kb * 128:a * DM + (kb + 1) * 128],
                                                        ident_b[:])
                                                return last
                                            PE(_trx)
                                            VEC(lambda v, na=na: v.tensor_copy(
                                                xc_j[:, 0:na * 128], pTb[:, 0:na * 128]))
                                            GP(lambda g, kb=kb, q=q, a4=a4, na=na: g.dma_start(
                                                out=xnfm_h[kb * 128:(kb + 1) * 128,
                                                           q * QT * 128 + a4 * 128:q * QT * 128 + (a4 + na) * 128],
                                                in_=xc_j[:, 0:na * 128]))

                            if dbg and own:
                                GP(lambda g: g.dma_start(
                                    out=xn2d.rearrange("(k p) t -> p k t", p=128),
                                    in_=xn2b[:].rearrange("p (k t) -> p k t", k=NKD)))

                            for j in range(NJ):
                                SY(lambda sy, j=j, d=d: sy.dma_start(
                                    out=wrot[:].rearrange("p (k n) -> p k n", k=NKD),
                                    in_=wxi_h[d][:, j * 128:(j + 1) * 128].rearrange("(k p) n -> p k n", p=128)))

                                for mi in range(NMM):
                                    def _mmx(t, mi=mi):
                                        last = None
                                        for kb in range(NKD):
                                            last = t.matmul(
                                                pA[:, 0:MM], wrot[:, kb * 128:(kb + 1) * 128],
                                                xn2b[:, kb * CH + mi * MM:kb * CH + (mi + 1) * MM],
                                                start=(kb == 0), stop=(kb == NKD - 1))
                                        return last
                                    PE(_mmx)
                                    VEC(lambda v, mi=mi: v.tensor_copy(
                                        xi_j[:, 3 + mi * MM:3 + (mi + 1) * MM], pA[:, 0:MM]))

                                def _conv(v, C, j=j, c=c, d=d):
                                    C(v.tensor_scalar(xi_j[:, 0:3], tails[:, j * 3:(j + 1) * 3], cGT(d, c), None, ALU.mult))
                                    C(v.tensor_scalar(accv[:], xi_j[:, 0:CH], cCW(d, j, 0), None, ALU.mult))
                                    for tap in range(1, 4):
                                        C(v.scalar_tensor_tensor(accv[:], xi_j[:, tap:tap + CH], cCW(d, j, tap),
                                                                 accv[:], ALU.mult, ALU.add))
                                    return v.tensor_copy(tails[:, j * 3:(j + 1) * 3], xi_j[:, CH:CH + 3])
                                VEC(_conv)
                                if dbg and own:
                                    GP(lambda g, j=j: g.dma_start(
                                        out=xid[j * 128:(j + 1) * 128, :], in_=xi_j[:, 3:CH + 3]))
                                ACTG(lambda a_, j=j, d=d: a_.activation(xc_j[:], accv[:], AF.Silu, bias=cCB(d, j)))
                                GP(lambda g, j=j: g.dma_start(out=xc_h[j * 128:(j + 1) * 128, :], in_=xc_j[:]))

                            SY(lambda sy, t0=t0, d=d: sy.dma_start(out=pensf[:], in_=pen[d:d + 1, t0:t0 + CH]))
                            VEC(lambda v: v.tensor_copy(dtaug[64:65, :], pensf[:]))
                            M5 = min(512, CH)
                            for mi in range(CH // M5):
                                for kb in range(NJ):
                                    SY(lambda sy, mi=mi, kb=kb: sy.dma_start(
                                        out=xcmv[:, 0:M5], in_=xc_h[kb * 128:(kb + 1) * 128, mi * M5:(mi + 1) * M5]))

                                    PE(lambda t, mi=mi, kb=kb, d=d: t.matmul(
                                        pD[0:96, 0:M5], wxp_sb[:, (d * NJ + kb) * 96:(d * NJ + kb + 1) * 96],
                                        xcmv[:, 0:M5], start=(kb == 0), stop=(kb == NJ - 1)))

                                def _evp(v, C, mi=mi):
                                    C(v.tensor_copy(dtaug[0:64, mi * M5:(mi + 1) * M5], pD[0:64, 0:M5]))
                                    return v.tensor_copy(bcsb[:, mi * M5:(mi + 1) * M5], pD[64:96, 0:M5])
                                VEC(_evp)

                            GP(lambda g: g.dma_start(out=bc_h[:], in_=bcsb[:]))

                            for sc in range(2):
                                s0 = sc * SC

                                def _ldbr(sy, s0=s0, own=own):
                                    for s in range(S):
                                        b_("ld", sy.dma_start(
                                            out=brep[:, s * SC:(s + 1) * SC],
                                            in_=bc_h[s, s0:s0 + SC].partition_broadcast(128)))
                                    if own:
                                        for s in range(S):
                                            b_("ld", sy.dma_start(
                                                out=crep[:, s * SC:(s + 1) * SC],
                                                in_=bc_h[16 + s, s0:s0 + SC].partition_broadcast(128)))
                                SY(_ldbr)

                                for j in range(NJ):
                                    dl = dlj[:, (j % 2) * SC:(j % 2 + 1) * SC]
                                    SY(lambda sy, j=j, s0=s0: sy.dma_start(
                                        out=xc_j[:, 0:SC], in_=xc_h[j * 128:(j + 1) * 128, s0:s0 + SC]))

                                    def _mdt(t, j=j, s0=s0, d=d):
                                        last = None
                                        ms = min(MM, SC)
                                        for h2 in range(0, SC, ms):
                                            last = t.matmul(
                                                pB[:, h2:h2 + ms],
                                                wdt_sb[:, d * DI + j * 128:d * DI + (j + 1) * 128],
                                                dtaug[:, s0 + h2:s0 + h2 + ms], start=True, stop=True)
                                        return last
                                    PE(_mdt)

                                    def _sp(a_, C, j=j, d=d, dl=dl):
                                        C(a_.activation(dl, pB[:, 0:SC], AF.Exp, bias=cDTB(d, j)))
                                        return a_.activation(dl, dl, AF.Ln, bias=1.0)
                                    ACTG(_sp)

                                    VEC(lambda v, dl=dl: v.tensor_tensor(duj[:], dl, xc_j[:, 0:SC], ALU.mult))

                                    act_hist = {}
                                    bcnt = {}
                                    scnt = {}
                                    sp_cnt = ctr["act"]

                                    VEC(lambda v: v.tensor_tensor(bj[:, 0:SC], duj[:], brep[:, 0:SC], ALU.mult))
                                    bcnt[0] = ctr["vec"]

                                    for s in range(S):
                                        @block.scalar
                                        def _(a_, j=j, s=s, d=d, dl=dl):
                                            if s == 0:
                                                a_.wait_ge(sACT, sp_cnt)
                                            if s >= 2:
                                                a_.wait_ge(sVEC, scnt[s - 2])
                                            inst = a_.activation(dAj[:, (s % 2) * SC:(s % 2 + 1) * SC],
                                                                 dl, AF.Exp, scale=cA(d, j, s))
                                            ctr["act"] += 1
                                            inst.then_inc(sACT, 1)
                                            act_hist[s] = ctr["act"]

                                        @block.vector
                                        def _(v, j=j, s=s, own=own):
                                            v.wait_ge(sACT, act_hist[s])
                                            v.wait_ge(sVEC, bcnt[s])
                                            hjs = hj[:, (s % 2) * SC:(s % 2 + 1) * SC]
                                            i1 = v.tensor_tensor_scan(
                                                hjs, dAj[:, (s % 2) * SC:(s % 2 + 1) * SC],
                                                bj[:, (s % 2) * SC:(s % 2 + 1) * SC],
                                                states[:, j * S + s:j * S + s + 1], ALU.mult, ALU.add)
                                            ctr["vec"] += 1
                                            i1.then_inc(sVEC, 1)
                                            scnt[s] = ctr["vec"]
                                            if s + 1 < S:
                                                i2 = v.tensor_tensor(
                                                    bj[:, ((s + 1) % 2) * SC:((s + 1) % 2 + 1) * SC],
                                                    duj[:], brep[:, (s + 1) * SC:(s + 2) * SC], ALU.mult)
                                                ctr["vec"] += 1
                                                i2.then_inc(sVEC, 1)
                                                bcnt[s + 1] = ctr["vec"]
                                            v.wait_ge(sVEC, scnt[s])
                                            i3 = v.tensor_copy(states[:, j * S + s:j * S + s + 1], hjs[:, SC - 1:SC])
                                            ctr["vec"] += 1
                                            i3.then_inc(sVEC, 1)
                                            if own:
                                                i4 = v.tensor_tensor(tmpj[:], hjs, crep[:, s * SC:(s + 1) * SC], ALU.mult)
                                                ctr["vec"] += 1
                                                i4.then_inc(sVEC, 1)
                                                v.wait_ge(sVEC, ctr["vec"])
                                                if s == 0:
                                                    i5 = v.tensor_copy(yj[:], tmpj[:])
                                                else:
                                                    i5 = v.tensor_tensor(yj[:], yj[:], tmpj[:], ALU.add)
                                                ctr["vec"] += 1
                                                i5.then_inc(sVEC, 1)

                                    if own:
                                        SY(lambda sy, j=j, d=d: sy.dma_start(
                                            out=wzrot[:].rearrange("p (k n) -> p k n", k=NKD),
                                            in_=wz_h[d][:, j * 128:(j + 1) * 128].rearrange("(k p) n -> p k n", p=128)))

                                        def _mz(t, s0=s0):
                                            last = None
                                            ms = min(MM, SC)
                                            for h2 in range(0, SC, ms):
                                                for kb in range(NKD):
                                                    last = t.matmul(pC[:, h2:h2 + ms], wzrot[:, kb * 128:(kb + 1) * 128],
                                                                    xn2b[:, kb * CH + s0 + h2:kb * CH + s0 + h2 + ms],
                                                                    start=(kb == 0), stop=(kb == NKD - 1))
                                            return last
                                        PE(_mz)
                                        ACTG(lambda a_: a_.activation(gj[:], pC[:, 0:SC], AF.Silu))

                                        def _yf(v, C, j=j, d=d):
                                            C(v.scalar_tensor_tensor(yj[:], xc_j[:, 0:SC], cD(d, j), yj[:],
                                                                     ALU.mult, ALU.add))
                                            if d == 0:
                                                return v.tensor_tensor(yfj[:], yj[:], gj[:], ALU.mult)
                                            return v.tensor_tensor(yfj[:], yj[:, ::-1], gj[:, ::-1], ALU.mult)
                                        VEC(_yf)

                                        def _styf(g, j=j, s0=s0, d=d):
                                            if d == 0:
                                                return g.dma_start(out=yfin_h[d, j * 128:(j + 1) * 128, s0:s0 + SC],
                                                                   in_=yfj[:])
                                            r0 = CH - (s0 + SC)
                                            return g.dma_start(out=yfin_h[d, j * 128:(j + 1) * 128, r0:r0 + SC],
                                                               in_=yfj[:])
                                        GP(_styf)

                            if skippable:
                                deltas = {k: ctr[k] - snap[k] for k in ctr}
                                for _k, _deco in _edecos:
                                    @_deco
                                    def _(eng, _k=_k, c=c, d=d, snap=snap):
                                        guards[_k].__exit__(None, None, None)
                                        with eng.Else():
                                            if deltas[_k] > 0:
                                                eng.wait_ge(SEM[_k], snap[_k])
                                                rem = deltas[_k]
                                                pi = 0
                                                if _k == "pe":
                                                    eng.wait_ge(SEM["act"], snap["act"])
                                                while rem > 0:
                                                    piece = min(240, rem)
                                                    if _k in ("ld", "st"):
                                                        col = dumcol[0]
                                                        dumcol[0] += 1
                                                        assert col < 512
                                                        eng.dma_start(
                                                            out=cfg_sb[0:1, col:col + 1],
                                                            in_=cfg_sb[0:1, 0:1]).then_inc(SEM[_k], piece)
                                                    elif _k == "pe":
                                                        eng.matmul(pD[0:8, 0:8], ident_b[0:8, 0:8],
                                                                   ident_b[0:8, 0:8], start=True,
                                                                   stop=True).then_inc(SEM[_k], piece)
                                                    elif _k == "act":
                                                        assert pi < 32
                                                        eng.activation(scr_sb[0:1, pi:pi + 1],
                                                                       consts[0:1, oK:oK + 1],
                                                                       AF.Copy).then_inc(SEM[_k], piece)
                                                    else:
                                                        assert pi < 32
                                                        eng.memset(scr_sb[0:1, 32 + pi:33 + pi],
                                                                   0.0).then_inc(SEM[_k], piece)
                                                    pi += 1
                                                    rem -= piece
                                for _e in waited:
                                    for _k in waited[_e]:
                                        waited[_e][_k] = 0

                MC = min(512, CH)
                NMC = CH // MC
                with contextlib.ExitStack() as ESP:
                    wsA = ESP.enter_context(nc.sbuf_tensor("wsA", [128, 2 * NKD * 128], BF16))
                    wsB = ESP.enter_context(nc.sbuf_tensor("wsB", [128, 2 * NKD * 128], BF16))
                    oaccf = ESP.enter_context(nc.sbuf_tensor("oaccf", [128, NKD * CH], F32))
                    with contextlib.ExitStack() as ESG:
                        xfm = ESG.enter_context(nc.sbuf_tensor("xfm", [128, NKD * MC], BF16))
                        scf = ESG.enter_context(nc.sbuf_tensor("scf", [16, CH], F32))
                        sct = ESG.enter_context(nc.sbuf_tensor("sct", [128, 16], F32))
                        gsc = ESG.enter_context(nc.sbuf_tensor("gsc", [128, 16], F32))
                        pbf = ESG.enter_context(nc.sbuf_tensor("pbf", [16, CH], BF16))
                        for mi in range(NMC):
                            SY(lambda sy, mi=mi: sy.dma_start(
                                out=xfm[:].rearrange("p (k n) -> p k n", k=NKD),
                                in_=xnfm_h[:, mi * MC:(mi + 1) * MC].rearrange("(k p) n -> p k n", p=128)))

                            def _msc(t, mi=mi):
                                last = None
                                for kb in range(NKD):
                                    last = t.matmul(pA[0:E, 0:MC], wg_sb[:, kb * E:(kb + 1) * E],
                                                    xfm[:, kb * MC:(kb + 1) * MC],
                                                    start=(kb == 0), stop=(kb == NKD - 1))
                                return last
                            PE(_msc)
                            VEC(lambda v, mi=mi: v.tensor_copy(scf[0:E, mi * MC:(mi + 1) * MC], pA[0:E, 0:MC]))

                        for a in range(NT):
                            PE(lambda t, a=a: t.transpose(pD[:, 0:8], scf[0:E, a * 128:(a + 1) * 128], ident_f[0:8, 0:8]))
                            ACTG(lambda a_: a_.activation(sct[:, 0:8], pD[:, 0:8], AF.Copy))

                            VEC(lambda v: v.reduce_max(sct[:, 8:9], sct[:, 0:8], AX.X, negate=True))
                            ACTG(lambda a_: a_.activation(sct[:, 13:14], sct[:, 8:9], AF.Copy))

                            def _g8(v, C):
                                s8 = sct[:, 0:8]
                                C(v.tensor_scalar(sct[:, 9:10], sct[:, 13:14], -1.0, None, ALU.mult))
                                C(v.tensor_scalar(gsc[:, 0:8], s8, sct[:, 9:10], None, ALU.is_ge))
                                C(v.scalar_tensor_tensor(gsc[:, 8:16], gsc[:, 0:8], -1e30, s8, ALU.mult, ALU.add))
                                return v.reduce_max(sct[:, 10:11], gsc[:, 8:16], AX.X)
                            VEC(_g8)

                            def _gl(a_):
                                a_.activation(sct[:, 14:15], sct[:, 10:11], AF.Copy)
                                return a_.activation(gsc[:, 8:16], sct[:, 0:8], AF.Exp, bias=sct[:, 13:14])
                            ACTG(_gl)

                            def _g9(v, C):
                                C(v.tensor_scalar(gsc[:, 0:8], sct[:, 0:8], sct[:, 14:15], None, ALU.is_ge))
                                C(v.tensor_tensor(gsc[:, 8:16], gsc[:, 8:16], gsc[:, 0:8], ALU.mult))
                                return v.reduce_sum(sct[:, 11:12], gsc[:, 8:16], AX.X)
                            VEC(_g9)
                            ACTG(lambda a_: a_.activation(sct[:, 15:16], sct[:, 11:12], AF.Copy))
                            VEC(lambda v: v.reciprocal(sct[:, 12:13], sct[:, 15:16]))
                            ACTG(lambda a_: a_.activation(sct[:, 9:10], sct[:, 12:13], AF.Copy))
                            VEC(lambda v: v.tensor_scalar(gsc[:, 0:8], gsc[:, 8:16], sct[:, 9:10], None, ALU.mult))
                            PE(lambda t, a=a: t.transpose(pD[0:8, 128:256], gsc[:, 0:8], ident_f[:]))
                            VEC(lambda v, a=a: v.tensor_copy(pbf[0:E, a * 128:(a + 1) * 128], pD[0:8, 128:256]))
                        GP(lambda g: g.dma_start(out=p_h[:], in_=pbf[0:E, :]))

                    with contextlib.ExitStack() as ESFF:
                        featf = ESFF.enter_context(nc.sbuf_tensor("featf", [128, NKD * CH], BF16))
                        featb = ESFF.enter_context(nc.sbuf_tensor("featb", [128, NKD * CH], BF16))
                        yfm = ESFF.enter_context(nc.sbuf_tensor("yfm", [128, NJ * MC], BF16))
                        wci = [0]
                        for d in range(2):
                            feat = featf if d == 0 else featb
                            for mi in range(NMC):
                                SY(lambda sy, d=d, mi=mi: sy.dma_start(
                                    out=yfm[:].rearrange("p (k n) -> p k n", k=NJ),
                                    in_=yfin_h[d][:, mi * MC:(mi + 1) * MC].rearrange("(k p) n -> p k n", p=128)))
                                for n in range(NKD):
                                    wb = wsA if wci[0] % 2 == 0 else wsB
                                    wci[0] += 1
                                    SY(lambda sy, d=d, n=n, wb=wb: sy.dma_start(
                                        out=wb[:, 0:NKD * 128].rearrange("p (k n) -> p k n", k=NKD),
                                        in_=wout_h[d][0:DI // 2, n * 128:(n + 1) * 128].rearrange("(k p) n -> p k n", p=128)))
                                    GP(lambda g, d=d, n=n, wb=wb: g.dma_start(
                                        out=wb[:, NKD * 128:2 * NKD * 128].rearrange("p (k n) -> p k n", k=NKD),
                                        in_=wout_h[d][DI // 2:DI, n * 128:(n + 1) * 128].rearrange("(k p) n -> p k n", p=128)))

                                    def _mf(t, n=n, wb=wb):
                                        last = None
                                        for k in range(NJ):
                                            last = t.matmul(pA[:, 0:MC], wb[:, k * 128:(k + 1) * 128],
                                                            yfm[:, k * MC:(k + 1) * MC],
                                                            start=(k == 0), stop=(k == NJ - 1))
                                        return last
                                    PE(_mf)
                                    VEC(lambda v, n=n, mi=mi, feat=feat: v.tensor_copy(
                                        feat[:, n * CH + mi * MC:n * CH + (mi + 1) * MC], pA[:, 0:MC]))

                        for nn in range(NKD):
                            wb = wsA if wci[0] % 2 == 0 else wsB
                            wci[0] += 1
                            SY(lambda sy, nn=nn, wb=wb: sy.dma_start(
                                out=wb[:, 0:NKD * 128].rearrange("p (k n) -> p k n", k=NKD),
                                in_=wfus_h[0:DM, nn * 128:(nn + 1) * 128].rearrange("(k p) n -> p k n", p=128)))
                            GP(lambda g, nn=nn, wb=wb: g.dma_start(
                                out=wb[:, NKD * 128:2 * NKD * 128].rearrange("p (k n) -> p k n", k=NKD),
                                in_=wfus_h[DM:2 * DM, nn * 128:(nn + 1) * 128].rearrange("(k p) n -> p k n", p=128)))
                            for mi in range(NMC):
                                def _mfus(t, nn=nn, mi=mi, wb=wb):
                                    for k in range(NKD):
                                        t.matmul(pB[:, 0:MC], wb[:, k * 128:(k + 1) * 128],
                                                 featf[:, k * CH + mi * MC:k * CH + (mi + 1) * MC],
                                                 start=(k == 0), stop=False)
                                    last = None
                                    for k in range(NKD):
                                        last = t.matmul(pB[:, 0:MC], wb[:, (NKD + k) * 128:(NKD + k + 1) * 128],
                                                        featb[:, k * CH + mi * MC:k * CH + (mi + 1) * MC],
                                                        start=False, stop=(k == NKD - 1))
                                    return last
                                PE(_mfus)
                                VEC(lambda v, nn=nn, mi=mi: v.tensor_scalar(
                                    oaccf[:, nn * CH + mi * MC:nn * CH + (mi + 1) * MC],
                                    pB[:, 0:MC], cFB(nn), None, ALU.add))

                    MP = min(2 * MC, CH)
                    NMP = CH // MP
                    HH = MP // MC
                    with contextlib.ExitStack() as ESE:
                        ws3 = ESE.enter_context(nc.sbuf_tensor("ws3", [128, 4 * NJ * 128], BF16))
                        xfm_p = ESE.enter_context(nc.sbuf_tensor("xfm_p", [128, NKD * MP], BF16))
                        prep_ep = ESE.enter_context(nc.sbuf_tensor("prep_ep", [128, MP], BF16))
                        hidden_p = ESE.enter_context(nc.sbuf_tensor("hidden_p", [128, NJ * MP], BF16))
                        hsil2 = ESE.enter_context(nc.sbuf_tensor("hsil2", [128, MP], F32))
                        otok = ESE.enter_context(nc.sbuf_tensor("otok", [128, DM], F32))
                        xrl = ESE.enter_context(nc.sbuf_tensor("xrl", [128, DM], F32))
                        xnrl = ESE.enter_context(nc.sbuf_tensor("xnrl", [128, DM], F32))
                        pe_mm = {}
                        pe_w3s = {}
                        act_sil = {}
                        vec_hid = {}
                        ii = [0]
                        pe_start = ctr["pe"]
                        act_start = ctr["act"]
                        vec_start = ctr["vec"]
                        pe_w3_prev = [pe_start]
                        last_sy = [ctr["ld"]]
                        last_gp = [ctr["st"]]
                        last_ad = [ctr["ad"]]
                        for mp in range(NMP):
                            @block.sync
                            def _(sy, mp=mp, pin=last_sy[0]):
                                sy.wait_ge(sLD, pin)
                                sy.wait_ge(sPE, ctr["pe"])
                                b_("ld", sy.dma_start(
                                    out=xfm_p[:].rearrange("p (k n) -> p k n", k=NKD),
                                    in_=xnfm_h[:, mp * MP:(mp + 1) * MP].rearrange("(k p) n -> p k n", p=128)))
                            last_sy[0] = ctr["ld"]
                            for e in range(E):
                                @block.sync
                                def _(sy, e=e, mp=mp, pin=last_sy[0]):
                                    sy.wait_ge(sLD, pin)
                                    sy.wait_ge(sVEC, ctr["vec"])
                                    b_("ld", sy.dma_start(
                                        out=prep_ep[:],
                                        in_=p_h[e, mp * MP:(mp + 1) * MP].partition_broadcast(128)))
                                last_sy[0] = ctr["ld"]
                                ld_prep = ctr["ld"]
                                ld_w12 = {}
                                ld_w3 = {}
                                for nh in range(NJ):
                                    wb = wsA if ii[0] % 2 == 0 else wsB

                                    @block.sync
                                    def _(sy, e=e, nh=nh, wb=wb, i0=ii[0], pin=last_sy[0]):
                                        sy.wait_ge(sLD, pin)
                                        sy.wait_ge(sPE, pe_mm.get(i0 - 2, pe_start))
                                        b_("ld", sy.dma_start(
                                            out=wb[:, 0:NKD * 128].rearrange("p (k n) -> p k n", k=NKD),
                                            in_=w12_h[e][:, nh * 128:(nh + 1) * 128].rearrange("(k p) n -> p k n", p=128)))
                                    last_sy[0] = ctr["ld"]

                                    @block.gpsimd
                                    def _(g, e=e, nh=nh, wb=wb, i0=ii[0], pin=last_gp[0]):
                                        g.wait_ge(sST, pin)
                                        g.wait_ge(sPE, pe_mm.get(i0 - 2, pe_start))
                                        b_("st", g.dma_start(
                                            out=wb[:, NKD * 128:2 * NKD * 128].rearrange("p (k n) -> p k n", k=NKD),
                                            in_=w12_h[e][:, HF + nh * 128:HF + (nh + 1) * 128].rearrange("(k p) n -> p k n", p=128)))
                                    last_gp[0] = ctr["st"]
                                    ld_w12[ii[0]] = (ctr["ld"], ctr["st"])

                                    if nh < 4:
                                        @block.scalar
                                        def _(a, e=e, nn=nh, pin=last_ad[0]):
                                            a.wait_ge(sAD, pin)
                                            a.wait_ge(sPE, pe_w3_prev[0])
                                            b_("ad", a.dma_start(
                                                out=ws3[:, (nn % 4) * NJ * 128:(nn % 4 + 1) * NJ * 128].rearrange(
                                                    "p (k n) -> p k n", k=NJ),
                                                in_=w3_h[e][:, nn * 128:(nn + 1) * 128].rearrange("(k p) n -> p k n", p=128)))
                                        last_ad[0] = ctr["ad"]
                                        ld_w3[nh] = ctr["ad"]

                                    @block.tensor
                                    def _(t, wb=wb, i0=ii[0]):
                                        lw, sw = ld_w12[i0]
                                        t.wait_ge(sLD, lw)
                                        t.wait_ge(sST, sw)
                                        t.wait_ge(sACT, act_sil.get(i0 - 1, act_start))
                                        t.wait_ge(sVEC, vec_hid.get(i0 - 1, vec_start))
                                        for hh in range(HH):
                                            for kb in range(NKD):
                                                t.matmul(pA[:, hh * MC:(hh + 1) * MC],
                                                         wb[:, kb * 128:(kb + 1) * 128],
                                                         xfm_p[:, kb * MP + hh * MC:kb * MP + (hh + 1) * MC],
                                                         start=(kb == 0), stop=(kb == NKD - 1))
                                        last = None
                                        for hh in range(HH):
                                            for kb in range(NKD):
                                                last = t.matmul(pB[:, hh * MC:(hh + 1) * MC],
                                                                wb[:, (NKD + kb) * 128:(NKD + kb + 1) * 128],
                                                                xfm_p[:, kb * MP + hh * MC:kb * MP + (hh + 1) * MC],
                                                                start=(kb == 0), stop=(kb == NKD - 1))
                                        ctr["pe"] += 1
                                        last.then_inc(sPE, 1)
                                    pe_mm[ii[0]] = ctr["pe"]

                                    @block.scalar
                                    def _(a, i0=ii[0]):
                                        a.wait_ge(sPE, pe_mm[i0])
                                        a.wait_ge(sVEC, vec_hid.get(i0 - 1, vec_start))
                                        inst = a.activation(hsil2[:], pA[:, 0:MP], AF.Silu)
                                        ctr["act"] += 1
                                        inst.then_inc(sACT, 1)
                                    act_sil[ii[0]] = ctr["act"]

                                    @block.vector
                                    def _(v, nh=nh, i0=ii[0], ld_prep=ld_prep, vself=ctr["vec"]):
                                        v.wait_ge(sACT, act_sil[i0])
                                        v.wait_ge(sLD, ld_prep)
                                        v.wait_ge(sVEC, vself)
                                        hslice = hidden_p[:, nh * MP:(nh + 1) * MP]
                                        i1 = v.tensor_tensor(hslice, hsil2[:], pB[:, 0:MP], ALU.mult)
                                        ctr["vec"] += 1
                                        i1.then_inc(sVEC, 1)
                                        v.wait_ge(sVEC, ctr["vec"])
                                        i2 = v.tensor_tensor(hslice, hslice, prep_ep[:], ALU.mult)
                                        ctr["vec"] += 1
                                        i2.then_inc(sVEC, 1)
                                    vec_hid[ii[0]] = ctr["vec"]
                                    ii[0] += 1

                                vec_hid_last = ctr["vec"]
                                vec_oacc = {}
                                for nn in range(NKD):
                                    if nn >= 4:
                                        @block.scalar
                                        def _(a, e=e, nn=nn, pin=last_ad[0]):
                                            a.wait_ge(sAD, pin)
                                            a.wait_ge(sPE, pe_w3s[nn - 4])
                                            b_("ad", a.dma_start(
                                                out=ws3[:, (nn % 4) * NJ * 128:(nn % 4 + 1) * NJ * 128].rearrange(
                                                    "p (k n) -> p k n", k=NJ),
                                                in_=w3_h[e][:, nn * 128:(nn + 1) * 128].rearrange("(k p) n -> p k n", p=128)))
                                        last_ad[0] = ctr["ad"]
                                        ld_w3[nn] = ctr["ad"]

                                    @block.tensor
                                    def _(t, nn=nn, vh=vec_hid_last):
                                        t.wait_ge(sAD, ld_w3[nn])
                                        t.wait_ge(sVEC, vh)
                                        if nn >= 1:
                                            t.wait_ge(sVEC, vec_oacc[nn - 1])
                                        last = None
                                        for hh in range(HH):
                                            for k in range(NJ):
                                                last = t.matmul(pC[:, hh * MC:(hh + 1) * MC],
                                                                ws3[:, (nn % 4) * NJ * 128 + k * 128:
                                                                    (nn % 4) * NJ * 128 + (k + 1) * 128],
                                                                hidden_p[:, k * MP + hh * MC:k * MP + (hh + 1) * MC],
                                                                start=(k == 0), stop=(k == NJ - 1))
                                        ctr["pe"] += 1
                                        last.then_inc(sPE, 1)
                                    pe_w3s[nn] = ctr["pe"]

                                    @block.vector
                                    def _(v, nn=nn, mp=mp, vself=ctr["vec"]):
                                        v.wait_ge(sPE, pe_w3s[nn])
                                        v.wait_ge(sVEC, vself)
                                        col = nn * CH + mp * MP
                                        i1 = v.tensor_tensor(oaccf[:, col:col + MP],
                                                             oaccf[:, col:col + MP], pC[:, 0:MP], ALU.add)
                                        ctr["vec"] += 1
                                        i1.then_inc(sVEC, 1)
                                    vec_oacc[nn] = ctr["vec"]
                                pe_w3_prev[0] = ctr["pe"]

                        for mi in range(NMC):
                            for tt in range(MC // 128):
                                def _mtr(t, mi=mi, tt=tt):
                                    last = None
                                    for nn in range(NKD):
                                        c0 = nn * CH + mi * MC + tt * 128
                                        last = t.transpose(pA[:, nn * 128:(nn + 1) * 128],
                                                           oaccf[:, c0:c0 + 128], ident_f[:])
                                    return last
                                PE(_mtr)
                                r0 = mi * MC + tt * 128

                                def _ldr(sy, r0=r0):
                                    b_("ld", sy.dma_start(out=xrl[:], in_=xf[T - CH + r0:T - CH + r0 + 128, :]))
                                    b_("ld", sy.dma_start(out=xnrl[:], in_=xn_h[r0:r0 + 128, :]))
                                SY(_ldr)

                                def _fin(v, C):
                                    C(v.tensor_tensor(otok[:], pA[:, 0:DM], xrl[:], ALU.add))
                                    return v.tensor_tensor(otok[:], otok[:], xnrl[:], ALU.add)
                                VEC(_fin)
                                GP(lambda g, r0=r0: g.dma_start(out=y_out[r0:r0 + 128, :], in_=otok[:]))

        @block.sync
        def _(sy):
            sy.wait_ge(sST, ctr["st"])

    return nc


_BUILD_CACHE = {}


def _get_nc(T, CH):
    key = (T, CH)
    if key not in _BUILD_CACHE:
        _BUILD_CACHE[key] = build(T, CH)
    return _BUILD_CACHE[key]


def make_inmaps(T, CH, x, weights):
    NCH = T // CH
    OWNT = T // 8
    f32 = np.float32

    def pp(vec, nblk):
        return np.ascontiguousarray(vec.reshape(nblk, 128).T.astype(f32))

    wd = {}
    wd["w_xi"] = np.ascontiguousarray(np.stack([
        weights["f_in_w"][:DI].T, weights["b_in_w"][:DI].T]).astype(f32))
    wd["w_z"] = np.ascontiguousarray(np.stack([
        weights["f_in_w"][DI:].T, weights["b_in_w"][DI:].T]).astype(f32))
    wd["w_xp"] = np.ascontiguousarray(np.stack([
        weights["f_xproj_w"].T, weights["b_xproj_w"].T]).astype(f32))
    wd["w_dt"] = np.ascontiguousarray(np.stack([
        weights["f_dt_w"].T, weights["b_dt_w"].T]).astype(f32))
    wd["dtb"] = np.ascontiguousarray(np.stack([
        pp(weights["f_dt_b"], NJ), pp(weights["b_dt_b"], NJ)]))
    cw = []
    for p in ("f", "b"):
        c = weights[p + "_conv_w"][:, 0, :]
        cw.append(np.ascontiguousarray(
            c.reshape(NJ, 128, 4).transpose(1, 0, 2).reshape(128, NJ * 4).astype(f32)))
    wd["convw"] = np.ascontiguousarray(np.stack(cw))
    wd["convb"] = np.ascontiguousarray(np.stack([
        pp(weights["f_conv_b"], NJ), pp(weights["b_conv_b"], NJ)]))
    al = []
    for p in ("f", "b"):
        a = weights[p + "_A_log"]
        al.append(np.ascontiguousarray(
            a.reshape(NJ, 128, S).transpose(1, 0, 2).reshape(128, NJ * S).astype(f32)))
    wd["alog"] = np.ascontiguousarray(np.stack(al))
    wd["dvec"] = np.ascontiguousarray(np.stack([
        pp(weights["f_D"], NJ), pp(weights["b_D"], NJ)]))
    wd["w_out"] = np.ascontiguousarray(np.stack([
        weights["f_out_w"].T, weights["b_out_w"].T]).astype(f32))
    wd["w_fus"] = np.ascontiguousarray(weights["fus_w"].T.astype(f32))
    wd["fusb"] = pp(weights["fus_b"], NKD)
    wd["lng"] = pp(weights["ln_g"], NKD)
    wd["lnb"] = pp(weights["ln_b"], NKD)
    wd["w12"] = np.ascontiguousarray(weights["ffn_w12"].transpose(0, 2, 1).astype(f32))
    wd["w3"] = np.ascontiguousarray(weights["ffn_w3"].transpose(0, 2, 1).astype(f32))
    wd["wgate"] = np.ascontiguousarray(weights["ffn_gate_w"].T.astype(f32))
    wd["ident"] = np.eye(128, dtype=f32)

    xr = x[::-1]
    in_maps = []
    for k in range(8):
        L = OWNT * (k + 1)
        padf = T - L
        padb = OWNT * k
        xfk = np.concatenate([np.zeros((padf, DM), f32), x[:L]], axis=0)
        xbk = np.concatenate([np.zeros((padb, DM), f32), xr[:T - padb]], axis=0)
        penk = np.zeros((2, T), f32)
        penk[0, :padf] = PEN
        penk[1, :padb] = PEN
        penk = penk.astype(ml_dtypes.bfloat16)
        gk = np.ones((128, 2 * NCH), f32)
        npadf = padf // CH
        npadb = padb // CH
        gk[:, 0:npadf + 1] = 0.0
        gk[:, NCH:NCH + npadb + 1] = 0.0
        m = dict(xf=np.ascontiguousarray(xfk), xb=np.ascontiguousarray(xbk),
                 pen=penk, gates=gk,
                 cfg=np.array([[npadf, npadb, 0, 0]], dtype=np.int32))
        m.update(wd)
        in_maps.append(m)
    return in_maps


def kernel(L0_attn_x, L0_attn_ln_g, L0_attn_ln_b, L0_attn_fus_w, L0_attn_fus_b,
           L0_attn_f_in_w, L0_attn_f_conv_w, L0_attn_f_conv_b, L0_attn_f_xproj_w,
           L0_attn_f_dt_w, L0_attn_f_dt_b, L0_attn_f_A_log, L0_attn_f_D, L0_attn_f_out_w,
           L0_attn_b_in_w, L0_attn_b_conv_w, L0_attn_b_conv_b, L0_attn_b_xproj_w,
           L0_attn_b_dt_w, L0_attn_b_dt_b, L0_attn_b_A_log, L0_attn_b_D, L0_attn_b_out_w,
           L0_ffn_w12, L0_ffn_w3, L0_ffn_gate_w):
    T, CH = 16384, 2048
    x = np.ascontiguousarray(np.asarray(L0_attn_x, dtype=np.float32)[0])
    weights = dict(
        ln_g=np.asarray(L0_attn_ln_g), ln_b=np.asarray(L0_attn_ln_b),
        fus_w=np.asarray(L0_attn_fus_w), fus_b=np.asarray(L0_attn_fus_b),
        f_in_w=np.asarray(L0_attn_f_in_w), f_conv_w=np.asarray(L0_attn_f_conv_w),
        f_conv_b=np.asarray(L0_attn_f_conv_b), f_xproj_w=np.asarray(L0_attn_f_xproj_w),
        f_dt_w=np.asarray(L0_attn_f_dt_w), f_dt_b=np.asarray(L0_attn_f_dt_b),
        f_A_log=np.asarray(L0_attn_f_A_log), f_D=np.asarray(L0_attn_f_D),
        f_out_w=np.asarray(L0_attn_f_out_w),
        b_in_w=np.asarray(L0_attn_b_in_w), b_conv_w=np.asarray(L0_attn_b_conv_w),
        b_conv_b=np.asarray(L0_attn_b_conv_b), b_xproj_w=np.asarray(L0_attn_b_xproj_w),
        b_dt_w=np.asarray(L0_attn_b_dt_w), b_dt_b=np.asarray(L0_attn_b_dt_b),
        b_A_log=np.asarray(L0_attn_b_A_log), b_D=np.asarray(L0_attn_b_D),
        b_out_w=np.asarray(L0_attn_b_out_w),
        ffn_w12=np.asarray(L0_ffn_w12), ffn_w3=np.asarray(L0_ffn_w3),
        ffn_gate_w=np.asarray(L0_ffn_gate_w),
    )
    nc = _get_nc(T, CH)
    in_maps = make_inmaps(T, CH, x, weights)
    res = run_bass_kernel_spmd(nc, in_maps, core_ids=list(range(8)))
    out = np.concatenate([res.results[k]["y_out"] for k in range(8)], axis=0)
    return out.reshape(1, T, DM).astype(np.float32)


# version 12 (final)
# speedup vs baseline: 2.5738x; 1.0258x over previous
import contextlib
import ml_dtypes
import numpy as np
import concourse.bass as bass
import concourse.mybir as mybir
from concourse.bass_utils import run_bass_kernel_spmd

F32 = mybir.dt.float32
BF16 = mybir.dt.bfloat16
AF = mybir.ActivationFunctionType
ALU = mybir.AluOpType
AX = mybir.AxisListType

DM, DI, S, RANK, E, HF = 1024, 2048, 16, 64, 8, 2048
NJ, NKD = DI // 128, DM // 128
EPS = 1e-5
PEN = -30000.0


def build(T, CH, dbg=False):
    NCH = T // CH
    NT = CH // 128
    QT = max(1, NT // 8)
    NQ = NT // QT
    SC = CH // 2
    MM = min(512, CH)
    NMM = CH // MM
    OWN = NCH - 1
    nc = bass.Bass("TRN2", target_bir_lowering=False)
    dram = nc.dram_tensor

    xf = dram("xf", [T, DM], F32, kind="ExternalInput").ap()
    xb = dram("xb", [T, DM], F32, kind="ExternalInput").ap()
    pen = dram("pen", [2, T], BF16, kind="ExternalInput").ap()
    gates = dram("gates", [128, 2 * NCH], F32, kind="ExternalInput").ap()
    w_xi = dram("w_xi", [2, DM, DI], F32, kind="ExternalInput").ap()
    w_z = dram("w_z", [2, DM, DI], F32, kind="ExternalInput").ap()
    w_xp = dram("w_xp", [2, DI, 96], F32, kind="ExternalInput").ap()
    w_dt = dram("w_dt", [2, RANK, DI], F32, kind="ExternalInput").ap()
    dtb = dram("dtb", [2, 128, NJ], F32, kind="ExternalInput").ap()
    convw = dram("convw", [2, 128, NJ * 4], F32, kind="ExternalInput").ap()
    convb = dram("convb", [2, 128, NJ], F32, kind="ExternalInput").ap()
    alog = dram("alog", [2, 128, NJ * S], F32, kind="ExternalInput").ap()
    dvec = dram("dvec", [2, 128, NJ], F32, kind="ExternalInput").ap()
    w_out = dram("w_out", [2, DI, DM], F32, kind="ExternalInput").ap()
    w_fus = dram("w_fus", [2 * DM, DM], F32, kind="ExternalInput").ap()
    fusb = dram("fusb", [128, NKD], F32, kind="ExternalInput").ap()
    lng = dram("lng", [128, NKD], F32, kind="ExternalInput").ap()
    lnb = dram("lnb", [128, NKD], F32, kind="ExternalInput").ap()
    w12 = dram("w12", [E, DM, 2 * HF], F32, kind="ExternalInput").ap()
    w3 = dram("w3", [E, HF, DM], F32, kind="ExternalInput").ap()
    wgate = dram("wgate", [DM, E], F32, kind="ExternalInput").ap()
    ident_in = dram("ident", [128, 128], F32, kind="ExternalInput").ap()
    cfg = dram("cfg", [1, 4], mybir.dt.int32, kind="ExternalInput").ap()

    y_out = dram("y_out", [CH, DM], F32, kind="ExternalOutput").ap()

    IK = "ExternalOutput" if dbg else "Internal"
    wxi_h = dram("wxi_h", [2, DM, DI], BF16, kind="Internal").ap()
    wz_h = dram("wz_h", [2, DM, DI], BF16, kind="Internal").ap()
    wout_h = dram("wout_h", [2, DI, DM], BF16, kind="Internal").ap()
    wfus_h = dram("wfus_h", [2 * DM, DM], BF16, kind="Internal").ap()
    w12_h = dram("w12_h", [E, DM, 2 * HF], BF16, kind="Internal").ap()
    w3_h = dram("w3_h", [E, HF, DM], BF16, kind="Internal").ap()
    bc_h = dram("bc_h", [32, CH], BF16, kind=IK).ap()
    xc_h = dram("xc_h", [DI, CH], BF16, kind=IK).ap()
    xn_h = dram("xn_h", [CH, DM], F32, kind=IK).ap()
    xnfm_h = dram("xnfm_h", [DM, CH], BF16, kind=IK).ap()
    yfin_h = dram("yfin_h", [2, DI, CH], BF16, kind=IK).ap()
    p_h = dram("p_h", [E, CH], BF16, kind=IK).ap()
    xid = dram("xid", [DI, CH], BF16, kind=IK).ap() if dbg else None
    xn2d = dram("xn2d", [DM, CH], BF16, kind=IK).ap() if dbg else None

    ctr = dict(ld=0, st=0, pe=0, act=0, vec=0, ad=0)

    with contextlib.ExitStack() as ES:
        sLD = ES.enter_context(nc.semaphore("sLD"))
        sST = ES.enter_context(nc.semaphore("sST"))
        sPE = ES.enter_context(nc.semaphore("sPE"))
        sACT = ES.enter_context(nc.semaphore("sACT"))
        sVEC = ES.enter_context(nc.semaphore("sVEC"))
        sAD = ES.enter_context(nc.semaphore("sAD"))
        block = ES.enter_context(nc.Block())
        SEM = dict(ld=sLD, st=sST, pe=sPE, act=sACT, vec=sVEC, ad=sAD)

        waited = {e: dict(ld=0, st=0, pe=0, act=0, vec=0, ad=0) for e in
                  ("ld", "st", "pe", "act", "vec")}

        def wall(eng, me):
            w_ = waited[me]
            for k, sem in SEM.items():
                if ctr[k] > w_[k]:
                    eng.wait_ge(sem, ctr[k])
                    w_[k] = ctr[k]

        def b_(key, inst):
            n = 16 if key in ("ld", "st", "ad") else 1
            ctr[key] += n
            inst.then_inc(SEM[key], n)
            return inst

        import inspect as _insp

        def _wants_chain(fn):
            try:
                return len(_insp.signature(fn).parameters) >= 2 and \
                    "C" in _insp.signature(fn).parameters
            except (TypeError, ValueError):
                return False

        def _mk_chain(eng, key):
            def C(inst):
                ctr[key] += 1
                inst.then_inc(SEM[key], 1)
                eng.wait_ge(SEM[key], ctr[key])
                return inst
            return C

        def _group(block_deco, key):
            def wrap(fn):
                @block_deco
                def _(eng):
                    wall(eng, key)
                    if _wants_chain(fn):
                        r = fn(eng, _mk_chain(eng, key))
                    else:
                        r = fn(eng)
                    if r is not None:
                        b_(key, r)
            return wrap

        def SY(fn):
            _group(block.sync, "ld")(fn)

        def GP(fn):
            _group(block.gpsimd, "st")(fn)

        def PE(fn):
            _group(block.tensor, "pe")(fn)

        def ACTG(fn):
            _group(block.scalar, "act")(fn)

        def VEC(fn):
            _group(block.vector, "vec")(fn)

        NCON = 2 * NJ * S + 14 * NJ + 3 * NKD + 2 * NCH + 16
        if True:
            ident_b = ES.enter_context(nc.sbuf_tensor("ident_b", [128, 128], BF16))
            ident_f = ES.enter_context(nc.sbuf_tensor("ident_f", [128, 128], F32))
            consts = ES.enter_context(nc.sbuf_tensor("consts", [128, NCON], F32))
            wdt_sb = ES.enter_context(nc.sbuf_tensor("wdt_sb", [65, 2 * DI], BF16))
            wxp_sb = ES.enter_context(nc.sbuf_tensor("wxp_sb", [128, 2 * NJ * 96], BF16))
            wg_sb = ES.enter_context(nc.sbuf_tensor("wg_sb", [128, NKD * E], BF16))
            oA = 0
            oDTB = oA + 2 * NJ * S
            oCW = oDTB + 2 * NJ
            oCB = oCW + 2 * NJ * 4
            oD = oCB + 2 * NJ
            oFB = oD + 2 * NJ
            oLG = oFB + NKD
            oLB = oLG + NKD
            oGT = oLB + NKD

            def CC(i):
                return consts[:, i:i + 1]

            cA = lambda d, j, s: CC(oA + d * NJ * S + j * S + s)
            cDTB = lambda d, j: CC(oDTB + d * NJ + j)
            cCW = lambda d, j, tap: CC(oCW + d * NJ * 4 + j * 4 + tap)
            cCB = lambda d, j: CC(oCB + d * NJ + j)
            cD = lambda d, j: CC(oD + d * NJ + j)
            cFB = lambda n: CC(oFB + n)
            cLG = lambda n: CC(oLG + n)
            cLB = lambda n: CC(oLB + n)
            cGT = lambda d, c: CC(oGT + d * NCH + c)

            SY(lambda sy: sy.dma_start(out=ident_f[:], in_=ident_in))
            SY(lambda sy: sy.dma_start(out=consts[:, oA:oA + 2 * NJ * S].rearrange("p (d j) -> p d j", d=2), in_=alog.rearrange("d p j -> p d j")))
            SY(lambda sy: sy.dma_start(out=consts[:, oDTB:oDTB + 2 * NJ].rearrange("p (d j) -> p d j", d=2), in_=dtb.rearrange("d p j -> p d j")))
            SY(lambda sy: sy.dma_start(out=consts[:, oCW:oCW + 8 * NJ].rearrange("p (d j) -> p d j", d=2), in_=convw.rearrange("d p j -> p d j")))
            SY(lambda sy: sy.dma_start(out=consts[:, oCB:oCB + 2 * NJ].rearrange("p (d j) -> p d j", d=2), in_=convb.rearrange("d p j -> p d j")))
            SY(lambda sy: sy.dma_start(out=consts[:, oD:oD + 2 * NJ].rearrange("p (d j) -> p d j", d=2), in_=dvec.rearrange("d p j -> p d j")))
            SY(lambda sy: sy.dma_start(out=consts[:, oFB:oFB + NKD], in_=fusb))
            SY(lambda sy: sy.dma_start(out=consts[:, oLG:oLG + NKD], in_=lng))
            SY(lambda sy: sy.dma_start(out=consts[:, oLB:oLB + NKD], in_=lnb))
            SY(lambda sy: sy.dma_start(out=consts[:, oGT:oGT + 2 * NCH], in_=gates))

            oK = oGT + 2 * NCH

            def _regs(v):
                v.memset(CC(oK + 0), 1.0)
                v.memset(CC(oK + 1), EPS)
                last = v.memset(CC(oK + 2), EPS * EPS)
                nc.const_aps.register(F32, 1.0, CC(oK + 0))
                nc.const_aps.register(F32, EPS, CC(oK + 1))
                nc.const_aps.register(F32, EPS * EPS, CC(oK + 2))
                return last
            VEC(_regs)

            ACTG(lambda a: a.activation(consts[:, oA:oA + 2 * NJ * S], consts[:, oA:oA + 2 * NJ * S], AF.Exp))

            def _c0(v):
                v.tensor_scalar_mul(consts[:, oA:oA + 2 * NJ * S], consts[:, oA:oA + 2 * NJ * S], -1.0)
                return v.tensor_copy(ident_b[:], ident_f[:])
            VEC(_c0)

            cfg_sb = ES.enter_context(nc.sbuf_tensor("cfg_sb", [1, 512], mybir.dt.int32))
            scr_sb = ES.enter_context(nc.sbuf_tensor("scr_sb", [1, 64], F32))
            dumcol = [8]
            SY(lambda sy: sy.dma_start(out=cfg_sb[0:1, 0:4], in_=cfg))
            skipreg = {}
            _edecos = [("ld", block.sync), ("st", block.gpsimd), ("pe", block.tensor),
                       ("act", block.scalar), ("vec", block.vector)]
            for _k, _deco in _edecos:
                @_deco
                def _(eng, _k=_k):
                    wall(eng, _k)
                    pid = eng.partition_id()
                    r0 = eng.to_reg(NCH - 1 - pid)
                    r1 = eng.to_reg(pid)
                    skipreg[_k] = (r0, r1)

            with contextlib.ExitStack() as ESC:
                NBUF, CSZ = 4, 4096
                cast_f = ESC.enter_context(nc.sbuf_tensor("cast_f", [128, NBUF * CSZ], F32))
                cast_b = ESC.enter_context(nc.sbuf_tensor("cast_b", [128, NBUF * CSZ], BF16))
                jobs = []
                for d in range(2):
                    jobs += [(w_xi[d], wxi_h[d], DM, DI), (w_z[d], wz_h[d], DM, DI),
                             (w_out[d], wout_h[d], DI, DM)]
                jobs.append((w_fus, wfus_h, 2 * DM, DM))
                for e in range(E):
                    jobs += [(w12[e], w12_h[e], DM, 2 * HF), (w3[e], w3_h[e], HF, DM)]
                tiles = []
                for src, dst, rows, cols in jobs:
                    csz = min(CSZ, cols)
                    for r0 in range(0, rows, 128):
                        for c0 in range(0, cols, csz):
                            tiles.append((src, dst, r0, c0, min(csz, cols - c0)))
                NTL = len(tiles)
                pre = dict(ctr)
                ld_base, vec_base, st_base = ctr["ld"], ctr["vec"], ctr["st"]

                def wall_pre(eng, me):
                    w_ = waited[me]
                    for k, sem in SEM.items():
                        if pre[k] > w_[k]:
                            eng.wait_ge(sem, pre[k])
                            w_[k] = pre[k]

                @block.sync
                def _(sy):
                    wall_pre(sy, "ld")
                    for i, (src, dst, r0, c0, n) in enumerate(tiles):
                        if i >= NBUF:
                            sy.wait_ge(sST, st_base + (i - NBUF + 1) * 16)
                        b = (i % NBUF) * CSZ
                        sy.dma_start(out=cast_f[:, b:b + n],
                                     in_=src[r0:r0 + 128, c0:c0 + n]).then_inc(sLD, 16)

                @block.vector
                def _(v):
                    wall_pre(v, "vec")
                    for i, (src, dst, r0, c0, n) in enumerate(tiles):
                        v.wait_ge(sLD, ld_base + (i + 1) * 16)
                        b = (i % NBUF) * CSZ
                        v.tensor_copy(cast_b[:, b:b + n], cast_f[:, b:b + n]).then_inc(sVEC, 1)

                @block.gpsimd
                def _(g):
                    wall_pre(g, "st")
                    for i, (src, dst, r0, c0, n) in enumerate(tiles):
                        g.wait_ge(sVEC, vec_base + i + 1)
                        b = (i % NBUF) * CSZ
                        g.dma_start(out=dst[r0:r0 + 128, c0:c0 + n],
                                    in_=cast_b[:, b:b + n]).then_inc(sST, 16)

                ctr["ld"] += NTL * 16
                ctr["vec"] += NTL
                ctr["st"] += NTL * 16
                waited["ld"]["st"] = max(waited["ld"]["st"], st_base + (NTL - NBUF + 1) * 16) if NTL >= NBUF else waited["ld"]["st"]
                waited["vec"]["ld"] = max(waited["vec"]["ld"], ld_base + NTL * 16)
                waited["st"]["vec"] = max(waited["st"]["vec"], vec_base + NTL)

                SY(lambda sy: sy.dma_start(out=cast_f[0:RANK, 0:2 * DI].rearrange("p (d n) -> p d n", d=2), in_=w_dt.rearrange("d p n -> p d n")))

                def _sdt(v):
                    v.tensor_copy(wdt_sb[0:RANK, :], cast_f[0:RANK, 0:2 * DI])
                    return v.memset(wdt_sb[RANK:65, :], 1.0)
                VEC(_sdt)

                SY(lambda sy: sy.dma_start(
                    out=cast_f[:, 0:2 * NJ * 96].rearrange("p (d j n) -> p d j n", d=2, j=NJ),
                    in_=w_xp.rearrange("d (j p) n -> p d j n", p=128)))
                VEC(lambda v: v.tensor_copy(wxp_sb[:], cast_f[:, 0:2 * NJ * 96]))
                SY(lambda sy: sy.dma_start(
                    out=cast_f[:, 0:NKD * E].rearrange("p (k e) -> p k e", k=NKD),
                    in_=wgate.rearrange("(k p) e -> p k e", p=128)))
                VEC(lambda v: v.tensor_copy(wg_sb[:], cast_f[:, 0:NKD * E]))

            if True:
                pA = ES.enter_context(nc.psum_tensor("pA", [128, 1024], F32))
                pB = ES.enter_context(nc.psum_tensor("pB", [128, 1024], F32))
                pC = ES.enter_context(nc.psum_tensor("pC", [128, 1024], F32))
                pD = ES.enter_context(nc.psum_tensor("pD", [128, 512], F32))
                pTb = ES.enter_context(nc.psum_tensor("pTb", [128, 1024], BF16))
                with contextlib.ExitStack() as ESS:
                    xtok = ESS.enter_context(nc.sbuf_tensor("xtok", [128, QT * DM], F32))
                    sqv = ESS.enter_context(nc.sbuf_tensor("sqv", [128, QT * DM], F32))
                    lnmv = ESS.enter_context(nc.sbuf_tensor("lnmv", [128, 2 * QT], F32))
                    lnr = ESS.enter_context(nc.sbuf_tensor("lnr", [128, QT], F32))
                    lnmv2 = ESS.enter_context(nc.sbuf_tensor("lnmv2", [128, 2 * QT], F32))
                    xn2t = ESS.enter_context(nc.sbuf_tensor("xn2t", [128, QT * DM], BF16))
                    xn2b = ESS.enter_context(nc.sbuf_tensor("xn2b", [128, NKD * CH], BF16))
                    brep = ESS.enter_context(nc.sbuf_tensor("brep", [128, S * SC], BF16))
                    crep = ESS.enter_context(nc.sbuf_tensor("crep", [128, S * SC], BF16))
                    wrot = ESS.enter_context(nc.sbuf_tensor("wrot", [128, NKD * 128], BF16))
                    wzrot = ESS.enter_context(nc.sbuf_tensor("wzrot", [128, NKD * 128], BF16))
                    xi_j = ESS.enter_context(nc.sbuf_tensor("xi_j", [128, CH + 3], BF16))
                    tails = ESS.enter_context(nc.sbuf_tensor("tails", [128, NJ * 3], BF16))
                    accv = ESS.enter_context(nc.sbuf_tensor("accv", [128, CH], F32))
                    xc_j = ESS.enter_context(nc.sbuf_tensor("xc_j", [128, CH], BF16))
                    xcmv = ESS.enter_context(nc.sbuf_tensor("xcmv", [128, 4 * 512], BF16))
                    dtaug = ESS.enter_context(nc.sbuf_tensor("dtaug", [65, CH], BF16))
                    bcsb = ESS.enter_context(nc.sbuf_tensor("bcsb", [32, CH], BF16))
                    dlj = ESS.enter_context(nc.sbuf_tensor("dlj", [128, 2 * SC], F32))
                    duj = ESS.enter_context(nc.sbuf_tensor("duj", [128, SC], BF16))
                    dAj = ESS.enter_context(nc.sbuf_tensor("dAj", [128, 2 * SC], F32))
                    bj = ESS.enter_context(nc.sbuf_tensor("bj", [128, 2 * SC], BF16))
                    tmpj = ESS.enter_context(nc.sbuf_tensor("tmpj", [128, SC], BF16))
                    hj = ESS.enter_context(nc.sbuf_tensor("hj", [128, 2 * SC], BF16))
                    yj = ESS.enter_context(nc.sbuf_tensor("yj", [128, SC], F32))
                    gj = ESS.enter_context(nc.sbuf_tensor("gj", [128, SC], BF16))
                    yfj = ESS.enter_context(nc.sbuf_tensor("yfj", [128, SC], BF16))
                    states = ESS.enter_context(nc.sbuf_tensor("states", [128, NJ * S], F32))
                    for d in range(2):
                        xsrc = xf if d == 0 else xb

                        def _rst(v):
                            v.memset(states[:], 0.0)
                            return v.memset(tails[:], 0.0)
                        VEC(_rst)

                        for c in range(NCH):
                            own = (c == OWN)
                            t0 = c * CH

                            skippable = (c < OWN)
                            guards = {}
                            snap = dict(ctr)
                            if skippable:
                                for _k, _deco in _edecos:
                                    @_deco
                                    def _(eng, _k=_k, c=c, d=d):
                                        g = eng.If_lt(skipreg[_k][d], c + 1)
                                        g.__enter__()
                                        guards[_k] = g

                            for q in range(NQ):
                                tq0 = t0 + q * QT * 128
                                SY(lambda sy, tq0=tq0, xsrc=xsrc: sy.dma_start(
                                    out=xtok[:].rearrange("p (a m) -> p a m", a=QT),
                                    in_=xsrc[tq0:tq0 + QT * 128, :].rearrange("(a p) m -> p a m", p=128)))

                                def _ln1(v):
                                    last = None
                                    for a in range(QT):
                                        xt = xtok[:, a * DM:(a + 1) * DM]
                                        last = v.tensor_tensor(sqv[:, a * DM:(a + 1) * DM], xt, xt, ALU.mult)
                                    return last
                                VEC(_ln1)

                                def _ln1r(v):
                                    last = None
                                    for a in range(QT):
                                        v.reduce_sum(lnmv[:, a:a + 1], xtok[:, a * DM:(a + 1) * DM], AX.X)
                                        last = v.reduce_sum(lnmv[:, QT + a:QT + a + 1],
                                                            sqv[:, a * DM:(a + 1) * DM], AX.X)
                                    return last
                                VEC(_ln1r)

                                def _lnl(a_):
                                    a_.activation(lnmv2[:, 0:QT], lnmv[:, 0:QT], AF.Copy, scale=1.0 / DM)
                                    return a_.activation(lnmv2[:, QT:2 * QT], lnmv[:, QT:2 * QT], AF.Copy, scale=1.0 / DM)
                                ACTG(_lnl)

                                def _lnv(v, C):
                                    C(v.tensor_tensor(lnr[:, 0:QT], lnmv2[:, 0:QT], lnmv2[:, 0:QT], ALU.mult))
                                    return v.tensor_tensor(lnmv2[:, QT:2 * QT], lnmv2[:, QT:2 * QT],
                                                           lnr[:, 0:QT], ALU.subtract)
                                VEC(_lnv)

                                def _lnr(a_, C):
                                    C(a_.activation(lnr[:, 0:QT], lnmv2[:, QT:2 * QT], AF.Ln,
                                                    scale=1.0 + EPS, bias=EPS * EPS))
                                    return a_.activation(lnr[:, 0:QT], lnr[:, 0:QT], AF.Exp, scale=-0.5)
                                ACTG(_lnr)

                                def _ap2(v):
                                    last = None
                                    for a in range(QT):
                                        last = v.tensor_scalar(
                                            xn2t[:, a * DM:(a + 1) * DM], xtok[:, a * DM:(a + 1) * DM],
                                            lnmv2[:, a:a + 1], lnr[:, a:a + 1],
                                            ALU.subtract, ALU.mult)
                                    return last
                                VEC(_ap2)

                                for kb in range(NKD):
                                    for a4 in range(0, QT, 4):
                                        na = min(4, QT - a4)

                                        def _tr(t, kb=kb, a4=a4, na=na):
                                            last = None
                                            for i in range(na):
                                                a = a4 + i
                                                last = t.transpose(
                                                    pTb[:, i * 128:(i + 1) * 128],
                                                    xn2t[:, a * DM + kb * 128:a * DM + (kb + 1) * 128],
                                                    ident_b[:])
                                            return last
                                        PE(_tr)

                                        def _ev(v, kb=kb, q=q, a4=a4, na=na):
                                            col = q * QT * 128 + a4 * 128
                                            return v.tensor_scalar(
                                                xn2b[:, kb * CH + col:kb * CH + col + na * 128],
                                                pTb[:, 0:na * 128], cLG(kb), cLB(kb),
                                                ALU.mult, ALU.add)
                                        VEC(_ev)

                                if own and d == 0:
                                    def _lnr1(a_, C):
                                        C(a_.activation(lnr[:, 0:QT], lnmv2[:, QT:2 * QT], AF.Ln, bias=EPS))
                                        return a_.activation(lnr[:, 0:QT], lnr[:, 0:QT], AF.Exp, scale=-0.5)
                                    ACTG(_lnr1)

                                    def _apx(v):
                                        last = None
                                        for a in range(QT):
                                            last = v.tensor_scalar(
                                                xtok[:, a * DM:(a + 1) * DM], xtok[:, a * DM:(a + 1) * DM],
                                                lnmv2[:, a:a + 1], lnr[:, a:a + 1],
                                                ALU.subtract, ALU.mult)
                                        return last
                                    VEC(_apx)

                                    GP(lambda g, q=q: g.dma_start(
                                        out=xn_h[q * QT * 128:(q + 1) * QT * 128, :].rearrange("(a p) m -> p a m", p=128),
                                        in_=xtok[:].rearrange("p (a m) -> p a m", a=QT)))
                                    VEC(lambda v: v.tensor_copy(xn2t[:], xtok[:]))
                                    for kb in range(NKD):
                                        for a4 in range(0, QT, 4):
                                            na = min(4, QT - a4)

                                            def _trx(t, kb=kb, a4=a4, na=na):
                                                last = None
                                                for i in range(na):
                                                    a = a4 + i
                                                    last = t.transpose(
                                                        pTb[:, i * 128:(i + 1) * 128],
                                                        xn2t[:, a * DM + kb * 128:a * DM + (kb + 1) * 128],
                                                        ident_b[:])
                                                return last
                                            PE(_trx)
                                            VEC(lambda v, na=na: v.tensor_copy(
                                                xc_j[:, 0:na * 128], pTb[:, 0:na * 128]))
                                            GP(lambda g, kb=kb, q=q, a4=a4, na=na: g.dma_start(
                                                out=xnfm_h[kb * 128:(kb + 1) * 128,
                                                           q * QT * 128 + a4 * 128:q * QT * 128 + (a4 + na) * 128],
                                                in_=xc_j[:, 0:na * 128]))

                            if dbg and own:
                                GP(lambda g: g.dma_start(
                                    out=xn2d.rearrange("(k p) t -> p k t", p=128),
                                    in_=xn2b[:].rearrange("p (k t) -> p k t", k=NKD)))

                            for j in range(NJ):
                                SY(lambda sy, j=j, d=d: sy.dma_start(
                                    out=wrot[:].rearrange("p (k n) -> p k n", k=NKD),
                                    in_=wxi_h[d][:, j * 128:(j + 1) * 128].rearrange("(k p) n -> p k n", p=128)))

                                for mi in range(NMM):
                                    def _mmx(t, mi=mi):
                                        last = None
                                        for kb in range(NKD):
                                            last = t.matmul(
                                                pA[:, 0:MM], wrot[:, kb * 128:(kb + 1) * 128],
                                                xn2b[:, kb * CH + mi * MM:kb * CH + (mi + 1) * MM],
                                                start=(kb == 0), stop=(kb == NKD - 1))
                                        return last
                                    PE(_mmx)
                                    VEC(lambda v, mi=mi: v.tensor_copy(
                                        xi_j[:, 3 + mi * MM:3 + (mi + 1) * MM], pA[:, 0:MM]))

                                def _conv(v, C, j=j, c=c, d=d):
                                    C(v.tensor_scalar(xi_j[:, 0:3], tails[:, j * 3:(j + 1) * 3], cGT(d, c), None, ALU.mult))
                                    C(v.tensor_scalar(accv[:], xi_j[:, 0:CH], cCW(d, j, 0), None, ALU.mult))
                                    for tap in range(1, 4):
                                        C(v.scalar_tensor_tensor(accv[:], xi_j[:, tap:tap + CH], cCW(d, j, tap),
                                                                 accv[:], ALU.mult, ALU.add))
                                    return v.tensor_copy(tails[:, j * 3:(j + 1) * 3], xi_j[:, CH:CH + 3])
                                VEC(_conv)
                                if dbg and own:
                                    GP(lambda g, j=j: g.dma_start(
                                        out=xid[j * 128:(j + 1) * 128, :], in_=xi_j[:, 3:CH + 3]))
                                ACTG(lambda a_, j=j, d=d: a_.activation(xc_j[:], accv[:], AF.Silu, bias=cCB(d, j)))
                                GP(lambda g, j=j: g.dma_start(out=xc_h[j * 128:(j + 1) * 128, :], in_=xc_j[:]))

                            SY(lambda sy, t0=t0, d=d: sy.dma_start(out=dtaug[64:65, :], in_=pen[d:d + 1, t0:t0 + CH]))
                            M5 = min(512, CH)
                            NMI = CH // M5
                            for mi in range(NMI):
                                for kb4 in range(0, NJ, 4):
                                    def _ldx(sy, mi=mi, kb4=kb4):
                                        for kk in range(4):
                                            kb = kb4 + kk
                                            b_("ld", sy.dma_start(
                                                out=xcmv[:, kk * M5:(kk + 1) * M5],
                                                in_=xc_h[kb * 128:(kb + 1) * 128, mi * M5:(mi + 1) * M5]))
                                    SY(_ldx)

                                    def _mmxp(t, mi=mi, kb4=kb4, d=d):
                                        last = None
                                        for kk in range(4):
                                            kb = kb4 + kk
                                            last = t.matmul(
                                                pD[0:96, 0:M5], wxp_sb[:, (d * NJ + kb) * 96:(d * NJ + kb + 1) * 96],
                                                xcmv[:, kk * M5:(kk + 1) * M5],
                                                start=(kb == 0), stop=(kb == NJ - 1))
                                        return last
                                    PE(_mmxp)

                                def _evp(v, C, mi=mi):
                                    C(v.tensor_copy(dtaug[0:64, mi * M5:(mi + 1) * M5], pD[0:64, 0:M5]))
                                    return v.tensor_copy(bcsb[:, mi * M5:(mi + 1) * M5], pD[64:96, 0:M5])
                                VEC(_evp)

                            GP(lambda g: g.dma_start(out=bc_h[:], in_=bcsb[:]))

                            for sc in range(2):
                                s0 = sc * SC

                                def _ldbr(sy, s0=s0, own=own):
                                    for s in range(S):
                                        b_("ld", sy.dma_start(
                                            out=brep[:, s * SC:(s + 1) * SC],
                                            in_=bc_h[s, s0:s0 + SC].partition_broadcast(128)))
                                    if own:
                                        for s in range(S):
                                            b_("ld", sy.dma_start(
                                                out=crep[:, s * SC:(s + 1) * SC],
                                                in_=bc_h[16 + s, s0:s0 + SC].partition_broadcast(128)))
                                SY(_ldbr)

                                for j in range(NJ):
                                    dl = dlj[:, (j % 2) * SC:(j % 2 + 1) * SC]
                                    SY(lambda sy, j=j, s0=s0: sy.dma_start(
                                        out=xc_j[:, 0:SC], in_=xc_h[j * 128:(j + 1) * 128, s0:s0 + SC]))

                                    def _mdt(t, j=j, s0=s0, d=d):
                                        last = None
                                        ms = min(MM, SC)
                                        for h2 in range(0, SC, ms):
                                            last = t.matmul(
                                                pB[:, h2:h2 + ms],
                                                wdt_sb[:, d * DI + j * 128:d * DI + (j + 1) * 128],
                                                dtaug[:, s0 + h2:s0 + h2 + ms], start=True, stop=True)
                                        return last
                                    PE(_mdt)

                                    def _sp(a_, C, j=j, d=d, dl=dl):
                                        C(a_.activation(dl, pB[:, 0:SC], AF.Exp, bias=cDTB(d, j)))
                                        return a_.activation(dl, dl, AF.Ln, bias=1.0)
                                    ACTG(_sp)

                                    VEC(lambda v, dl=dl: v.tensor_tensor(duj[:], dl, xc_j[:, 0:SC], ALU.mult))

                                    act_hist = {}
                                    bcnt = {}
                                    scnt = {}
                                    sp_cnt = ctr["act"]

                                    VEC(lambda v: v.tensor_tensor(bj[:, 0:SC], duj[:], brep[:, 0:SC], ALU.mult))
                                    bcnt[0] = ctr["vec"]

                                    for s in range(S):
                                        @block.scalar
                                        def _(a_, j=j, s=s, d=d, dl=dl):
                                            if s == 0:
                                                a_.wait_ge(sACT, sp_cnt)
                                            if s >= 2:
                                                a_.wait_ge(sVEC, scnt[s - 2])
                                            inst = a_.activation(dAj[:, (s % 2) * SC:(s % 2 + 1) * SC],
                                                                 dl, AF.Exp, scale=cA(d, j, s))
                                            ctr["act"] += 1
                                            inst.then_inc(sACT, 1)
                                            act_hist[s] = ctr["act"]

                                        @block.vector
                                        def _(v, j=j, s=s, own=own):
                                            v.wait_ge(sACT, act_hist[s])
                                            v.wait_ge(sVEC, bcnt[s])
                                            hjs = hj[:, (s % 2) * SC:(s % 2 + 1) * SC]
                                            i1 = v.tensor_tensor_scan(
                                                hjs, dAj[:, (s % 2) * SC:(s % 2 + 1) * SC],
                                                bj[:, (s % 2) * SC:(s % 2 + 1) * SC],
                                                states[:, j * S + s:j * S + s + 1], ALU.mult, ALU.add)
                                            ctr["vec"] += 1
                                            i1.then_inc(sVEC, 1)
                                            scnt[s] = ctr["vec"]
                                            if s + 1 < S:
                                                i2 = v.tensor_tensor(
                                                    bj[:, ((s + 1) % 2) * SC:((s + 1) % 2 + 1) * SC],
                                                    duj[:], brep[:, (s + 1) * SC:(s + 2) * SC], ALU.mult)
                                                ctr["vec"] += 1
                                                i2.then_inc(sVEC, 1)
                                                bcnt[s + 1] = ctr["vec"]
                                            v.wait_ge(sVEC, scnt[s])
                                            i3 = v.tensor_copy(states[:, j * S + s:j * S + s + 1], hjs[:, SC - 1:SC])
                                            ctr["vec"] += 1
                                            i3.then_inc(sVEC, 1)
                                            if own:
                                                i4 = v.tensor_tensor(tmpj[:], hjs, crep[:, s * SC:(s + 1) * SC], ALU.mult)
                                                ctr["vec"] += 1
                                                i4.then_inc(sVEC, 1)
                                                v.wait_ge(sVEC, ctr["vec"])
                                                if s == 0:
                                                    i5 = v.tensor_copy(yj[:], tmpj[:])
                                                else:
                                                    i5 = v.tensor_tensor(yj[:], yj[:], tmpj[:], ALU.add)
                                                ctr["vec"] += 1
                                                i5.then_inc(sVEC, 1)

                                    if own:
                                        SY(lambda sy, j=j, d=d: sy.dma_start(
                                            out=wzrot[:].rearrange("p (k n) -> p k n", k=NKD),
                                            in_=wz_h[d][:, j * 128:(j + 1) * 128].rearrange("(k p) n -> p k n", p=128)))

                                        def _mz(t, s0=s0):
                                            last = None
                                            ms = min(MM, SC)
                                            for h2 in range(0, SC, ms):
                                                for kb in range(NKD):
                                                    last = t.matmul(pC[:, h2:h2 + ms], wzrot[:, kb * 128:(kb + 1) * 128],
                                                                    xn2b[:, kb * CH + s0 + h2:kb * CH + s0 + h2 + ms],
                                                                    start=(kb == 0), stop=(kb == NKD - 1))
                                            return last
                                        PE(_mz)
                                        ACTG(lambda a_: a_.activation(gj[:], pC[:, 0:SC], AF.Silu))

                                        def _yf(v, C, j=j, d=d):
                                            C(v.scalar_tensor_tensor(yj[:], xc_j[:, 0:SC], cD(d, j), yj[:],
                                                                     ALU.mult, ALU.add))
                                            if d == 0:
                                                return v.tensor_tensor(yfj[:], yj[:], gj[:], ALU.mult)
                                            return v.tensor_tensor(yfj[:], yj[:, ::-1], gj[:, ::-1], ALU.mult)
                                        VEC(_yf)

                                        def _styf(g, j=j, s0=s0, d=d):
                                            if d == 0:
                                                return g.dma_start(out=yfin_h[d, j * 128:(j + 1) * 128, s0:s0 + SC],
                                                                   in_=yfj[:])
                                            r0 = CH - (s0 + SC)
                                            return g.dma_start(out=yfin_h[d, j * 128:(j + 1) * 128, r0:r0 + SC],
                                                               in_=yfj[:])
                                        GP(_styf)

                            if skippable:
                                deltas = {k: ctr[k] - snap[k] for k in ctr}
                                for _k, _deco in _edecos:
                                    @_deco
                                    def _(eng, _k=_k, c=c, d=d, snap=snap):
                                        guards[_k].__exit__(None, None, None)
                                        with eng.Else():
                                            if deltas[_k] > 0:
                                                eng.wait_ge(SEM[_k], snap[_k])
                                                rem = deltas[_k]
                                                pi = 0
                                                if _k == "pe":
                                                    eng.wait_ge(SEM["act"], snap["act"])
                                                while rem > 0:
                                                    piece = min(240, rem)
                                                    if _k in ("ld", "st"):
                                                        col = dumcol[0]
                                                        dumcol[0] += 1
                                                        assert col < 512
                                                        eng.dma_start(
                                                            out=cfg_sb[0:1, col:col + 1],
                                                            in_=cfg_sb[0:1, 0:1]).then_inc(SEM[_k], piece)
                                                    elif _k == "pe":
                                                        eng.matmul(pD[0:8, 0:8], ident_b[0:8, 0:8],
                                                                   ident_b[0:8, 0:8], start=True,
                                                                   stop=True).then_inc(SEM[_k], piece)
                                                    elif _k == "act":
                                                        assert pi < 32
                                                        eng.activation(scr_sb[0:1, pi:pi + 1],
                                                                       consts[0:1, oK:oK + 1],
                                                                       AF.Copy).then_inc(SEM[_k], piece)
                                                    else:
                                                        assert pi < 32
                                                        eng.memset(scr_sb[0:1, 32 + pi:33 + pi],
                                                                   0.0).then_inc(SEM[_k], piece)
                                                    pi += 1
                                                    rem -= piece
                                for _e in waited:
                                    for _k in waited[_e]:
                                        waited[_e][_k] = 0

                MC = min(512, CH)
                NMC = CH // MC
                with contextlib.ExitStack() as ESP:
                    wsA = ESP.enter_context(nc.sbuf_tensor("wsA", [128, 2 * NKD * 128], BF16))
                    wsB = ESP.enter_context(nc.sbuf_tensor("wsB", [128, 2 * NKD * 128], BF16))
                    oaccf = ESP.enter_context(nc.sbuf_tensor("oaccf", [128, NKD * CH], F32))
                    with contextlib.ExitStack() as ESG:
                        xfm = ESG.enter_context(nc.sbuf_tensor("xfm", [128, NKD * MC], BF16))
                        scf = ESG.enter_context(nc.sbuf_tensor("scf", [16, CH], F32))
                        sct = ESG.enter_context(nc.sbuf_tensor("sct", [128, 16], F32))
                        gsc = ESG.enter_context(nc.sbuf_tensor("gsc", [128, 16], F32))
                        pbf = ESG.enter_context(nc.sbuf_tensor("pbf", [16, CH], BF16))
                        for mi in range(NMC):
                            SY(lambda sy, mi=mi: sy.dma_start(
                                out=xfm[:].rearrange("p (k n) -> p k n", k=NKD),
                                in_=xnfm_h[:, mi * MC:(mi + 1) * MC].rearrange("(k p) n -> p k n", p=128)))

                            def _msc(t, mi=mi):
                                last = None
                                for kb in range(NKD):
                                    last = t.matmul(pA[0:E, 0:MC], wg_sb[:, kb * E:(kb + 1) * E],
                                                    xfm[:, kb * MC:(kb + 1) * MC],
                                                    start=(kb == 0), stop=(kb == NKD - 1))
                                return last
                            PE(_msc)
                            VEC(lambda v, mi=mi: v.tensor_copy(scf[0:E, mi * MC:(mi + 1) * MC], pA[0:E, 0:MC]))

                        for a in range(NT):
                            PE(lambda t, a=a: t.transpose(pD[:, 0:8], scf[0:E, a * 128:(a + 1) * 128], ident_f[0:8, 0:8]))
                            ACTG(lambda a_: a_.activation(sct[:, 0:8], pD[:, 0:8], AF.Copy))

                            VEC(lambda v: v.reduce_max(sct[:, 8:9], sct[:, 0:8], AX.X, negate=True))
                            ACTG(lambda a_: a_.activation(sct[:, 13:14], sct[:, 8:9], AF.Copy))

                            def _g8(v, C):
                                s8 = sct[:, 0:8]
                                C(v.tensor_scalar(sct[:, 9:10], sct[:, 13:14], -1.0, None, ALU.mult))
                                C(v.tensor_scalar(gsc[:, 0:8], s8, sct[:, 9:10], None, ALU.is_ge))
                                C(v.scalar_tensor_tensor(gsc[:, 8:16], gsc[:, 0:8], -1e30, s8, ALU.mult, ALU.add))
                                return v.reduce_max(sct[:, 10:11], gsc[:, 8:16], AX.X)
                            VEC(_g8)

                            def _gl(a_):
                                a_.activation(sct[:, 14:15], sct[:, 10:11], AF.Copy)
                                return a_.activation(gsc[:, 8:16], sct[:, 0:8], AF.Exp, bias=sct[:, 13:14])
                            ACTG(_gl)

                            def _g9(v, C):
                                C(v.tensor_scalar(gsc[:, 0:8], sct[:, 0:8], sct[:, 14:15], None, ALU.is_ge))
                                C(v.tensor_tensor(gsc[:, 8:16], gsc[:, 8:16], gsc[:, 0:8], ALU.mult))
                                return v.reduce_sum(sct[:, 11:12], gsc[:, 8:16], AX.X)
                            VEC(_g9)
                            ACTG(lambda a_: a_.activation(sct[:, 15:16], sct[:, 11:12], AF.Copy))
                            VEC(lambda v: v.reciprocal(sct[:, 12:13], sct[:, 15:16]))
                            ACTG(lambda a_: a_.activation(sct[:, 9:10], sct[:, 12:13], AF.Copy))
                            VEC(lambda v: v.tensor_scalar(gsc[:, 0:8], gsc[:, 8:16], sct[:, 9:10], None, ALU.mult))
                            PE(lambda t, a=a: t.transpose(pD[0:8, 128:256], gsc[:, 0:8], ident_f[:]))
                            VEC(lambda v, a=a: v.tensor_copy(pbf[0:E, a * 128:(a + 1) * 128], pD[0:8, 128:256]))
                        GP(lambda g: g.dma_start(out=p_h[:], in_=pbf[0:E, :]))

                    with contextlib.ExitStack() as ESFF:
                        featf = ESFF.enter_context(nc.sbuf_tensor("featf", [128, NKD * CH], BF16))
                        featb = ESFF.enter_context(nc.sbuf_tensor("featb", [128, NKD * CH], BF16))
                        yfm = ESFF.enter_context(nc.sbuf_tensor("yfm", [128, NJ * MC], BF16))
                        wci = [0]
                        for d in range(2):
                            feat = featf if d == 0 else featb
                            for mi in range(NMC):
                                SY(lambda sy, d=d, mi=mi: sy.dma_start(
                                    out=yfm[:].rearrange("p (k n) -> p k n", k=NJ),
                                    in_=yfin_h[d][:, mi * MC:(mi + 1) * MC].rearrange("(k p) n -> p k n", p=128)))
                                for n in range(NKD):
                                    wb = wsA if wci[0] % 2 == 0 else wsB
                                    wci[0] += 1
                                    SY(lambda sy, d=d, n=n, wb=wb: sy.dma_start(
                                        out=wb[:, 0:NKD * 128].rearrange("p (k n) -> p k n", k=NKD),
                                        in_=wout_h[d][0:DI // 2, n * 128:(n + 1) * 128].rearrange("(k p) n -> p k n", p=128)))
                                    GP(lambda g, d=d, n=n, wb=wb: g.dma_start(
                                        out=wb[:, NKD * 128:2 * NKD * 128].rearrange("p (k n) -> p k n", k=NKD),
                                        in_=wout_h[d][DI // 2:DI, n * 128:(n + 1) * 128].rearrange("(k p) n -> p k n", p=128)))

                                    def _mf(t, n=n, wb=wb):
                                        last = None
                                        for k in range(NJ):
                                            last = t.matmul(pA[:, 0:MC], wb[:, k * 128:(k + 1) * 128],
                                                            yfm[:, k * MC:(k + 1) * MC],
                                                            start=(k == 0), stop=(k == NJ - 1))
                                        return last
                                    PE(_mf)
                                    VEC(lambda v, n=n, mi=mi, feat=feat: v.tensor_copy(
                                        feat[:, n * CH + mi * MC:n * CH + (mi + 1) * MC], pA[:, 0:MC]))

                        for nn in range(NKD):
                            wb = wsA if wci[0] % 2 == 0 else wsB
                            wci[0] += 1
                            SY(lambda sy, nn=nn, wb=wb: sy.dma_start(
                                out=wb[:, 0:NKD * 128].rearrange("p (k n) -> p k n", k=NKD),
                                in_=wfus_h[0:DM, nn * 128:(nn + 1) * 128].rearrange("(k p) n -> p k n", p=128)))
                            GP(lambda g, nn=nn, wb=wb: g.dma_start(
                                out=wb[:, NKD * 128:2 * NKD * 128].rearrange("p (k n) -> p k n", k=NKD),
                                in_=wfus_h[DM:2 * DM, nn * 128:(nn + 1) * 128].rearrange("(k p) n -> p k n", p=128)))
                            for mi in range(NMC):
                                def _mfus(t, nn=nn, mi=mi, wb=wb):
                                    for k in range(NKD):
                                        t.matmul(pB[:, 0:MC], wb[:, k * 128:(k + 1) * 128],
                                                 featf[:, k * CH + mi * MC:k * CH + (mi + 1) * MC],
                                                 start=(k == 0), stop=False)
                                    last = None
                                    for k in range(NKD):
                                        last = t.matmul(pB[:, 0:MC], wb[:, (NKD + k) * 128:(NKD + k + 1) * 128],
                                                        featb[:, k * CH + mi * MC:k * CH + (mi + 1) * MC],
                                                        start=False, stop=(k == NKD - 1))
                                    return last
                                PE(_mfus)
                                VEC(lambda v, nn=nn, mi=mi: v.tensor_scalar(
                                    oaccf[:, nn * CH + mi * MC:nn * CH + (mi + 1) * MC],
                                    pB[:, 0:MC], cFB(nn), None, ALU.add))

                    MP = min(2 * MC, CH)
                    NMP = CH // MP
                    HH = MP // MC
                    with contextlib.ExitStack() as ESE:
                        ws3 = ESE.enter_context(nc.sbuf_tensor("ws3", [128, 4 * NJ * 128], BF16))
                        xfm_p = ESE.enter_context(nc.sbuf_tensor("xfm_p", [128, NKD * MP], BF16))
                        prep_ep = ESE.enter_context(nc.sbuf_tensor("prep_ep", [128, MP], BF16))
                        hidden_p = ESE.enter_context(nc.sbuf_tensor("hidden_p", [128, NJ * MP], BF16))
                        hsil2 = ESE.enter_context(nc.sbuf_tensor("hsil2", [128, MP], F32))
                        otok = ESE.enter_context(nc.sbuf_tensor("otok", [128, DM], F32))
                        xrl = ESE.enter_context(nc.sbuf_tensor("xrl", [128, DM], F32))
                        xnrl = ESE.enter_context(nc.sbuf_tensor("xnrl", [128, DM], F32))
                        pe_mm = {}
                        pe_w3s = {}
                        act_sil = {}
                        vec_hid = {}
                        ii = [0]
                        pe_start = ctr["pe"]
                        act_start = ctr["act"]
                        vec_start = ctr["vec"]
                        pe_w3_prev = [pe_start]
                        last_sy = [ctr["ld"]]
                        last_gp = [ctr["st"]]
                        last_ad = [ctr["ad"]]
                        for mp in range(NMP):
                            @block.sync
                            def _(sy, mp=mp, pin=last_sy[0]):
                                sy.wait_ge(sLD, pin)
                                sy.wait_ge(sPE, ctr["pe"])
                                b_("ld", sy.dma_start(
                                    out=xfm_p[:].rearrange("p (k n) -> p k n", k=NKD),
                                    in_=xnfm_h[:, mp * MP:(mp + 1) * MP].rearrange("(k p) n -> p k n", p=128)))
                            last_sy[0] = ctr["ld"]
                            for e in range(E):
                                @block.sync
                                def _(sy, e=e, mp=mp, pin=last_sy[0]):
                                    sy.wait_ge(sLD, pin)
                                    sy.wait_ge(sVEC, ctr["vec"])
                                    b_("ld", sy.dma_start(
                                        out=prep_ep[:],
                                        in_=p_h[e, mp * MP:(mp + 1) * MP].partition_broadcast(128)))
                                last_sy[0] = ctr["ld"]
                                ld_prep = ctr["ld"]
                                ld_w12 = {}
                                ld_w3 = {}
                                for nh in range(NJ):
                                    wb = wsA if ii[0] % 2 == 0 else wsB

                                    @block.sync
                                    def _(sy, e=e, nh=nh, wb=wb, i0=ii[0], pin=last_sy[0]):
                                        sy.wait_ge(sLD, pin)
                                        sy.wait_ge(sPE, pe_mm.get(i0 - 2, pe_start))
                                        b_("ld", sy.dma_start(
                                            out=wb[:, 0:NKD * 128].rearrange("p (k n) -> p k n", k=NKD),
                                            in_=w12_h[e][:, nh * 128:(nh + 1) * 128].rearrange("(k p) n -> p k n", p=128)))
                                    last_sy[0] = ctr["ld"]

                                    @block.gpsimd
                                    def _(g, e=e, nh=nh, wb=wb, i0=ii[0], pin=last_gp[0]):
                                        g.wait_ge(sST, pin)
                                        g.wait_ge(sPE, pe_mm.get(i0 - 2, pe_start))
                                        b_("st", g.dma_start(
                                            out=wb[:, NKD * 128:2 * NKD * 128].rearrange("p (k n) -> p k n", k=NKD),
                                            in_=w12_h[e][:, HF + nh * 128:HF + (nh + 1) * 128].rearrange("(k p) n -> p k n", p=128)))
                                    last_gp[0] = ctr["st"]
                                    ld_w12[ii[0]] = (ctr["ld"], ctr["st"])

                                    if nh < 4:
                                        @block.scalar
                                        def _(a, e=e, nn=nh, pin=last_ad[0]):
                                            a.wait_ge(sAD, pin)
                                            a.wait_ge(sPE, pe_w3_prev[0])
                                            b_("ad", a.dma_start(
                                                out=ws3[:, (nn % 4) * NJ * 128:(nn % 4 + 1) * NJ * 128].rearrange(
                                                    "p (k n) -> p k n", k=NJ),
                                                in_=w3_h[e][:, nn * 128:(nn + 1) * 128].rearrange("(k p) n -> p k n", p=128)))
                                        last_ad[0] = ctr["ad"]
                                        ld_w3[nh] = ctr["ad"]

                                    @block.tensor
                                    def _(t, wb=wb, i0=ii[0]):
                                        lw, sw = ld_w12[i0]
                                        t.wait_ge(sLD, lw)
                                        t.wait_ge(sST, sw)
                                        t.wait_ge(sACT, act_sil.get(i0 - 1, act_start))
                                        t.wait_ge(sVEC, vec_hid.get(i0 - 1, vec_start))
                                        for hh in range(HH):
                                            for kb in range(NKD):
                                                t.matmul(pA[:, hh * MC:(hh + 1) * MC],
                                                         wb[:, kb * 128:(kb + 1) * 128],
                                                         xfm_p[:, kb * MP + hh * MC:kb * MP + (hh + 1) * MC],
                                                         start=(kb == 0), stop=(kb == NKD - 1))
                                        last = None
                                        for hh in range(HH):
                                            for kb in range(NKD):
                                                last = t.matmul(pB[:, hh * MC:(hh + 1) * MC],
                                                                wb[:, (NKD + kb) * 128:(NKD + kb + 1) * 128],
                                                                xfm_p[:, kb * MP + hh * MC:kb * MP + (hh + 1) * MC],
                                                                start=(kb == 0), stop=(kb == NKD - 1))
                                        ctr["pe"] += 1
                                        last.then_inc(sPE, 1)
                                    pe_mm[ii[0]] = ctr["pe"]

                                    @block.scalar
                                    def _(a, i0=ii[0]):
                                        a.wait_ge(sPE, pe_mm[i0])
                                        a.wait_ge(sVEC, vec_hid.get(i0 - 1, vec_start))
                                        inst = a.activation(hsil2[:], pA[:, 0:MP], AF.Silu)
                                        ctr["act"] += 1
                                        inst.then_inc(sACT, 1)
                                    act_sil[ii[0]] = ctr["act"]

                                    @block.vector
                                    def _(v, nh=nh, i0=ii[0], ld_prep=ld_prep, vself=ctr["vec"]):
                                        v.wait_ge(sACT, act_sil[i0])
                                        v.wait_ge(sLD, ld_prep)
                                        v.wait_ge(sVEC, vself)
                                        hslice = hidden_p[:, nh * MP:(nh + 1) * MP]
                                        i1 = v.tensor_tensor(hslice, hsil2[:], pB[:, 0:MP], ALU.mult)
                                        ctr["vec"] += 1
                                        i1.then_inc(sVEC, 1)
                                        v.wait_ge(sVEC, ctr["vec"])
                                        i2 = v.tensor_tensor(hslice, hslice, prep_ep[:], ALU.mult)
                                        ctr["vec"] += 1
                                        i2.then_inc(sVEC, 1)
                                    vec_hid[ii[0]] = ctr["vec"]
                                    ii[0] += 1

                                vec_hid_last = ctr["vec"]
                                vec_oacc = {}
                                for nn in range(NKD):
                                    if nn >= 4:
                                        @block.scalar
                                        def _(a, e=e, nn=nn, pin=last_ad[0]):
                                            a.wait_ge(sAD, pin)
                                            a.wait_ge(sPE, pe_w3s[nn - 4])
                                            b_("ad", a.dma_start(
                                                out=ws3[:, (nn % 4) * NJ * 128:(nn % 4 + 1) * NJ * 128].rearrange(
                                                    "p (k n) -> p k n", k=NJ),
                                                in_=w3_h[e][:, nn * 128:(nn + 1) * 128].rearrange("(k p) n -> p k n", p=128)))
                                        last_ad[0] = ctr["ad"]
                                        ld_w3[nn] = ctr["ad"]

                                    @block.tensor
                                    def _(t, nn=nn, vh=vec_hid_last):
                                        t.wait_ge(sAD, ld_w3[nn])
                                        t.wait_ge(sVEC, vh)
                                        if nn >= 1:
                                            t.wait_ge(sVEC, vec_oacc[nn - 1])
                                        last = None
                                        for hh in range(HH):
                                            for k in range(NJ):
                                                last = t.matmul(pC[:, hh * MC:(hh + 1) * MC],
                                                                ws3[:, (nn % 4) * NJ * 128 + k * 128:
                                                                    (nn % 4) * NJ * 128 + (k + 1) * 128],
                                                                hidden_p[:, k * MP + hh * MC:k * MP + (hh + 1) * MC],
                                                                start=(k == 0), stop=(k == NJ - 1))
                                        ctr["pe"] += 1
                                        last.then_inc(sPE, 1)
                                    pe_w3s[nn] = ctr["pe"]

                                    @block.vector
                                    def _(v, nn=nn, mp=mp, vself=ctr["vec"]):
                                        v.wait_ge(sPE, pe_w3s[nn])
                                        v.wait_ge(sVEC, vself)
                                        col = nn * CH + mp * MP
                                        i1 = v.tensor_tensor(oaccf[:, col:col + MP],
                                                             oaccf[:, col:col + MP], pC[:, 0:MP], ALU.add)
                                        ctr["vec"] += 1
                                        i1.then_inc(sVEC, 1)
                                    vec_oacc[nn] = ctr["vec"]
                                pe_w3_prev[0] = ctr["pe"]

                        for mi in range(NMC):
                            for tt in range(MC // 128):
                                def _mtr(t, mi=mi, tt=tt):
                                    last = None
                                    for nn in range(NKD):
                                        c0 = nn * CH + mi * MC + tt * 128
                                        last = t.transpose(pA[:, nn * 128:(nn + 1) * 128],
                                                           oaccf[:, c0:c0 + 128], ident_f[:])
                                    return last
                                PE(_mtr)
                                r0 = mi * MC + tt * 128

                                def _ldr(sy, r0=r0):
                                    b_("ld", sy.dma_start(out=xrl[:], in_=xf[T - CH + r0:T - CH + r0 + 128, :]))
                                    b_("ld", sy.dma_start(out=xnrl[:], in_=xn_h[r0:r0 + 128, :]))
                                SY(_ldr)

                                def _fin(v, C):
                                    C(v.tensor_tensor(otok[:], pA[:, 0:DM], xrl[:], ALU.add))
                                    return v.tensor_tensor(otok[:], otok[:], xnrl[:], ALU.add)
                                VEC(_fin)
                                GP(lambda g, r0=r0: g.dma_start(out=y_out[r0:r0 + 128, :], in_=otok[:]))

        @block.sync
        def _(sy):
            sy.wait_ge(sST, ctr["st"])

    return nc


_BUILD_CACHE = {}


def _get_nc(T, CH):
    key = (T, CH)
    if key not in _BUILD_CACHE:
        _BUILD_CACHE[key] = build(T, CH)
    return _BUILD_CACHE[key]


def make_inmaps(T, CH, x, weights):
    NCH = T // CH
    OWNT = T // 8
    f32 = np.float32

    def pp(vec, nblk):
        return np.ascontiguousarray(vec.reshape(nblk, 128).T.astype(f32))

    wd = {}
    wd["w_xi"] = np.ascontiguousarray(np.stack([
        weights["f_in_w"][:DI].T, weights["b_in_w"][:DI].T]).astype(f32))
    wd["w_z"] = np.ascontiguousarray(np.stack([
        weights["f_in_w"][DI:].T, weights["b_in_w"][DI:].T]).astype(f32))
    wd["w_xp"] = np.ascontiguousarray(np.stack([
        weights["f_xproj_w"].T, weights["b_xproj_w"].T]).astype(f32))
    wd["w_dt"] = np.ascontiguousarray(np.stack([
        weights["f_dt_w"].T, weights["b_dt_w"].T]).astype(f32))
    wd["dtb"] = np.ascontiguousarray(np.stack([
        pp(weights["f_dt_b"], NJ), pp(weights["b_dt_b"], NJ)]))
    cw = []
    for p in ("f", "b"):
        c = weights[p + "_conv_w"][:, 0, :]
        cw.append(np.ascontiguousarray(
            c.reshape(NJ, 128, 4).transpose(1, 0, 2).reshape(128, NJ * 4).astype(f32)))
    wd["convw"] = np.ascontiguousarray(np.stack(cw))
    wd["convb"] = np.ascontiguousarray(np.stack([
        pp(weights["f_conv_b"], NJ), pp(weights["b_conv_b"], NJ)]))
    al = []
    for p in ("f", "b"):
        a = weights[p + "_A_log"]
        al.append(np.ascontiguousarray(
            a.reshape(NJ, 128, S).transpose(1, 0, 2).reshape(128, NJ * S).astype(f32)))
    wd["alog"] = np.ascontiguousarray(np.stack(al))
    wd["dvec"] = np.ascontiguousarray(np.stack([
        pp(weights["f_D"], NJ), pp(weights["b_D"], NJ)]))
    wd["w_out"] = np.ascontiguousarray(np.stack([
        weights["f_out_w"].T, weights["b_out_w"].T]).astype(f32))
    wd["w_fus"] = np.ascontiguousarray(weights["fus_w"].T.astype(f32))
    wd["fusb"] = pp(weights["fus_b"], NKD)
    wd["lng"] = pp(weights["ln_g"], NKD)
    wd["lnb"] = pp(weights["ln_b"], NKD)
    wd["w12"] = np.ascontiguousarray(weights["ffn_w12"].transpose(0, 2, 1).astype(f32))
    wd["w3"] = np.ascontiguousarray(weights["ffn_w3"].transpose(0, 2, 1).astype(f32))
    wd["wgate"] = np.ascontiguousarray(weights["ffn_gate_w"].T.astype(f32))
    wd["ident"] = np.eye(128, dtype=f32)

    xr = x[::-1]
    in_maps = []
    for k in range(8):
        L = OWNT * (k + 1)
        padf = T - L
        padb = OWNT * k
        xfk = np.concatenate([np.zeros((padf, DM), f32), x[:L]], axis=0)
        xbk = np.concatenate([np.zeros((padb, DM), f32), xr[:T - padb]], axis=0)
        penk = np.zeros((2, T), f32)
        penk[0, :padf] = PEN
        penk[1, :padb] = PEN
        penk = penk.astype(ml_dtypes.bfloat16)
        gk = np.ones((128, 2 * NCH), f32)
        npadf = padf // CH
        npadb = padb // CH
        gk[:, 0:npadf + 1] = 0.0
        gk[:, NCH:NCH + npadb + 1] = 0.0
        m = dict(xf=np.ascontiguousarray(xfk), xb=np.ascontiguousarray(xbk),
                 pen=penk, gates=gk,
                 cfg=np.array([[npadf, npadb, 0, 0]], dtype=np.int32))
        m.update(wd)
        in_maps.append(m)
    return in_maps


def kernel(L0_attn_x, L0_attn_ln_g, L0_attn_ln_b, L0_attn_fus_w, L0_attn_fus_b,
           L0_attn_f_in_w, L0_attn_f_conv_w, L0_attn_f_conv_b, L0_attn_f_xproj_w,
           L0_attn_f_dt_w, L0_attn_f_dt_b, L0_attn_f_A_log, L0_attn_f_D, L0_attn_f_out_w,
           L0_attn_b_in_w, L0_attn_b_conv_w, L0_attn_b_conv_b, L0_attn_b_xproj_w,
           L0_attn_b_dt_w, L0_attn_b_dt_b, L0_attn_b_A_log, L0_attn_b_D, L0_attn_b_out_w,
           L0_ffn_w12, L0_ffn_w3, L0_ffn_gate_w):
    T, CH = 16384, 2048
    x = np.ascontiguousarray(np.asarray(L0_attn_x, dtype=np.float32)[0])
    weights = dict(
        ln_g=np.asarray(L0_attn_ln_g), ln_b=np.asarray(L0_attn_ln_b),
        fus_w=np.asarray(L0_attn_fus_w), fus_b=np.asarray(L0_attn_fus_b),
        f_in_w=np.asarray(L0_attn_f_in_w), f_conv_w=np.asarray(L0_attn_f_conv_w),
        f_conv_b=np.asarray(L0_attn_f_conv_b), f_xproj_w=np.asarray(L0_attn_f_xproj_w),
        f_dt_w=np.asarray(L0_attn_f_dt_w), f_dt_b=np.asarray(L0_attn_f_dt_b),
        f_A_log=np.asarray(L0_attn_f_A_log), f_D=np.asarray(L0_attn_f_D),
        f_out_w=np.asarray(L0_attn_f_out_w),
        b_in_w=np.asarray(L0_attn_b_in_w), b_conv_w=np.asarray(L0_attn_b_conv_w),
        b_conv_b=np.asarray(L0_attn_b_conv_b), b_xproj_w=np.asarray(L0_attn_b_xproj_w),
        b_dt_w=np.asarray(L0_attn_b_dt_w), b_dt_b=np.asarray(L0_attn_b_dt_b),
        b_A_log=np.asarray(L0_attn_b_A_log), b_D=np.asarray(L0_attn_b_D),
        b_out_w=np.asarray(L0_attn_b_out_w),
        ffn_w12=np.asarray(L0_ffn_w12), ffn_w3=np.asarray(L0_ffn_w3),
        ffn_gate_w=np.asarray(L0_ffn_gate_w),
    )
    nc = _get_nc(T, CH)
    in_maps = make_inmaps(T, CH, x, weights)
    res = run_bass_kernel_spmd(nc, in_maps, core_ids=list(range(8)))
    out = np.concatenate([res.results[k]["y_out"] for k in range(8)], axis=0)
    return out.reshape(1, T, DM).astype(np.float32)
